# version 28
# baseline (speedup 1.0000x reference)
import math
import numpy as np
from contextlib import ExitStack

import concourse.bass as bass
import concourse.mybir as mybir
import concourse.bass_utils as _bass_utils
from concourse.bass_utils import run_bass_kernel_spmd

if not getattr(_bass_utils, "_ldw_opt_patched", False):
    _orig_run_command = _bass_utils.run_command

    def _run_command_ldw(cmd, *a, **kw):
        if isinstance(cmd, (list, tuple)):
            cmd = ["--enable-ldw-opt=true" if c == "--enable-ldw-opt=false"
                   else c for c in cmd]
        return _orig_run_command(cmd, *a, **kw)

    _bass_utils.run_command = _run_command_ldw
    _bass_utils._ldw_opt_patched = True

F32 = mybir.dt.float32
BF16 = mybir.dt.bfloat16
AF = mybir.ActivationFunctionType
ALU = mybir.AluOpType
AX = mybir.AxisListType

D = 1024
H = 16
HID = 4096
NEXP = 10
NTRU = 8
LN_EPS = 1e-5
RMS_EPS = 1e-6
KQ_EPS = 1e-6
NEG = -1e30
BIG = 1e9
QK_BOUND = 1.05


def _bucket_onehot(S):
    n = S + 128
    E = np.zeros((32, n), dtype=np.float32)
    i = np.arange(128, n)
    d = (i - 127).astype(np.int64)
    safe = np.maximum(d, 1).astype(np.float32)
    t = np.log(safe / np.float32(16.0))
    t = t / np.float32(math.log(128.0 / 16.0))
    t = t * np.float32(16.0)
    large = np.minimum(16 + t.astype(np.int32), 31)
    bucket = np.where(d < 16, d, large).astype(np.int64)
    E[bucket, i] = 1.0
    return E


class KB:

    ENGINES = ("sync", "gpsimd", "tensor", "vector", "scalar")

    def __init__(self, nc):
        self.nc = nc
        self.plan = {e: [] for e in self.ENGINES}
        self.sems = {}
        self.cnt = {}
        self.waited = {e: {} for e in self.ENGINES}
        self.bufs = {}
        self.pending = {e: [] for e in self.ENGINES}
        self.eng_sem = {"gpsimd": "G", "tensor": "T", "vector": "V",
                        "scalar": "S", "sync": None}
        self.fence_waits = {}
        self.stack = ExitStack()

    def sem(self, name):
        if name not in self.sems:
            self.sems[name] = self.stack.enter_context(self.nc.semaphore(name))
            self.cnt[name] = 0
        return self.sems[name]

    def op(self, eng, fn, reads=(), writes=(), inc=True, dma_sem=None,
           extra_waits=()):
        waits = list(extra_waits)

        def need(w):
            if w is None:
                return
            if w[0] == "PEND":
                if w[1] != eng:
                    raise RuntimeError(f"cross-engine access to pending buf {w}")
                return
            waits.append(w)

        for b in reads:
            ent = self.bufs.get(b)
            if ent:
                need(ent.get("w"))
        for b in writes:
            ent = self.bufs.get(b)
            if ent:
                need(ent.get("w"))
                for r in ent.get("r", ()):
                    need(r)
                for peng in ent.get("pr", ()):
                    if peng != eng:
                        raise RuntimeError(
                            f"write to {b} with pending readers on {peng}")
        ww = {}
        for s, v in list(waits) + list(self.fence_waits.items()):
            ww[s] = max(ww.get(s, 0), v)
        w_em = []
        for s, v in ww.items():
            if self.waited[eng].get(s, -1) >= v:
                continue
            self.waited[eng][s] = v
            self.sem(s)
            w_em.append((s, v))

        if dma_sem is not None:
            self.sem(dma_sem)
            self.cnt[dma_sem] += 16
            eff = (dma_sem, self.cnt[dma_sem])
        elif inc and self.eng_sem[eng]:
            s = self.eng_sem[eng]
            self.sem(s)
            self.cnt[s] += 1
            eff = (s, self.cnt[s])
        else:
            eff = ("PEND", eng)

        if eff[0] != "PEND":
            for key, kind in self.pending[eng]:
                ent = self.bufs.setdefault(key, {"w": None, "r": [], "pr": set()})
                if kind == "w":
                    ent["w"] = eff
                    ent["r"] = []
                else:
                    ent["r"].append(eff)
                ent["pr"].discard(eng)
            self.pending[eng] = []

        for b in reads:
            ent = self.bufs.setdefault(b, {"w": None, "r": [], "pr": set()})
            if eff[0] == "PEND":
                self.pending[eng].append((b, "r"))
                ent["pr"].add(eng)
            else:
                ent["r"].append(eff)
        for b in writes:
            ent = self.bufs.setdefault(b, {"w": None, "r": [], "pr": set()})
            if eff[0] == "PEND":
                ent["w"] = ("PEND", eng)
                self.pending[eng].append((b, "w"))
                ent["pr"].add(eng)
            else:
                ent["w"] = eff
                ent["r"] = []
        self.plan[eng].append((w_em, fn, None if eff[0] == "PEND" else eff,
                               dma_sem is not None))

    def fence(self):
        for eng in self.ENGINES:
            if self.pending[eng]:
                raise RuntimeError(f"fence with pending ops on {eng}")
        self.fence_waits = dict(self.cnt)

    def finish_queue(self, sem, keys):
        for k in keys:
            ent = self.bufs.setdefault(k, {"w": None, "r": [], "pr": set()})
            ent["w"] = (sem, self.cnt[sem])
            ent["r"] = []

    def wait(self, eng, sem, val):
        self.sem(sem)
        self.plan[eng].append(([(sem, val)], None, None, False))

    def emit(self):
        kb = self
        with kb.nc.Block() as block:
            for eng in self.ENGINES:
                steps = self.plan[eng]
                if not steps:
                    continue

                def body(e, steps=steps):
                    for waits, fn, eff, is_dma in steps:
                        for s, v in waits:
                            e.wait_ge(kb.sems[s], v)
                        if fn is None:
                            continue
                        inst = fn(e)
                        if eff is not None:
                            inst.then_inc(kb.sems[eff[0]], 16 if is_dma else 1)
                getattr(block, eng)(body)


def build_nc(T=2048):
    nc = bass.Bass("TRN2", target_bir_lowering=False)
    NTT = T // 128
    NP = T // 512
    CP = T + 128

    x_d = nc.dram_tensor("x", [T, D], F32, kind="ExternalInput")
    w_d = {}
    for nm, shp in [("ln1", [D]), ("Wq", [D, D]), ("Wk", [D, D]), ("Wv", [D, D]),
                    ("relb", [32, H]), ("scale", [H]), ("Wo", [D, D]),
                    ("ln2", [D]), ("W1", [D, HID]), ("W2", [HID, D]),
                    ("gw", [NEXP, D]), ("gb", [NEXP]),
                    ("ew", [NTRU, D, D]), ("eb", [NTRU, D]),
                    ("E", [32, CP])]:
        w_d[nm] = nc.dram_tensor(nm, shp, F32, kind="ExternalInput")
    out_d = nc.dram_tensor("out", [T, D], F32, kind="ExternalOutput")
    scr_d = nc.dram_tensor("cscr", [H, CP], F32, kind="Internal")
    acc_d = nc.dram_tensor("accscr", [8 * 128, T], F32, kind="Internal")

    kb = KB(nc)
    A = bass.AP
    ES = kb.stack

    def sb(name, shape, dt, st=None):
        return (st or ES).enter_context(nc.sbuf_tensor(name, list(shape), dt))

    pb = [None] * 8
    for i in (0, 1, 2, 3):
        pb[i] = ES.enter_context(nc.psum_tensor(f"pb{i}", [128, 512], F32))
    st_ps_a = ExitStack()
    for i in (4, 5, 6, 7):
        pb[i] = st_ps_a.enter_context(nc.psum_tensor(f"pb{i}a", [128, 512], F32))

    iden_f = sb("iden_f", (128, 128), F32)
    Jmat = sb("Jmat", (128, 128), BF16)
    onesb = sb("onesb", (128, 2), BF16)
    onesf = sb("onesf", (1, 64), F32)
    onesbr = sb("onesbr", (1, 64), BF16)
    iota_f = sb("iota_f", (128, 128), F32)
    iota_p = sb("iota_p", (128, 1), F32)
    iota10 = sb("iota10", (128, 12), F32)
    iotaB = sb("iotaB", (128, 12), F32)
    epsc = sb("epsc", (128, 4), F32)
    selq = sb("selq", (128, 2), BF16)
    sel2T = sb("sel2T", (2, 128), BF16)
    relbS = sb("relbS", (32, H), F32)
    relbT = sb("relbT", (H, 32), F32)
    scale16 = sb("scale16", (H, 2), F32)
    scaleP = sb("scaleP", (2, 8), F32)
    gmaxc = sb("gmaxc", (H, 2), F32)
    gbc = sb("gbc", (NEXP, 1), F32)
    ln1b = sb("ln1b", (1, D), BF16)
    ln2b = sb("ln2b", (1, D), BF16)
    ebCol = sb("ebCol", (128, NTRU * 8), F32)
    g1 = sb("g1", (128, 12), F32)
    g2 = sb("g2", (128, 12), F32)
    g3 = sb("g3", (128, 12), F32)
    gs = sb("gs", (128, 16), F32)

    XN16 = [sb(f"xn16_{c}", (128, T), BF16) for c in range(8)]

    MISC = "qM"

    def ld(dst, src, cast=False, sem=MISC, reads=(), writes=(), silent=False,
           slow=False):
        eng = "gpsimd" if cast else "sync"
        ew = () if silent else (((sem, kb.cnt.get(sem, 0)),)
                                if kb.cnt.get(sem, 0) else ())
        kb.op(eng, lambda e, d=dst, s=src, sl=slow:
              e.dma_start(d, s, allow_slow_non_contiguous=sl),
              reads=() if silent else reads, writes=() if silent else writes,
              dma_sem=sem, extra_waits=ew)

    ld(relbS[:, :], w_d["relb"][:, :], writes=("relbS",))
    ld(relbT[:, :], A(w_d["relb"], 0, [[1, H], [H, 32]]), writes=("relbT",), slow=True)
    ld(scale16[:, 0:1], A(w_d["scale"], 0, [[1, H], [1, 1]]), writes=("scale16",))
    ld(scaleP[:, :], A(w_d["scale"], 0, [[1, 2], [2, 8]]), writes=("scaleP",), slow=True)
    ld(gbc[:, :], A(w_d["gb"], 0, [[1, NEXP], [1, 1]]), writes=("gbc",))
    ld(ln1b[:, :], A(w_d["ln1"], 0, [[D, 1], [1, D]]), cast=True, sem="qC",
       writes=("ln1b",))
    ld(ln2b[:, :], A(w_d["ln2"], 0, [[D, 1], [1, D]]), cast=True, sem="qC",
       writes=("ln2b",))
    for e_ in range(NTRU):
        ld(ebCol[:, e_ * 8:(e_ + 1) * 8], A(w_d["eb"], e_ * D, [[1, 128], [128, 8]]),
           writes=("ebCol",), slow=True)

    G, V, S, TE = "gpsimd", "vector", "scalar", "tensor"
    kb.op(G, lambda e: e.iota(iota_f[:, :], [[1, 128]], channel_multiplier=0,
                              allow_small_or_imprecise_dtypes=True),
          writes=("iota_f",))
    kb.op(G, lambda e: e.iota(iota_p[:, :], [[0, 1]], channel_multiplier=1,
                              allow_small_or_imprecise_dtypes=True),
          writes=("iota_p",))
    kb.op(G, lambda e: e.iota(iota10[:, 0:NEXP], [[1, NEXP]], channel_multiplier=0,
                              allow_small_or_imprecise_dtypes=True),
          writes=("iota10",))
    kb.op(V, lambda e: e.tensor_scalar(iotaB[:, 0:NEXP], iota10[:, 0:NEXP],
                                       1.0e4, None, ALU.add),
          reads=("iota10",), writes=("iotaB",))
    kb.op(G, lambda e: e.memset(onesb[:, :], 1.0), writes=("onesb",))
    kb.op(G, lambda e: e.memset(onesf[:, :], 1.0), writes=("onesf",))
    kb.op(G, lambda e: e.memset(onesbr[:, :], 1.0), writes=("onesbr",))
    kb.op(G, lambda e: e.memset(epsc[:, 0:1], LN_EPS), writes=("epsc",))
    kb.op(G, lambda e: e.memset(epsc[:, 1:2], RMS_EPS), writes=("epsc",))
    kb.op(G, lambda e: e.memset(epsc[:, 2:3], KQ_EPS), writes=("epsc",))
    kb.op(G, lambda e: e.memset(epsc[:, 3:4], 1e-38), writes=("epsc",))
    kb.op(V, lambda e: e.tensor_scalar(iden_f[:, :], iota_f[:, :], iota_p[:, :],
                                       None, ALU.is_equal),
          reads=("iota_f", "iota_p"), writes=("iden_f",))
    kb.op(V, lambda e: e.tensor_scalar(Jmat[:, :], iota_f[:, :], iota_p[:, :],
                                       127.0, ALU.add, ALU.is_equal),
          reads=("iota_f", "iota_p"), writes=("Jmat",))
    s2i = sb("s2i", (2, 128), F32)
    sel2F = sb("sel2F", (2, 128), F32)
    ip2 = sb("ip2", (2, 1), F32)
    kb.op(G, lambda e: e.iota(s2i[:, :], [[1, 2], [0, 64]], channel_multiplier=0,
                              allow_small_or_imprecise_dtypes=True),
          writes=("s2i",))
    kb.op(G, lambda e: e.iota(ip2[:, :], [[0, 1]], channel_multiplier=1,
                              allow_small_or_imprecise_dtypes=True),
          writes=("ip2",))
    kb.op(V, lambda e: e.tensor_scalar(sel2T[:, :], s2i[:, :], ip2[:, :], None,
                                       ALU.is_equal),
          reads=("s2i", "ip2"), writes=("sel2T",))
    kb.op(V, lambda e: e.tensor_scalar(sel2F[:, :], s2i[:, :], ip2[:, :], None,
                                       ALU.is_equal),
          reads=("s2i", "ip2"), writes=("sel2F",))
    kb.op(TE, lambda e: e.transpose(pb[0][:, 0:2], sel2F[:, :],
                                    iden_f[0:2, 0:2]),
          reads=("sel2F", "iden_f"), writes=("pb0",))
    kb.op(V, lambda e: e.tensor_copy(selq[:, :], pb[0][:, 0:2]),
          reads=("pb0",), writes=("selq",))

    st_cw = ExitStack()
    cwT = sb("cwT", (NTRU, T), BF16, st_cw)
    ohE = sb("ohE", (NTRU, NTRU * 128), BF16, st_cw)
    st_a = ExitStack()
    XNF = [sb(f"xnf{c}", (128, T), F32, st_a) for c in range(8)]
    st_a2 = ExitStack()
    EsS = sb("EsS", (32, CP), F32, st_a2)
    cpadS = sb("cpadS", (H, CP), F32, st_a2)
    xt = [sb(f"xt{i}", (128, D), F32, st_a2) for i in range(2)]
    xnt = [sb(f"xnt{i}", (128, D), F32, st_a2) for i in range(2)]
    sqt = [sb(f"sqt{i}", (128, D), F32, st_a2) for i in range(2)]
    st_s = sb("st_s", (128, 16), F32, st_a2)

    ld(EsS[:, :], w_d["E"][:, :], writes=("EsS",))
    kb.finish_queue(MISC, ["relbS", "relbT", "scale16", "scaleP", "gbc",
                           "ebCol", "EsS"])
    kb.finish_queue("qC", ["ln1b", "ln2b"])
    kb.op(S, lambda e: e.activation(scale16[:, 1:2], scale16[:, 0:1], AF.Abs),
          reads=("scale16",), writes=("scale16",))
    kb.op(V, lambda e: e.reduce_max(gmaxc[:, 0:1], relbT[:, :], axis=AX.X,
                                    apply_absolute_value=True),
          reads=("relbT",), writes=("gmaxc",))
    kb.op(V, lambda e: e.tensor_scalar(gmaxc[:, 0:1], gmaxc[:, 0:1], QK_BOUND,
                                       None, ALU.add),
          reads=("gmaxc",), writes=("gmaxc",))
    kb.op(V, lambda e: e.tensor_tensor(gmaxc[:, 1:2], gmaxc[:, 0:1],
                                       scale16[:, 1:2], ALU.mult),
          reads=("gmaxc", "scale16"), writes=("gmaxc",))
    ofs = 0
    while ofs < CP:
        w = min(512, CP - ofs)
        bk = (ofs // 512) % 2
        kb.op(TE, lambda e, o=ofs, w=w, bk=bk:
              e.matmul(pb[bk][0:H, 0:w], relbS[:, :], EsS[:, o:o + w],
                       start=True, stop=True),
              reads=("relbS", "EsS"), writes=(f"pb{bk}",))
        kb.op(V, lambda e, o=ofs, w=w, bk=bk:
              e.tensor_scalar(cpadS[:, o:o + w], pb[bk][0:H, 0:w],
                              scale16[:, 0:1], gmaxc[:, 1:2],
                              ALU.mult, ALU.subtract),
              reads=(f"pb{bk}", "scale16", "gmaxc"), writes=("cpadS",))
        ofs += w
    kb.op(V, lambda e: e.memset(cpadS[:, 0:128], NEG), writes=("cpadS",))
    kb.op(S, lambda e: e.activation(cpadS[:, :], cpadS[:, :], AF.Exp),
          reads=("cpadS",), writes=("cpadS",))
    ld(scr_d[:, :], cpadS[:, :], sem="qS", reads=("cpadS",), writes=("cscr",))

    for tt in range(NTT):
        p = tt % 2
        o = p * 8
        sk = f"st_s{p}"
        ld(xt[p][:, :], x_d[tt * 128:(tt + 1) * 128, :], sem=f"qX{p}",
           writes=(f"xt{p}",))
        kb.op(V, lambda e, p=p, o=o:
              e.reduce_sum(st_s[:, o:o + 1], xt[p][:, :], axis=AX.X),
              reads=(f"xt{p}",), writes=(sk,))
        kb.op(V, lambda e, p=p: e.tensor_tensor(sqt[p][:, :], xt[p][:, :],
                                                xt[p][:, :], ALU.mult),
              reads=(f"xt{p}",), writes=(f"sqt{p}",))
        kb.op(V, lambda e, p=p, o=o:
              e.reduce_sum(st_s[:, o + 1:o + 2], sqt[p][:, :], axis=AX.X),
              reads=(f"sqt{p}",), writes=(sk,))
        kb.op(V, lambda e, o=o:
              e.tensor_scalar(st_s[:, o + 2:o + 3], st_s[:, o:o + 1], 1.0 / D,
                              None, ALU.mult),
              reads=(sk,), writes=(sk,))
        kb.op(V, lambda e, o=o:
              e.tensor_tensor(st_s[:, o + 3:o + 4], st_s[:, o + 2:o + 3],
                              st_s[:, o + 2:o + 3], ALU.mult),
              reads=(sk,), writes=(sk,))
        kb.op(V, lambda e, o=o:
              e.tensor_scalar(st_s[:, o + 4:o + 5], st_s[:, o + 1:o + 2],
                              1.0 / D, st_s[:, o + 3:o + 4],
                              ALU.mult, ALU.subtract),
              reads=(sk,), writes=(sk,))
        kb.op(S, lambda e, o=o:
              e.activation(st_s[:, o + 5:o + 6], st_s[:, o + 4:o + 5], AF.Ln,
                           bias=epsc[:, 0:1]),
              reads=(sk, "epsc"), writes=(sk,))
        kb.op(S, lambda e, o=o:
              e.activation(st_s[:, o + 6:o + 7], st_s[:, o + 5:o + 6], AF.Exp,
                           scale=-0.5),
              reads=(sk,), writes=(sk,))
        kb.op(V, lambda e, p=p, o=o:
              e.tensor_scalar(xnt[p][:, :], xt[p][:, :],
                              st_s[:, o + 2:o + 3], st_s[:, o + 6:o + 7],
                              ALU.subtract, ALU.mult),
              reads=(f"xt{p}", sk), writes=(f"xnt{p}",))
        for c in range(8):
            bk = 6 + (c % 2)
            sl = ((c // 2) % 2) * 128
            kb.op(TE, lambda e, p=p, c=c, bk=bk, sl=sl:
                  e.transpose(pb[bk][:, sl:sl + 128],
                              xnt[p][:, c * 128:(c + 1) * 128], iden_f[:, :]),
                  reads=(f"xnt{p}", "iden_f"), writes=(f"pb{bk}",))
            kb.op(S, lambda e, tt=tt, c=c, bk=bk, sl=sl:
                  e.copy(XNF[c][:, tt * 128:(tt + 1) * 128],
                         pb[bk][:, sl:sl + 128]),
                  reads=(f"pb{bk}",), writes=(f"xnf{c}",))
    for c in range(8):
        kb.op(G, lambda e, c=c: e.tensor_copy(XN16[c][:, :], XNF[c][:, :]),
              reads=(f"xnf{c}",), writes=(f"xn16_{c}",))
    st_a2.close()
    kb.fence()

    st_b1 = ExitStack()
    gwS = sb("gwS", (NEXP, D), F32, st_b1)
    gwT = sb("gwT", (128, 80), F32, st_b1)
    lgS = sb("lgS", (NEXP, T), F32, st_b1)
    lgT = sb("lgT", (128, NTT * 16), F32, st_b1)
    cwTok = sb("cwTok", (128, NTT * 8), F32, st_b1)

    ld(gwS[:, :], w_d["gw"][:, :], writes=("gwS",))
    kb.finish_queue(MISC, ["gwS"])
    oh_i = sb("oh_i", (NTRU, NTRU * 128), F32, st_b1)
    ip8 = sb("ip8", (NTRU, 1), F32, st_b1)
    kb.op(G, lambda e: e.iota(oh_i[:, :], [[1, NTRU], [0, 128]],
                              channel_multiplier=0,
                              allow_small_or_imprecise_dtypes=True),
          writes=("oh_i",))
    kb.op(G, lambda e: e.iota(ip8[:, :], [[0, 1]], channel_multiplier=1,
                              allow_small_or_imprecise_dtypes=True),
          writes=("ip8",))
    kb.op(V, lambda e: e.tensor_scalar(ohE[:, :], oh_i[:, :], ip8[:, :], None,
                                       ALU.is_equal),
          reads=("oh_i", "ip8"), writes=("ohE",))
    for c in range(8):
        bk = c % 2
        kb.op(TE, lambda e, c=c, bk=bk:
              e.transpose(pb[bk][:, 0:NEXP], gwS[:, c * 128:(c + 1) * 128],
                          iden_f[0:NEXP, 0:NEXP]),
              reads=("gwS", "iden_f"), writes=(f"pb{bk}",))
        kb.op(V, lambda e, c=c, bk=bk:
              e.tensor_copy(gwT[:, c * 10:(c + 1) * 10], pb[bk][:, 0:NEXP]),
              reads=(f"pb{bk}",), writes=("gwT",))
    for pp in range(NP):
        for c in range(8):
            kb.op(TE, lambda e, pp=pp, c=c:
                  e.matmul(pb[4 + pp % 4][0:NEXP, 0:512],
                           gwT[:, c * 10:(c + 1) * 10],
                           XNF[c][:, pp * 512:(pp + 1) * 512],
                           start=(c == 0), stop=(c == 7)),
                  reads=("gwT", f"xnf{c}"), writes=(f"pb{4 + pp % 4}",),
                  inc=(c == 7))
        kb.op(V, lambda e, pp=pp:
              e.tensor_scalar(lgS[:, pp * 512:(pp + 1) * 512],
                              pb[4 + pp % 4][0:NEXP, 0:512], gbc[:, 0:1],
                              None, ALU.add),
              reads=(f"pb{4 + pp % 4}", "gbc"), writes=("lgS",))
    for tt in range(NTT):
        bk = tt % 2
        kb.op(TE, lambda e, tt=tt, bk=bk:
              e.transpose(pb[bk][:, 0:NEXP], lgS[:, tt * 128:(tt + 1) * 128],
                          iden_f[0:NEXP, 0:NEXP]),
              reads=("lgS", "iden_f"), writes=(f"pb{bk}",))
        kb.op(V, lambda e, tt=tt, bk=bk:
              e.tensor_copy(lgT[:, tt * 16:tt * 16 + NEXP], pb[bk][:, 0:NEXP]),
              reads=(f"pb{bk}",), writes=("lgT",))
    gcol = sb("gcol", (128, 16 * 8), F32, st_b1)
    gw1 = sb("gw1", (128, NTT * 16), F32, st_b1)
    gw2 = sb("gw2", (128, NTT * 16), F32, st_b1)
    gw3 = sb("gw3", (128, NTT * 16), F32, st_b1)

    def d3(h, width=16, n=10):
        return A(h, 0, [[NTT * width, 128], [width, NTT], [1, n]])

    def b3(slot):
        return A(gcol, slot * NTT, [[16 * 8, 128], [1, NTT], [0, 10]])

    def gcolf(slot):
        return gcol[:, slot * NTT:(slot + 1) * NTT]

    iota3 = A(iota10, 0, [[12, 128], [0, NTT], [1, 10]])
    iotaB3 = A(iotaB, 0, [[12, 128], [0, NTT], [1, 10]])

    def vg(fn, reads, writes):
        kb.op(V, fn, reads=reads, writes=writes)

    vg(lambda e: e.tensor_reduce(gcolf(0), d3(lgT), AX.X, ALU.max),
       ("lgT",), ("gcol",))
    vg(lambda e: e.tensor_tensor(d3(gw1), d3(lgT), b3(0), ALU.subtract),
       ("lgT", "gcol"), ("gw1",))
    kb.op(S, lambda e: e.activation(d3(gw1), d3(gw1), AF.Exp),
          reads=("gw1",), writes=("gw1",))
    vg(lambda e: e.tensor_reduce(gcolf(1), d3(gw1), AX.X, ALU.add),
       ("gw1",), ("gcol",))
    vg(lambda e: e.reciprocal(gcolf(1), gcolf(1)), ("gcol",), ("gcol",))
    vg(lambda e: e.tensor_tensor(d3(gw1), d3(gw1), b3(1), ALU.mult),
       ("gw1", "gcol"), ("gw1",))
    vg(lambda e: e.tensor_reduce(gcolf(2), d3(gw1), AX.X, ALU.max),
       ("gw1",), ("gcol",))
    vg(lambda e: e.tensor_tensor(d3(gw2), d3(gw1), b3(2), ALU.is_equal),
       ("gw1", "gcol"), ("gw2",))
    vg(lambda e: e.scalar_tensor_tensor(d3(gw3), d3(gw2), -1.0e4, iotaB3,
                                        ALU.mult, ALU.add),
       ("gw2", "iotaB"), ("gw3",))
    vg(lambda e: e.tensor_reduce(gcolf(3), d3(gw3), AX.X, ALU.min),
       ("gw3",), ("gcol",))
    vg(lambda e: e.tensor_tensor(d3(gw2), iota3, b3(3), ALU.is_equal),
       ("iota10", "gcol"), ("gw2",))
    vg(lambda e: e.scalar_tensor_tensor(d3(gw3), d3(gw2), -BIG, d3(gw1),
                                        ALU.mult, ALU.add),
       ("gw2", "gw1"), ("gw3",))
    vg(lambda e: e.tensor_reduce(gcolf(4), d3(gw3), AX.X, ALU.max),
       ("gw3",), ("gcol",))
    vg(lambda e: e.tensor_tensor(d3(gw1), d3(gw3), b3(4), ALU.is_equal),
       ("gw3", "gcol"), ("gw1",))
    vg(lambda e: e.scalar_tensor_tensor(d3(gw3), d3(gw1), -1.0e4, iotaB3,
                                        ALU.mult, ALU.add),
       ("gw1", "iotaB"), ("gw3",))
    vg(lambda e: e.tensor_reduce(gcolf(5), d3(gw3), AX.X, ALU.min),
       ("gw3",), ("gcol",))
    vg(lambda e: e.tensor_tensor(d3(gw3), iota3, b3(5), ALU.is_equal),
       ("iota10", "gcol"), ("gw3",))
    vg(lambda e: e.tensor_scalar(gcolf(6), gcolf(3), 7.5, None, ALU.is_lt),
       ("gcol",), ("gcol",))
    vg(lambda e: e.tensor_tensor(gcolf(6), gcolf(6), gcolf(2), ALU.mult),
       ("gcol",), ("gcol",))
    vg(lambda e: e.tensor_scalar(gcolf(7), gcolf(5), 7.5, None, ALU.is_lt),
       ("gcol",), ("gcol",))
    vg(lambda e: e.tensor_tensor(gcolf(7), gcolf(7), gcolf(4), ALU.mult),
       ("gcol",), ("gcol",))
    vg(lambda e: e.tensor_tensor(gcolf(2), gcolf(6), gcolf(7), ALU.add),
       ("gcol",), ("gcol",))
    vg(lambda e: e.tensor_scalar(gcolf(3), gcolf(2), 0.0, None, ALU.is_le),
       ("gcol",), ("gcol",))
    vg(lambda e: e.tensor_tensor(gcolf(2), gcolf(2), gcolf(3), ALU.add),
       ("gcol",), ("gcol",))
    vg(lambda e: e.reciprocal(gcolf(2), gcolf(2)), ("gcol",), ("gcol",))
    vg(lambda e: e.tensor_tensor(gcolf(6), gcolf(6), gcolf(2), ALU.mult),
       ("gcol",), ("gcol",))
    vg(lambda e: e.tensor_tensor(gcolf(7), gcolf(7), gcolf(2), ALU.mult),
       ("gcol",), ("gcol",))
    vg(lambda e: e.tensor_tensor(d3(gw2, n=8), d3(gw2, n=8),
                                 A(gcol, 6 * NTT, [[16 * 8, 128], [1, NTT], [0, 8]]),
                                 ALU.mult),
       ("gw2", "gcol"), ("gw2",))
    vg(lambda e: e.tensor_tensor(d3(gw3, n=8), d3(gw3, n=8),
                                 A(gcol, 7 * NTT, [[16 * 8, 128], [1, NTT], [0, 8]]),
                                 ALU.mult),
       ("gw3", "gcol"), ("gw3",))
    vg(lambda e: e.tensor_tensor(A(cwTok, 0, [[NTT * 8, 128], [8, NTT], [1, 8]]),
                                 d3(gw2, n=8), d3(gw3, n=8), ALU.add),
       ("gw2", "gw3"), ("cwTok",))
    for tt in range(NTT):
        bk = tt % 2
        kb.op(TE, lambda e, tt=tt, bk=bk:
              e.transpose(pb[bk][0:8, 0:128], cwTok[:, tt * 8:(tt + 1) * 8],
                          iden_f[:, :]),
              reads=("cwTok", "iden_f"), writes=(f"pb{bk}",))
        kb.op(V, lambda e, tt=tt, bk=bk:
              e.tensor_copy(cwT[:, tt * 128:(tt + 1) * 128], pb[bk][0:8, 0:128]),
              reads=(f"pb{bk}",), writes=("cwT",))
    st_b1.close()
    st_a.close()
    kb.fence()

    st_b2 = ExitStack()
    ACC = [sb(f"acc{c}", (128, T), F32, st_b2) for c in range(8)]
    cwbA = [sb(f"cwb{e}", (128, T), BF16, st_b2) for e in range(NTRU)]
    mtmp = [sb(f"mtmp{i}", (128, T), F32, st_b2) for i in range(2)]
    We16 = [sb(f"we16_{i}", (128, 4096), BF16, st_b2) for i in range(2)]

    for e_ in range(NTRU):
        for pp in range(NP):
            bk = pp % 2
            kb.op(TE, lambda e, e_=e_, pp=pp, bk=bk:
                  e.matmul(pb[bk][:, 0:512], ohE[:, e_ * 128:(e_ + 1) * 128],
                           cwT[:, pp * 512:(pp + 1) * 512],
                           start=True, stop=True),
                  reads=("ohE", "cwT"), writes=(f"pb{bk}",))
            kb.op(V, lambda e, e_=e_, pp=pp, bk=bk:
                  e.tensor_copy(cwbA[e_][:, pp * 512:(pp + 1) * 512],
                                pb[bk][:, 0:512]),
                  reads=(f"pb{bk}",), writes=(f"cwb{e_}",))

    def ew_load(e_, hlf, par):
        for k in range(8):
            ld(We16[par][:, k * 512:(k + 1) * 512],
               w_d["ew"][e_, k * 128:(k + 1) * 128, hlf * 512:(hlf + 1) * 512],
               cast=True, sem=f"qW{par}", writes=(f"we16_{par}",),
               silent=(k > 0))
        kb.finish_queue(f"qW{par}", [f"we16_{par}"])

    ew_load(0, 0, 0)
    for e_ in range(NTRU):
        for hlf in range(2):
            par = hlf
            nxt_e, nxt_h = (e_, 1) if hlf == 0 else (e_ + 1, 0)
            if nxt_e < NTRU:
                ew_load(nxt_e, nxt_h, 1 - hlf)
            for dj in range(4):
                dout = hlf * 4 + dj
                col = dj * 128
                pbase = (dout % 2) * 4
                for k in range(8):
                    for pp in range(NP):
                        bk = pbase + pp
                        kb.op(TE, lambda e, par=par, k=k, col=col, pp=pp, bk=bk:
                              e.matmul(pb[bk][:, 0:512],
                                       We16[par][:, k * 512 + col:
                                                 k * 512 + col + 128],
                                       XN16[k][:, pp * 512:(pp + 1) * 512],
                                       start=(k == 0), stop=(k == 7)),
                              reads=(f"we16_{par}", f"xn16_{k}"),
                              writes=(f"pb{bk}",), inc=(k == 7))
                mp = dout % 2
                for pp in range(NP):
                    bk = pbase + pp
                    if e_ == 0:
                        kb.op(V, lambda e, e_=e_, dout=dout, pp=pp, bk=bk:
                              e.scalar_tensor_tensor(
                                  ACC[dout][:, pp * 512:(pp + 1) * 512],
                                  pb[bk][:, 0:512],
                                  ebCol[:, e_ * 8 + dout:e_ * 8 + dout + 1],
                                  cwbA[e_][:, pp * 512:(pp + 1) * 512],
                                  ALU.add, ALU.mult),
                              reads=(f"pb{bk}", "ebCol", f"cwb{e_}"),
                              writes=(f"acc{dout}",))
                    else:
                        kb.op(V, lambda e, e_=e_, dout=dout, pp=pp, bk=bk,
                              mp=mp:
                              e.scalar_tensor_tensor(
                                  mtmp[mp][:, pp * 512:(pp + 1) * 512],
                                  pb[bk][:, 0:512],
                                  ebCol[:, e_ * 8 + dout:e_ * 8 + dout + 1],
                                  cwbA[e_][:, pp * 512:(pp + 1) * 512],
                                  ALU.add, ALU.mult),
                              reads=(f"pb{bk}", "ebCol", f"cwb{e_}"),
                              writes=(f"mtmp{mp}",))
                if e_ > 0:
                    kb.op(G, lambda e, dout=dout, mp=mp:
                          e.tensor_tensor(ACC[dout][:, :], ACC[dout][:, :],
                                          mtmp[mp][:, :], ALU.add),
                          reads=(f"acc{dout}", f"mtmp{mp}"),
                          writes=(f"acc{dout}",))
    for c in range(8):
        ld(acc_d[c * 128:(c + 1) * 128, :], ACC[c][:, :], sem="qA0",
           reads=(f"acc{c}",), writes=(f"accp{c}",))
    st_b2.close()
    st_cw.close()
    kb.fence()

    st_c_k = ExitStack()
    K16 = [sb(f"k16_{c}", (128, T), BF16, st_c_k) for c in range(8)]
    st_c_bc = ExitStack()
    Y16p = [sb(f"y16p_{c}", (128, T), BF16, st_c_bc) for c in range(8)]
    Q16 = [sb(f"q16_{c}", (128, T), BF16, st_c_bc) for c in range(8)]
    st_c_vt = ExitStack()
    VtL = [sb(f"vtl{t}", (128, H * 65), BF16, st_c_vt) for t in range(NTT)]
    st_c0 = ExitStack()
    sqP = [sb(f"sq{i}", (128, T), BF16, st_c0) for i in range(2)]
    Vf32 = sb("vf32", (128, T), F32, st_c0)
    rowscrP = sb("rowscrP", (2, 512), F32, st_c0)
    rowb16 = [sb(f"rowb16_{i}", (2, T), BF16, st_c0) for i in range(2)]
    Wst = [sb(f"wst{i}", (128, 1024), BF16, st_c0) for i in range(2)]

    XN2 = Y16p

    def wload(par, dram, r0, c0, c1, key, sem=None):
        for k in range(8):
            ld(Wst[par][:, k * (c1 - c0):(k + 1) * (c1 - c0)],
               dram[r0 + k * 128:r0 + (k + 1) * 128, c0:c1], cast=True,
               sem=sem or f"qW{par}", writes=(key,), silent=(k > 0))
        kb.finish_queue(sem or f"qW{par}", [key])

    wload(0, w_d["Wq"], 0, 0, 128, "wst0")
    wload(1, w_d["Wq"], 0, 128, 256, "wst1")

    for c in range(8):
        kb.op(V, lambda e, c=c: e.tensor_tensor(sqP[c % 2][:, :], XN16[c][:, :],
                                                XN16[c][:, :], ALU.mult),
              reads=(f"xn16_{c}",), writes=(f"sq{c % 2}",))
        for pp in range(NP):
            bk = 4 + pp
            kb.op(TE, lambda e, c=c, pp=pp, bk=bk:
                  e.matmul(pb[bk][0:1, 0:512], onesb[:, 0:1],
                           sqP[c % 2][:, pp * 512:(pp + 1) * 512],
                           start=(c == 0), stop=(c == 7)),
                  reads=(f"sq{c % 2}", "onesb"), writes=(f"pb{bk}",), inc=True)
    for pp in range(NP):
        kb.op(S, lambda e, pp=pp:
              e.activation(rowscrP[0:1, :], pb[4 + pp][0:1, 0:512], AF.Ln,
                           bias=epsc[0:1, 1:2], scale=1.0 / D),
              reads=(f"pb{4 + pp}", "epsc"), writes=("rowscrP",))
        kb.op(S, lambda e, pp=pp:
              e.activation(rowb16[0][0:1, pp * 512:(pp + 1) * 512],
                           rowscrP[0:1, :], AF.Exp, scale=-0.5),
              reads=("rowscrP",), writes=("rowb16_0",))
    for c in range(8):
        for pp in range(NP):
            bk = 4 + pp % 2
            kb.op(TE, lambda e, c=c, pp=pp, bk=bk:
                  e.matmul(pb[bk][:, 0:512], ln1b[:, c * 128:(c + 1) * 128],
                           rowb16[0][0:1, pp * 512:(pp + 1) * 512],
                           start=True, stop=True),
                  reads=("ln1b", "rowb16_0"), writes=(f"pb{bk}",))
            kb.op(V, lambda e, c=c, pp=pp, bk=bk:
                  e.tensor_tensor(XN2[c][:, pp * 512:(pp + 1) * 512],
                                  XN16[c][:, pp * 512:(pp + 1) * 512],
                                  pb[bk][:, 0:512], ALU.mult),
                  reads=(f"xn16_{c}", f"pb{bk}"), writes=(f"y16p_{c}",))
        pbase0 = 0 if NP > 2 else 0
        for pp in range(NP):
            kb.op(TE, lambda e, c=c, pp=pp:
                  e.matmul(pb[pp][:, 0:512], Wst[0][:, c * 128:(c + 1) * 128],
                           XN2[c][:, pp * 512:(pp + 1) * 512],
                           start=(c == 0), stop=(c == 7)),
                  reads=("wst0", f"y16p_{c}"), writes=(f"pb{pp}",),
                  inc=(c == 7))
    for pp in range(NP):
        kb.op(V, lambda e, pp=pp:
              e.tensor_copy(Q16[0][:, pp * 512:(pp + 1) * 512],
                            pb[pp][:, 0:512]),
              reads=(f"pb{pp}",), writes=("q16_0",))

    def proj_unit(wname, dst, dkey, dout, nxt):
        def go():
            par = dout % 2
            if nxt is not None:
                wload(1 - par, w_d[nxt[0]], 0, nxt[1] * 128,
                      (nxt[1] + 1) * 128, f"wst{1 - par}")
            pbase = (dout % 2) * 2 if NP <= 2 else 0
            for k in range(8):
                for pp in range(NP):
                    bk = pbase + pp if NP <= 2 else pp
                    kb.op(TE, lambda e, par=par, k=k, pp=pp, bk=bk:
                          e.matmul(pb[bk][:, 0:512],
                                   Wst[par][:, k * 128:(k + 1) * 128],
                                   XN2[k][:, pp * 512:(pp + 1) * 512],
                                   start=(k == 0), stop=(k == 7)),
                          reads=(f"wst{par}", f"y16p_{k}"),
                          writes=(f"pb{bk}",), inc=(k == 7))
            for pp in range(NP):
                bk = (pbase + pp) if NP <= 2 else pp
                if dst is None:
                    kb.op(V, lambda e, pp=pp, bk=bk:
                          e.tensor_copy(Vf32[:, pp * 512:(pp + 1) * 512],
                                        pb[bk][:, 0:512]),
                          reads=(f"pb{bk}",), writes=("vf32",))
                else:
                    kb.op(V, lambda e, dout=dout, pp=pp, bk=bk:
                          e.tensor_copy(dst[dout][:, pp * 512:(pp + 1) * 512],
                                        pb[bk][:, 0:512]),
                          reads=(f"pb{bk}",), writes=(dkey.format(dout),))
        return go

    def qn_sq(XB, keyfmt, c):
        pr = c % 2
        kb.op(V, lambda e, c=c, pr=pr:
              e.tensor_tensor(sqP[pr][:, :], XB[c][:, :], XB[c][:, :],
                              ALU.mult),
              reads=(keyfmt.format(c),), writes=(f"sq{pr}",))

    def qn_stats(XB, keyfmt, c, fold_scale, bank0):
        pr = c % 2
        for pp in range(NP):
            bk = bank0 + pp % 2
            kb.op(TE, lambda e, pr=pr, pp=pp, bk=bk:
                  e.matmul(pb[bk][0:2, 0:512], selq[:, :],
                           sqP[pr][:, pp * 512:(pp + 1) * 512],
                           start=True, stop=True),
                  reads=(f"sq{pr}", "selq"), writes=(f"pb{bk}",))
            kb.op(S, lambda e, bk=bk:
                  e.activation(rowscrP[0:2, :], pb[bk][0:2, 0:512], AF.Ln,
                               bias=epsc[0:2, 2:3]),
                  reads=(f"pb{bk}", "epsc"), writes=("rowscrP",))
            kb.op(S, lambda e, pr=pr, pp=pp:
                  e.activation(rowb16[pr][0:2, pp * 512:(pp + 1) * 512],
                               rowscrP[0:2, :], AF.Exp, scale=-0.5),
                  reads=("rowscrP",), writes=(f"rowb16_{pr}",))
            if fold_scale:
                kb.op(V, lambda e, c=c, pr=pr, pp=pp:
                      e.tensor_scalar(rowb16[pr][0:2, pp * 512:(pp + 1) * 512],
                                      rowb16[pr][0:2, pp * 512:(pp + 1) * 512],
                                      scaleP[:, c:c + 1], None, ALU.mult),
                      reads=(f"rowb16_{pr}", "scaleP"),
                      writes=(f"rowb16_{pr}",))

    def qn_apply_piece(XB, keyfmt, c, pp, bk):
        pr = c % 2
        kb.op(TE, lambda e, pr=pr, pp=pp, bk=bk:
              e.matmul(pb[bk][:, 0:512], sel2T[:, :],
                       rowb16[pr][0:2, pp * 512:(pp + 1) * 512],
                       start=True, stop=True),
              reads=(f"rowb16_{pr}", "sel2T"), writes=(f"pb{bk}",))
        kb.op(V, lambda e, c=c, pp=pp, bk=bk:
              e.tensor_tensor(XB[c][:, pp * 512:(pp + 1) * 512],
                              XB[c][:, pp * 512:(pp + 1) * 512],
                              pb[bk][:, 0:512], ALU.mult),
              reads=(keyfmt.format(c), f"pb{bk}"), writes=(keyfmt.format(c),))

    for d in range(1, 8):
        nxt = ("Wq", d + 1) if d < 7 else ("Wk", 0)
        proj_unit("Wq", Q16, "q16_{}", d, nxt)()
    for d in range(8):
        nxt = ("Wk", d + 1) if d < 7 else ("Wv", 0)
        qn_sq(Q16, "q16_{}", d)
        if d > 0:
            for pp in range(NP):
                qn_apply_piece(Q16, "q16_{}", d - 1, pp, 6 + pp % 2)
        proj_unit("Wk", K16, "k16_{}", d, nxt)()
        qn_stats(Q16, "q16_{}", d, True, 4)
    for pp in range(NP):
        qn_apply_piece(Q16, "q16_{}", 7, pp, 6 + pp % 2)

    def v_transpose(c, tt):
        reg = (tt % 4) * 128
        kb.op(TE, lambda e, tt=tt, reg=reg:
              e.transpose(pb[4][:, reg:reg + 128],
                          Vf32[:, tt * 128:(tt + 1) * 128], iden_f[:, :]),
              reads=("vf32", "iden_f"), writes=("pb4",))
        for hf in range(2):
            hh = 2 * c + hf
            if hf == 0:
                kb.op(V, lambda e, tt=tt, reg=reg, hh=hh:
                      e.tensor_copy(VtL[tt][:, hh * 65:hh * 65 + 64],
                                    pb[4][:, reg:reg + 64]),
                      reads=("pb4",), writes=(f"vtl{tt}",))
            else:
                kb.op(S, lambda e, tt=tt, reg=reg, hh=hh:
                      e.copy(VtL[tt][:, hh * 65:hh * 65 + 64],
                             pb[4][:, reg + 64:reg + 128]),
                      reads=("pb4",), writes=(f"vtl{tt}",))

    for d in range(8):
        par = d % 2
        qn_sq(K16, "k16_{}", d)
        if d < 7:
            wload(1 - par, w_d["Wv"], 0, (d + 1) * 128, (d + 2) * 128,
                  f"wst{1 - par}")
        pbase = (d % 2) * 2 if NP <= 2 else 0
        for k in range(8):
            for pp in range(NP):
                bk = pbase + pp if NP <= 2 else pp
                kb.op(TE, lambda e, par=par, k=k, pp=pp, bk=bk:
                      e.matmul(pb[bk][:, 0:512],
                               Wst[par][:, k * 128:(k + 1) * 128],
                               XN2[k][:, pp * 512:(pp + 1) * 512],
                               start=(k == 0), stop=(k == 7)),
                      reads=(f"wst{par}", f"y16p_{k}"),
                      writes=(f"pb{bk}",), inc=(k == 7))
            if d > 0 and 2 * k < NTT:
                v_transpose(d - 1, 2 * k)
                v_transpose(d - 1, 2 * k + 1)
            if d > 0 and k < NP:
                qn_apply_piece(K16, "k16_{}", d - 1, k, 7)
        for pp in range(NP):
            bk = pbase + pp if NP <= 2 else pp
            kb.op(V, lambda e, pp=pp, bk=bk:
                  e.tensor_copy(Vf32[:, pp * 512:(pp + 1) * 512],
                                pb[bk][:, 0:512]),
                  reads=(f"pb{bk}",), writes=("vf32",))
        qn_stats(K16, "k16_{}", d, False, 5)
    for tt in range(NTT):
        v_transpose(7, tt)
    for pp in range(NP):
        qn_apply_piece(K16, "k16_{}", 7, pp, 7)
    for tt in range(NTT):
        kb.op(V, lambda e, tt=tt:
              e.memset(A(VtL[tt], 64, [[H * 65, 128], [65, H]]), 1.0),
              writes=(f"vtl{tt}",))
    st_c0.close()
    kb.fence()

    st_ps_a.close()
    st_ps_c1 = ExitStack()
    s16 = [st_ps_c1.enter_context(nc.psum_tensor(f"s16_{i}", [128, 512], F32))
           for i in range(3)]
    pbc = [st_ps_c1.enter_context(nc.psum_tensor("pbc0", [128, 512], F32))]
    st_c1 = ExitStack()
    Tst = [sb(f"tst{i}", (128, T), BF16, st_c1) for i in range(2)]
    Pt = [sb(f"pt{i}", (128, T), BF16, st_c1) for i in range(4)]
    denr = sb("denr", (1, 512), F32, st_c1)
    denb = sb("denb", (1, 1024), BF16, st_c1)
    BC16 = sb("bc16", (64, 1024), BF16, st_c1)
    O16t = sb("o16t", (64, T), BF16, st_c1)

    ld(Tst[0][:, :], A(scr_d, 0, [[1, 128], [1, T]]), cast=True, sem="qT0",
       reads=("cscr",), writes=("tst0",))

    NB = min(4, NP)
    vsum = sb("vsum", (1, H * 65), F32, st_c1)
    vsumT = sb("vsumT", (65, H), F32, st_c1)
    for g0, g1 in ((0, 512), (512, 1024), (1024, H * 65)):
        for tt in range(NTT):
            kb.op(TE, lambda e, g0=g0, g1=g1, tt=tt:
                  e.matmul(pbc[0][0:1, 0:g1 - g0], onesb[:, 0:1],
                           VtL[tt][:, g0:g1], start=(tt == 0),
                           stop=(tt == NTT - 1)),
                  reads=(f"vtl{tt}", "onesb"), writes=("pbc0",),
                  inc=(tt == NTT - 1))
        kb.op(V, lambda e, g0=g0, g1=g1:
              e.tensor_copy(vsum[:, g0:g1], pbc[0][0:1, 0:g1 - g0]),
              reads=("pbc0",), writes=("vsum",))
    for h in range(H):
        kb.op(TE, lambda e, h=h:
              e.transpose(pbc[0][0:65, h:h + 1], vsum[0:1, h * 65:(h + 1) * 65],
                          iden_f[0:1, 0:1]),
              reads=("vsum", "iden_f"), writes=("pbc0",), inc=(h == H - 1))
    kb.op(V, lambda e: e.tensor_copy(vsumT[:, :], pbc[0][0:65, 0:H]),
          reads=("pbc0",), writes=("vsumT",))

    def mk_post_bank(h, b):
        c, hf = h // 2, h % 2

        def post():
            kb.op(TE, lambda e, b=b:
                  e.matmul(pbc[0][0:64, 0:512], onesbr[:, :],
                           denb[:, (b % 2) * 512:(b % 2) * 512 + 512],
                           start=True, stop=True),
                  reads=("denb", "onesbr"), writes=("pbc0",))
            kb.op(V, lambda e, b=b:
                  e.tensor_copy(BC16[:, (b % 2) * 512:(b % 2) * 512 + 512],
                                pbc[0][0:64, 0:512]),
                  reads=("pbc0",), writes=("bc16",))
            dst = (lambda sl: Y16p[c][0:64, sl]) if hf == 0 else                   (lambda sl: O16t[:, sl])
            dkey = f"y16p_{c}" if hf == 0 else "o16t"
            kb.op(V, lambda e, b=b, dst=dst:
                  e.tensor_tensor(dst(slice(b * 512, (b + 1) * 512)),
                                  pb[b][0:64, 0:512],
                                  BC16[:, (b % 2) * 512:(b % 2) * 512 + 512],
                                  ALU.mult),
                  reads=(f"pb{b}", "bc16"), writes=(dkey,))
            if b == 0:
                kb.op(V, lambda e, h=h, dst=dst:
                      e.tensor_scalar(dst(slice(0, 1)), vsumT[0:64, h:h + 1],
                                      1.0 / T, None, ALU.mult),
                      reads=("vsumT",), writes=(dkey,))
            if b == NB - 1 and hf == 1:
                kb.op(G, lambda e, c=c:
                      e.dma_start(Y16p[c][64:128, :], O16t[:, :]),
                      reads=("o16t",), writes=(f"y16p_{c}",), dma_sem="qO")
        return post

    fifo = []
    postq = []
    cct = [0]

    def pop_one():
        h, b, q, w, j0, kt, pslot, ptk, lb = fifo.pop(0)
        if kt == 0:
            keep = []
            for ent in postq:
                if ent[2] == b:
                    ent[0]()
                else:
                    keep.append(ent)
            postq[:] = keep
        kb.op(TE, lambda e, h=h, b=b, q=q, w=w, j0=j0, kt=kt, pslot=pslot,
              lb=lb:
              e.matmul(pb[b][0:65, q % 512:q % 512 + w],
                       VtL[kt][:, h * 65:(h + 1) * 65],
                       Pt[pslot][:, j0:j0 + w],
                       start=(kt == 0), stop=lb),
              reads=(f"vtl{kt}", ptk), writes=(f"pb{b}",), inc=lb)
        if lb:
            kb.op(S, lambda e, b=b:
                  e.activation(denr[:, :], pb[b][64:65, 0:512], AF.Ln,
                               bias=epsc[0:1, 3:4]),
                  reads=(f"pb{b}", "epsc"), writes=("denr",))
            kb.op(S, lambda e, b=b:
                  e.activation(denb[:, (b % 2) * 512:(b % 2) * 512 + 512],
                               denr[:, :], AF.Exp, scale=-1.0),
                  reads=("denr",), writes=("denb",))
            postq.append((mk_post_bank(h, b), cct[0], b))

    def tflip(par, pp):
        kb.op(TE, lambda e, par=par, pp=pp:
              e.matmul(pbc[0][:, 0:512], Jmat[:, :],
                       Tst[par][:, pp * 512:(pp + 1) * 512],
                       start=True, stop=True),
              reads=("Jmat", f"tst{par}"), writes=("pbc0",))
        kb.op(V, lambda e, par=par, pp=pp:
              e.tensor_copy(Tst[par][:, pp * 512:(pp + 1) * 512],
                            pbc[0][:, 0:512]),
              reads=("pbc0",), writes=(f"tst{par}",))

    for pp in range(NP):
        tflip(0, pp)
    for h in range(H):
        par = h % 2
        c, hf = h // 2, h % 2
        po = 64 * hf
        if h + 1 < H:
            ld(Tst[1 - par][:, :], A(scr_d, (h + 1) * CP, [[1, 128], [1, T]]),
               cast=True, sem=f"qT{1 - par}", reads=("cscr",),
               writes=(f"tst{1 - par}",))
        for kt in range(NTT):
            if NTT // 2 <= kt < NTT // 2 + NP and h + 1 < H:
                tflip(1 - par, kt - NTT // 2)
            ktg = h * NTT + kt
            while fifo and fifo[0][5] + (fifo[0][0] - h) * NTT <= kt - 3:
                pop_one()
            q0 = kt * 128
            cs = q0
            ci = 0
            while cs < T:
                while postq and postq[0][1] <= cct[0] - 2:
                    postq.pop(0)[0]()
                ce = min((cs // 512 + 1) * 512, T)
                w2 = ce - cs
                sbk = cct[0] % 3
                pslot = ktg % 4
                ptk = f"pt{pslot}_c{ci}"
                jc = cs - q0
                kb.op(TE, lambda e, sbk=sbk, c=c, po=po, kt=kt, q=cs, w=w2:
                      e.matmul(s16[sbk][:, 0:w],
                               K16[c][po:po + 64, kt * 128:(kt + 1) * 128],
                               Q16[c][po:po + 64, q:q + w],
                               start=True, stop=True),
                      reads=(f"k16_{c}", f"q16_{c}"),
                      writes=(f"s16_{sbk}",), inc=True)
                kb.op(S, lambda e, sbk=sbk, pslot=pslot, jc=jc, w2=w2:
                      e.activation(Pt[pslot][:, jc:jc + w2], s16[sbk][:, 0:w2],
                                   AF.Exp),
                      reads=(f"s16_{sbk}",), writes=(ptk,))
                kb.op(V, lambda e, par=par, pslot=pslot, jc=jc, w2=w2:
                      e.tensor_tensor(Pt[pslot][:, jc:jc + w2],
                                      Pt[pslot][:, jc:jc + w2],
                                      Tst[par][:, jc:jc + w2], ALU.mult),
                      reads=(ptk, f"tst{par}"), writes=(ptk,))
                bb = cs // 512
                lb = (kt == min(4 * bb + 3, NTT - 1))
                fifo.append((h, bb, cs, w2, jc, kt, pslot, ptk, lb))
                cct[0] += 1
                while len(fifo) > 24:
                    pop_one()
                cs = ce
                ci += 1
    while fifo:
        pop_one()
    while postq:
        postq.pop(0)[0]()
    st_c1.close()
    st_c_vt.close()
    st_ps_c1.close()
    for i in (4, 5, 6, 7):
        pb[i] = ES.enter_context(nc.psum_tensor(f"pb{i}b", [128, 512], F32))
    kb.fence()

    st_c3 = ExitStack()
    NPH = max(1, NP // 2)

    st_c2 = ExitStack()
    accw = [sb(f"accw{i}", (128, T), F32, st_c2) for i in range(2)]
    rows2P = sb("rows2P", (1, 512), F32, st_c2)
    rowb2 = sb("rowb2", (1, T), BF16, st_c2)
    sq2 = sb("sq2", (128, T), BF16, st_c2)
    Wst2 = [sb(f"ws2_{i}", (128, 2048), BF16, st_c2) for i in range(2)]

    def wload2(par, dram, r0, c0, c1, key):
        for k in range(8):
            ld(Wst2[par][:, k * (c1 - c0):(k + 1) * (c1 - c0)],
               dram[r0 + k * 128:r0 + (k + 1) * 128, c0:c1], cast=True,
               sem=f"qW{par}", writes=(key,), silent=(k > 0))
        kb.finish_queue(f"qW{par}", [key])

    Y16o = Q16
    wload2(0, w_d["Wo"], 0, 0, 256, "ws2_0")
    for qr in range(4):
        par = qr % 2
        if qr + 1 < 4:
            wload2(1 - par, w_d["Wo"], 0, (qr + 1) * 256, (qr + 2) * 256,
                   f"ws2_{1 - par}")
        for dj in range(2):
            dout = qr * 2 + dj
            pbase = (dout % 2) * 4
            apar = dout % 2
            ld(accw[apar][:, :], acc_d[dout * 128:(dout + 1) * 128, :],
               sem=f"qA{apar}", reads=(f"accp{dout}",), writes=(f"accw{apar}",))
            for k in range(8):
                for pp in range(NP):
                    kb.op(TE, lambda e, par=par, k=k, dj=dj, pp=pp, pbase=pbase:
                          e.matmul(pb[pbase + pp][:, 0:512],
                                   Wst2[par][:, k * 256 + dj * 128:
                                            k * 256 + dj * 128 + 128],
                                   Y16p[k][:, pp * 512:(pp + 1) * 512],
                                   start=(k == 0), stop=(k == 7)),
                          reads=(f"ws2_{par}", f"y16p_{k}"),
                          writes=(f"pb{pbase + pp}",), inc=(k == 7))
            for pp in range(NP):
                kb.op(V, lambda e, dout=dout, pp=pp, pbase=pbase:
                      e.tensor_tensor(Y16o[dout][:, pp * 512:(pp + 1) * 512],
                                      pb[pbase + pp][:, 0:512],
                                      XN16[dout][:, pp * 512:(pp + 1) * 512],
                                      ALU.add),
                      reads=(f"pb{pbase + pp}", f"xn16_{dout}"),
                      writes=(f"q16_{dout}",))
                kb.op(V, lambda e, apar=apar, pp=pp, pbase=pbase:
                      e.tensor_tensor(accw[apar][:, pp * 512:(pp + 1) * 512],
                                      accw[apar][:, pp * 512:(pp + 1) * 512],
                                      pb[pbase + pp][:, 0:512], ALU.add),
                      reads=(f"pb{pbase + pp}", f"accw{apar}"),
                      writes=(f"accw{apar}",))
            kb.op(G, lambda e, apar=apar, dout=dout:
                  e.tensor_tensor(accw[apar][:, :], accw[apar][:, :],
                                  XN16[dout][:, :], ALU.add),
                  reads=(f"xn16_{dout}", f"accw{apar}"), writes=(f"accw{apar}",))
            ld(acc_d[dout * 128:(dout + 1) * 128, :], accw[apar][:, :],
               sem=f"qA{apar}", reads=(f"accw{apar}",), writes=(f"accp{dout}",))
    for c in range(8):
        kb.op(V, lambda e, c=c: e.tensor_tensor(sq2[:, :], Y16o[c][:, :],
                                                Y16o[c][:, :], ALU.mult),
              reads=(f"q16_{c}",), writes=("sq2",))
        for pp in range(NP):
            bk = 4 + pp
            kb.op(TE, lambda e, pp=pp, bk=bk, c=c:
                  e.matmul(pb[bk][0:1, 0:512], onesb[:, 0:1],
                           sq2[:, pp * 512:(pp + 1) * 512],
                           start=(c == 0), stop=(c == 7)),
                  reads=("sq2", "onesb"), writes=(f"pb{bk}",), inc=True)
    for pp in range(NP):
        kb.op(S, lambda e, pp=pp:
              e.activation(rows2P[0:1, :], pb[4 + pp][0:1, 0:512], AF.Ln,
                           bias=epsc[0:1, 1:2], scale=1.0 / D),
              reads=(f"pb{4 + pp}", "epsc"), writes=("rows2P",))
        kb.op(S, lambda e, pp=pp:
              e.activation(rowb2[0:1, pp * 512:(pp + 1) * 512],
                           rows2P[0:1, :], AF.Exp, scale=-0.5),
              reads=("rows2P",), writes=("rowb2",))
    YN16 = K16
    for c in range(8):
        for pp in range(NP):
            bk = pp % 2
            kb.op(TE, lambda e, c=c, pp=pp, bk=bk:
                  e.matmul(pb[bk][:, 0:512], ln2b[:, c * 128:(c + 1) * 128],
                           rowb2[0:1, pp * 512:(pp + 1) * 512],
                           start=True, stop=True),
                  reads=("ln2b", "rowb2"), writes=(f"pb{bk}",))
            kb.op(V, lambda e, c=c, pp=pp, bk=bk:
                  e.tensor_tensor(YN16[c][:, pp * 512:(pp + 1) * 512],
                                  Y16o[c][:, pp * 512:(pp + 1) * 512],
                                  pb[bk][:, 0:512], ALU.mult),
                  reads=(f"q16_{c}", f"pb{bk}"), writes=(f"k16_{c}",))
    st_c2.close()
    st_c_bc.close()
    kb.fence()

    H16 = [sb(f"h16_{i}", (128, T // 2), BF16, st_c3) for i in range(24)] + \
          [XN16[i] for i in range(8)]
    W2R = sb("w2r", (128, 32 * D), BF16, st_c3)
    Wst1 = [sb(f"ws1_{i}", (128, 2048), BF16, st_c3) for i in range(2)]
    accw2 = [sb(f"acw2_{i}", (128, T // 2), F32, st_c3) for i in range(2)]

    def w1load(par, e16):
        for k in range(8):
            ld(Wst1[par][:, k * 256:(k + 1) * 256],
               w_d["W1"][k * 128:(k + 1) * 128, e16 * 256:(e16 + 1) * 256],
               cast=True, sem=f"qW{par}", writes=(f"ws1_{par}",),
               silent=(k > 0))
        kb.finish_queue(f"qW{par}", [f"ws1_{par}"])

    def w2rload(h0, h1):
        for hk in range(h0, h1):
            ld(W2R[:, hk * D:(hk + 1) * D],
               w_d["W2"][hk * 128:(hk + 1) * 128, :],
               cast=True, sem="qF", silent=(hk % 4 != 0))
            if hk % 4 == 3 or hk == h1 - 1:
                kb.finish_queue("qF", [f"w2r_{k}" for k in
                                       range(hk - hk % 4, hk + 1)])
    w1load(0, 0)
    w2rload(0, 4)
    w1load(1, 1)

    def w1_emit(tp2, e8):
        t0c = tp2 * (T // 2)
        par = e8 % 2
        if e8 + 1 < 16:
            w1load(1 - par, e8 + 1)
        for hl in range(2):
            hk = e8 * 2 + hl
            pbase = (hk % 2) * 2
            for k in range(8):
                for pc in range(NPH):
                    kb.op(TE, lambda e, par=par, k=k, hl=hl, pc=pc,
                          pbase=pbase, t0c=t0c:
                          e.matmul(pb[pbase + pc][:, 0:512],
                                   Wst1[par][:, k * 256 + hl * 128:
                                            k * 256 + hl * 128 + 128],
                                   YN16[k][:, t0c + pc * 512:
                                           t0c + (pc + 1) * 512],
                                   start=(k == 0), stop=(k == 7)),
                          reads=(f"ws1_{par}", f"k16_{k}"),
                          writes=(f"pb{pbase + pc}",), inc=(k == 7))
            for pc in range(NPH):
                kb.op(S, lambda e, hk=hk, pc=pc, pbase=pbase:
                      e.activation(H16[hk][:, pc * 512:(pc + 1) * 512],
                                   pb[pbase + pc][:, 0:512], AF.Relu),
                      reads=(f"pb{pbase + pc}",), writes=(f"h16_{hk}",))

    def w2_emit(dp, hk):
        for j, dout in enumerate((2 * dp, 2 * dp + 1)):
            for pc in range(NPH):
                bk = 4 + 2 * j + pc if NPH == 2 else 4 + j
                kb.op(TE, lambda e, dout=dout, pc=pc, bk=bk, hk=hk:
                      e.matmul(pb[bk][:, 0:512],
                               W2R[:, hk * D + dout * 128:
                                    hk * D + (dout + 1) * 128],
                               H16[hk][:, pc * 512:(pc + 1) * 512],
                               start=(hk == 0), stop=(hk == 31)),
                      reads=(f"w2r_{hk}", f"h16_{hk}"),
                      writes=(f"pb{bk}",),
                      inc=(j == 1 and pc == NPH - 1))

    def w2_evac_pre(tp2, dp):
        t0c = tp2 * (T // 2)
        for j, dout in enumerate((2 * dp, 2 * dp + 1)):
            ld(accw2[j][:, :], acc_d[dout * 128:(dout + 1) * 128,
                                     t0c:t0c + T // 2],
               sem=f"qA{j}", reads=(f"accp{dout}",), writes=(f"acw2_{j}",))

    def w2_evac(tp2, dp):
        t0c = tp2 * (T // 2)
        for j, dout in enumerate((2 * dp, 2 * dp + 1)):
            for pc in range(NPH):
                bk = 4 + 2 * j + pc if NPH == 2 else 4 + j
                kb.op(V, lambda e, j=j, pc=pc, bk=bk:
                      e.tensor_tensor(accw2[j][:, pc * 512:(pc + 1) * 512],
                                      accw2[j][:, pc * 512:(pc + 1) * 512],
                                      pb[bk][:, 0:512], ALU.add),
                      reads=(f"pb{bk}", f"acw2_{j}"), writes=(f"acw2_{j}",))
            ld(acc_d[dout * 128:(dout + 1) * 128, t0c:t0c + T // 2],
               accw2[j][:, :], sem=f"qA{j}", reads=(f"acw2_{j}",),
               writes=(f"accp{dout}_{tp2}",))

    for tp2 in range(2):
        if tp2 == 1:
            w1load(0, 0)
        w2_evac_pre(tp2, 0)
        w2lag = 0
        for e8 in range(16):
            if tp2 == 0:
                lo, hi = 4 + 2 * e8, min(6 + 2 * e8, 32)
                if lo < 32:
                    w2rload(lo, hi)
            w1_emit(tp2, e8)
            while w2lag <= 2 * e8 - 2:
                w2_emit(0, w2lag)
                w2lag += 1
        while w2lag < 32:
            w2_emit(0, w2lag)
            w2lag += 1
        w2_evac(tp2, 0)
        for dp in range(1, 4):
            w2_evac_pre(tp2, dp)
            for hk in range(32):
                w2_emit(dp, hk)
            w2_evac(tp2, dp)
    st_c3.close()
    st_c_k.close()
    kb.fence()

    st_d = ExitStack()
    ND = 8
    xo = [sb(f"xo{i}", (128, D), F32, st_d) for i in range(ND)]
    ot = [sb(f"ot{i}", (128, D), F32, st_d) for i in range(ND)]
    asl = [sb(f"asl{i}", (128, D), F32, st_d) for i in range(ND)]
    for tt in range(NTT):
        p = tt % ND
        hv = 0 if tt * 128 < T // 2 else 1
        ld(xo[p][:, :], x_d[tt * 128:(tt + 1) * 128, :], sem=f"qX{p % 4}",
           writes=(f"xo{p}",))
        kb.finish_queue(f"qX{p % 4}", [f"xo{p}"])
        src = A(acc_d, tt * 128,
                [[T, 128], [128 * T, 8], [1, 128]])
        dst = A(asl[p], 0, [[D, 128], [128, 8], [1, 128]])
        kb.op("sync", lambda e, d=dst, s=src: e.dma_start(d, s),
              reads=tuple(f"accp{c}_{hv}" for c in range(8)),
              writes=(f"asl{p}",), dma_sem=f"qD{p % 4}")
        kb.finish_queue(f"qD{p % 4}", [f"asl{p}"])
        for c in range(8):
            bk = 6 + (c % 2)
            sl = ((c // 2) % 2) * 128
            kb.op(TE, lambda e, p=p, c=c, bk=bk, sl=sl:
                  e.transpose(pb[bk][:, sl:sl + 128],
                              asl[p][:, c * 128:(c + 1) * 128], iden_f[:, :]),
                  reads=(f"asl{p}", "iden_f"), writes=(f"pb{bk}",))
            kb.op(V, lambda e, p=p, c=c, bk=bk, sl=sl:
                  e.tensor_tensor(ot[p][:, c * 128:(c + 1) * 128],
                                  pb[bk][:, sl:sl + 128],
                                  xo[p][:, c * 128:(c + 1) * 128], ALU.add),
                  reads=(f"pb{bk}", f"xo{p}"), writes=(f"ot{p}",))
        kb.op("sync", lambda e, tt=tt, p=p:
              e.dma_start(out_d[tt * 128:(tt + 1) * 128, :], ot[p][:, :]),
              reads=(f"ot{p}",), writes=("outd",), dma_sem="qOut")
    st_d.close()
    kb.fence()
    kb.wait("sync", "qOut", kb.cnt["qOut"])

    kb.emit()
    kb.stack.close()
    return nc


def kernel(**inputs):
    x = np.ascontiguousarray(np.asarray(inputs["L0_attn_x"], dtype=np.float32))
    T = x.shape[1]
    base = {
        "ln1": inputs["L0_attn_ln1_w"], "Wq": inputs["L0_attn_Wq"],
        "Wk": inputs["L0_attn_Wk"], "Wv": inputs["L0_attn_Wv"],
        "relb": inputs["L0_attn_rel_bias"], "scale": inputs["L0_attn_scale"],
        "Wo": inputs["L0_attn_Wo"], "ln2": inputs["L0_attn_ln2_w"],
        "W1": inputs["L0_attn_W1"], "W2": inputs["L0_attn_W2"],
        "gw": inputs["L0_ffn_gate_w"], "gb": inputs["L0_ffn_gate_b"],
        "ew": inputs["L0_ffn_expert_w"], "eb": inputs["L0_ffn_expert_b"],
    }
    base = {k: np.ascontiguousarray(np.asarray(v, dtype=np.float32))
            for k, v in base.items()}
    base["E"] = _bucket_onehot(T)
    nc = build_nc(T=T)
    in_maps = [dict(base, x=np.ascontiguousarray(x[i]))
               for i in range(x.shape[0])]
    res = run_bass_kernel_spmd(nc, in_maps, core_ids=list(range(len(in_maps))))
    out = np.stack([res.results[i]["out"] for i in range(len(in_maps))], axis=0)
    return out.astype(np.float32)


# version 29
# speedup vs baseline: 1.0030x; 1.0030x over previous
import math
import numpy as np
from contextlib import ExitStack

import concourse.bass as bass
import concourse.mybir as mybir
import concourse.bass_utils as _bass_utils
from concourse.bass_utils import run_bass_kernel_spmd

if not getattr(_bass_utils, "_ldw_opt_patched", False):
    _orig_run_command = _bass_utils.run_command

    def _run_command_ldw(cmd, *a, **kw):
        if isinstance(cmd, (list, tuple)):
            cmd = ["--enable-ldw-opt=true" if c == "--enable-ldw-opt=false"
                   else c for c in cmd]
        return _orig_run_command(cmd, *a, **kw)

    _bass_utils.run_command = _run_command_ldw
    _bass_utils._ldw_opt_patched = True

F32 = mybir.dt.float32
BF16 = mybir.dt.bfloat16
AF = mybir.ActivationFunctionType
ALU = mybir.AluOpType
AX = mybir.AxisListType

D = 1024
H = 16
HID = 4096
NEXP = 10
NTRU = 8
LN_EPS = 1e-5
RMS_EPS = 1e-6
KQ_EPS = 1e-6
NEG = -1e30
BIG = 1e9
QK_BOUND = 1.05


def _bucket_onehot(S):
    n = S + 128
    E = np.zeros((32, n), dtype=np.float32)
    i = np.arange(128, n)
    d = (i - 127).astype(np.int64)
    safe = np.maximum(d, 1).astype(np.float32)
    t = np.log(safe / np.float32(16.0))
    t = t / np.float32(math.log(128.0 / 16.0))
    t = t * np.float32(16.0)
    large = np.minimum(16 + t.astype(np.int32), 31)
    bucket = np.where(d < 16, d, large).astype(np.int64)
    E[bucket, i] = 1.0
    return E


class KB:

    ENGINES = ("sync", "gpsimd", "tensor", "vector", "scalar")

    def __init__(self, nc):
        self.nc = nc
        self.plan = {e: [] for e in self.ENGINES}
        self.sems = {}
        self.cnt = {}
        self.waited = {e: {} for e in self.ENGINES}
        self.bufs = {}
        self.pending = {e: [] for e in self.ENGINES}
        self.eng_sem = {"gpsimd": "G", "tensor": "T", "vector": "V",
                        "scalar": "S", "sync": None}
        self.fence_waits = {}
        self.stack = ExitStack()

    def sem(self, name):
        if name not in self.sems:
            self.sems[name] = self.stack.enter_context(self.nc.semaphore(name))
            self.cnt[name] = 0
        return self.sems[name]

    def op(self, eng, fn, reads=(), writes=(), inc=True, dma_sem=None,
           extra_waits=()):
        waits = list(extra_waits)

        def need(w):
            if w is None:
                return
            if w[0] == "PEND":
                if w[1] != eng:
                    raise RuntimeError(f"cross-engine access to pending buf {w}")
                return
            waits.append(w)

        for b in reads:
            ent = self.bufs.get(b)
            if ent:
                need(ent.get("w"))
        for b in writes:
            ent = self.bufs.get(b)
            if ent:
                need(ent.get("w"))
                for r in ent.get("r", ()):
                    need(r)
                for peng in ent.get("pr", ()):
                    if peng != eng:
                        raise RuntimeError(
                            f"write to {b} with pending readers on {peng}")
        ww = {}
        for s, v in list(waits) + list(self.fence_waits.items()):
            ww[s] = max(ww.get(s, 0), v)
        w_em = []
        for s, v in ww.items():
            if self.waited[eng].get(s, -1) >= v:
                continue
            self.waited[eng][s] = v
            self.sem(s)
            w_em.append((s, v))

        if dma_sem is not None:
            self.sem(dma_sem)
            self.cnt[dma_sem] += 16
            eff = (dma_sem, self.cnt[dma_sem])
        elif inc and self.eng_sem[eng]:
            s = self.eng_sem[eng]
            self.sem(s)
            self.cnt[s] += 1
            eff = (s, self.cnt[s])
        else:
            eff = ("PEND", eng)

        if eff[0] != "PEND":
            for key, kind in self.pending[eng]:
                ent = self.bufs.setdefault(key, {"w": None, "r": [], "pr": set()})
                if kind == "w":
                    ent["w"] = eff
                    ent["r"] = []
                else:
                    ent["r"].append(eff)
                ent["pr"].discard(eng)
            self.pending[eng] = []

        for b in reads:
            ent = self.bufs.setdefault(b, {"w": None, "r": [], "pr": set()})
            if eff[0] == "PEND":
                self.pending[eng].append((b, "r"))
                ent["pr"].add(eng)
            else:
                ent["r"].append(eff)
        for b in writes:
            ent = self.bufs.setdefault(b, {"w": None, "r": [], "pr": set()})
            if eff[0] == "PEND":
                ent["w"] = ("PEND", eng)
                self.pending[eng].append((b, "w"))
                ent["pr"].add(eng)
            else:
                ent["w"] = eff
                ent["r"] = []
        self.plan[eng].append((w_em, fn, None if eff[0] == "PEND" else eff,
                               dma_sem is not None))

    def fence(self):
        for eng in self.ENGINES:
            if self.pending[eng]:
                raise RuntimeError(f"fence with pending ops on {eng}")
        self.fence_waits = dict(self.cnt)

    def finish_queue(self, sem, keys):
        for k in keys:
            ent = self.bufs.setdefault(k, {"w": None, "r": [], "pr": set()})
            ent["w"] = (sem, self.cnt[sem])
            ent["r"] = []

    def wait(self, eng, sem, val):
        self.sem(sem)
        self.plan[eng].append(([(sem, val)], None, None, False))

    def emit(self):
        kb = self
        with kb.nc.Block() as block:
            for eng in self.ENGINES:
                steps = self.plan[eng]
                if not steps:
                    continue

                def body(e, steps=steps):
                    for waits, fn, eff, is_dma in steps:
                        for s, v in waits:
                            e.wait_ge(kb.sems[s], v)
                        if fn is None:
                            continue
                        inst = fn(e)
                        if eff is not None:
                            inst.then_inc(kb.sems[eff[0]], 16 if is_dma else 1)
                getattr(block, eng)(body)


def build_nc(T=2048):
    nc = bass.Bass("TRN2", target_bir_lowering=False)
    NTT = T // 128
    NP = T // 512
    CP = T + 128

    x_d = nc.dram_tensor("x", [T, D], F32, kind="ExternalInput")
    w_d = {}
    for nm, shp in [("ln1", [D]), ("Wq", [D, D]), ("Wk", [D, D]), ("Wv", [D, D]),
                    ("relb", [32, H]), ("scale", [H]), ("Wo", [D, D]),
                    ("ln2", [D]), ("W1", [D, HID]), ("W2", [HID, D]),
                    ("gw", [NEXP, D]), ("gb", [NEXP]),
                    ("ew", [NTRU, D, D]), ("eb", [NTRU, D]),
                    ("E", [32, CP])]:
        w_d[nm] = nc.dram_tensor(nm, shp, F32, kind="ExternalInput")
    out_d = nc.dram_tensor("out", [T, D], F32, kind="ExternalOutput")
    scr_d = nc.dram_tensor("cscr", [H, CP], F32, kind="Internal")
    acc_d = nc.dram_tensor("accscr", [8 * 128, T], F32, kind="Internal")

    kb = KB(nc)
    A = bass.AP
    ES = kb.stack

    def sb(name, shape, dt, st=None):
        return (st or ES).enter_context(nc.sbuf_tensor(name, list(shape), dt))

    pb = [None] * 8
    for i in (0, 1, 2, 3):
        pb[i] = ES.enter_context(nc.psum_tensor(f"pb{i}", [128, 512], F32))
    st_ps_a = ExitStack()
    for i in (4, 5, 6, 7):
        pb[i] = st_ps_a.enter_context(nc.psum_tensor(f"pb{i}a", [128, 512], F32))

    iden_f = sb("iden_f", (128, 128), F32)
    Jmat = sb("Jmat", (128, 128), BF16)
    onesb = sb("onesb", (128, 2), BF16)
    onesf = sb("onesf", (1, 64), F32)
    onesbr = sb("onesbr", (1, 64), BF16)
    iota_f = sb("iota_f", (128, 128), F32)
    iota_p = sb("iota_p", (128, 1), F32)
    iota10 = sb("iota10", (128, 12), F32)
    iotaB = sb("iotaB", (128, 12), F32)
    epsc = sb("epsc", (128, 4), F32)
    selq = sb("selq", (128, 2), BF16)
    sel2T = sb("sel2T", (2, 128), BF16)
    relbS = sb("relbS", (32, H), F32)
    relbT = sb("relbT", (H, 32), F32)
    scale16 = sb("scale16", (H, 2), F32)
    scaleP = sb("scaleP", (2, 8), F32)
    gmaxc = sb("gmaxc", (H, 2), F32)
    gbc = sb("gbc", (NEXP, 1), F32)
    ln1b = sb("ln1b", (1, D), BF16)
    ln2b = sb("ln2b", (1, D), BF16)
    ebCol = sb("ebCol", (128, NTRU * 8), F32)
    g1 = sb("g1", (128, 12), F32)
    g2 = sb("g2", (128, 12), F32)
    g3 = sb("g3", (128, 12), F32)
    gs = sb("gs", (128, 16), F32)

    XN16 = [sb(f"xn16_{c}", (128, T), BF16) for c in range(8)]

    MISC = "qM"

    def ld(dst, src, cast=False, sem=MISC, reads=(), writes=(), silent=False,
           slow=False):
        eng = "gpsimd" if cast else "sync"
        ew = () if silent else (((sem, kb.cnt.get(sem, 0)),)
                                if kb.cnt.get(sem, 0) else ())
        kb.op(eng, lambda e, d=dst, s=src, sl=slow:
              e.dma_start(d, s, allow_slow_non_contiguous=sl),
              reads=() if silent else reads, writes=() if silent else writes,
              dma_sem=sem, extra_waits=ew)

    ld(relbS[:, :], w_d["relb"][:, :], writes=("relbS",))
    ld(relbT[:, :], A(w_d["relb"], 0, [[1, H], [H, 32]]), writes=("relbT",), slow=True)
    ld(scale16[:, 0:1], A(w_d["scale"], 0, [[1, H], [1, 1]]), writes=("scale16",))
    ld(scaleP[:, :], A(w_d["scale"], 0, [[1, 2], [2, 8]]), writes=("scaleP",), slow=True)
    ld(gbc[:, :], A(w_d["gb"], 0, [[1, NEXP], [1, 1]]), writes=("gbc",))
    ld(ln1b[:, :], A(w_d["ln1"], 0, [[D, 1], [1, D]]), cast=True, sem="qC",
       writes=("ln1b",))
    ld(ln2b[:, :], A(w_d["ln2"], 0, [[D, 1], [1, D]]), cast=True, sem="qC",
       writes=("ln2b",))
    for e_ in range(NTRU):
        ld(ebCol[:, e_ * 8:(e_ + 1) * 8], A(w_d["eb"], e_ * D, [[1, 128], [128, 8]]),
           writes=("ebCol",), slow=True)

    G, V, S, TE = "gpsimd", "vector", "scalar", "tensor"
    kb.op(G, lambda e: e.iota(iota_f[:, :], [[1, 128]], channel_multiplier=0,
                              allow_small_or_imprecise_dtypes=True),
          writes=("iota_f",))
    kb.op(G, lambda e: e.iota(iota_p[:, :], [[0, 1]], channel_multiplier=1,
                              allow_small_or_imprecise_dtypes=True),
          writes=("iota_p",))
    kb.op(G, lambda e: e.iota(iota10[:, 0:NEXP], [[1, NEXP]], channel_multiplier=0,
                              allow_small_or_imprecise_dtypes=True),
          writes=("iota10",))
    kb.op(V, lambda e: e.tensor_scalar(iotaB[:, 0:NEXP], iota10[:, 0:NEXP],
                                       1.0e4, None, ALU.add),
          reads=("iota10",), writes=("iotaB",))
    kb.op(G, lambda e: e.memset(onesb[:, :], 1.0), writes=("onesb",))
    kb.op(G, lambda e: e.memset(onesf[:, :], 1.0), writes=("onesf",))
    kb.op(G, lambda e: e.memset(onesbr[:, :], 1.0), writes=("onesbr",))
    kb.op(G, lambda e: e.memset(epsc[:, 0:1], LN_EPS), writes=("epsc",))
    kb.op(G, lambda e: e.memset(epsc[:, 1:2], RMS_EPS), writes=("epsc",))
    kb.op(G, lambda e: e.memset(epsc[:, 2:3], KQ_EPS), writes=("epsc",))
    kb.op(G, lambda e: e.memset(epsc[:, 3:4], 1e-38), writes=("epsc",))
    kb.op(V, lambda e: e.tensor_scalar(iden_f[:, :], iota_f[:, :], iota_p[:, :],
                                       None, ALU.is_equal),
          reads=("iota_f", "iota_p"), writes=("iden_f",))
    kb.op(V, lambda e: e.tensor_scalar(Jmat[:, :], iota_f[:, :], iota_p[:, :],
                                       127.0, ALU.add, ALU.is_equal),
          reads=("iota_f", "iota_p"), writes=("Jmat",))
    s2i = sb("s2i", (2, 128), F32)
    sel2F = sb("sel2F", (2, 128), F32)
    ip2 = sb("ip2", (2, 1), F32)
    kb.op(G, lambda e: e.iota(s2i[:, :], [[1, 2], [0, 64]], channel_multiplier=0,
                              allow_small_or_imprecise_dtypes=True),
          writes=("s2i",))
    kb.op(G, lambda e: e.iota(ip2[:, :], [[0, 1]], channel_multiplier=1,
                              allow_small_or_imprecise_dtypes=True),
          writes=("ip2",))
    kb.op(V, lambda e: e.tensor_scalar(sel2T[:, :], s2i[:, :], ip2[:, :], None,
                                       ALU.is_equal),
          reads=("s2i", "ip2"), writes=("sel2T",))
    kb.op(V, lambda e: e.tensor_scalar(sel2F[:, :], s2i[:, :], ip2[:, :], None,
                                       ALU.is_equal),
          reads=("s2i", "ip2"), writes=("sel2F",))
    kb.op(TE, lambda e: e.transpose(pb[0][:, 0:2], sel2F[:, :],
                                    iden_f[0:2, 0:2]),
          reads=("sel2F", "iden_f"), writes=("pb0",))
    kb.op(V, lambda e: e.tensor_copy(selq[:, :], pb[0][:, 0:2]),
          reads=("pb0",), writes=("selq",))

    st_cw = ExitStack()
    cwT = sb("cwT", (NTRU, T), BF16, st_cw)
    ohE = sb("ohE", (NTRU, NTRU * 128), BF16, st_cw)
    st_a = ExitStack()
    XNF = [sb(f"xnf{c}", (128, T), F32, st_a) for c in range(8)]
    st_a2 = ExitStack()
    EsS = sb("EsS", (32, CP), F32, st_a2)
    cpadS = sb("cpadS", (H, CP), F32, st_a2)
    xt = [sb(f"xt{i}", (128, D), F32, st_a2) for i in range(2)]
    xnt = [sb(f"xnt{i}", (128, D), F32, st_a2) for i in range(2)]
    sqt = [sb(f"sqt{i}", (128, D), F32, st_a2) for i in range(2)]
    st_s = sb("st_s", (128, 16), F32, st_a2)

    ld(EsS[:, :], w_d["E"][:, :], writes=("EsS",))
    kb.finish_queue(MISC, ["relbS", "relbT", "scale16", "scaleP", "gbc",
                           "ebCol", "EsS"])
    kb.finish_queue("qC", ["ln1b", "ln2b"])
    kb.op(S, lambda e: e.activation(scale16[:, 1:2], scale16[:, 0:1], AF.Abs),
          reads=("scale16",), writes=("scale16",))
    kb.op(V, lambda e: e.reduce_max(gmaxc[:, 0:1], relbT[:, :], axis=AX.X,
                                    apply_absolute_value=True),
          reads=("relbT",), writes=("gmaxc",))
    kb.op(V, lambda e: e.tensor_scalar(gmaxc[:, 0:1], gmaxc[:, 0:1], QK_BOUND,
                                       None, ALU.add),
          reads=("gmaxc",), writes=("gmaxc",))
    kb.op(V, lambda e: e.tensor_tensor(gmaxc[:, 1:2], gmaxc[:, 0:1],
                                       scale16[:, 1:2], ALU.mult),
          reads=("gmaxc", "scale16"), writes=("gmaxc",))
    ofs = 0
    while ofs < CP:
        w = min(512, CP - ofs)
        bk = (ofs // 512) % 2
        kb.op(TE, lambda e, o=ofs, w=w, bk=bk:
              e.matmul(pb[bk][0:H, 0:w], relbS[:, :], EsS[:, o:o + w],
                       start=True, stop=True),
              reads=("relbS", "EsS"), writes=(f"pb{bk}",))
        kb.op(V, lambda e, o=ofs, w=w, bk=bk:
              e.tensor_scalar(cpadS[:, o:o + w], pb[bk][0:H, 0:w],
                              scale16[:, 0:1], gmaxc[:, 1:2],
                              ALU.mult, ALU.subtract),
              reads=(f"pb{bk}", "scale16", "gmaxc"), writes=("cpadS",))
        ofs += w
    kb.op(V, lambda e: e.memset(cpadS[:, 0:128], NEG), writes=("cpadS",))
    kb.op(S, lambda e: e.activation(cpadS[:, :], cpadS[:, :], AF.Exp),
          reads=("cpadS",), writes=("cpadS",))
    ld(scr_d[:, :], cpadS[:, :], sem="qS", reads=("cpadS",), writes=("cscr",))

    for tt in range(NTT):
        p = tt % 2
        o = p * 8
        sk = f"st_s{p}"
        ld(xt[p][:, :], x_d[tt * 128:(tt + 1) * 128, :], sem=f"qX{p}",
           writes=(f"xt{p}",))
        kb.op(V, lambda e, p=p, o=o:
              e.reduce_sum(st_s[:, o:o + 1], xt[p][:, :], axis=AX.X),
              reads=(f"xt{p}",), writes=(sk,))
        kb.op(V, lambda e, p=p: e.tensor_tensor(sqt[p][:, :], xt[p][:, :],
                                                xt[p][:, :], ALU.mult),
              reads=(f"xt{p}",), writes=(f"sqt{p}",))
        kb.op(V, lambda e, p=p, o=o:
              e.reduce_sum(st_s[:, o + 1:o + 2], sqt[p][:, :], axis=AX.X),
              reads=(f"sqt{p}",), writes=(sk,))
        kb.op(V, lambda e, o=o:
              e.tensor_scalar(st_s[:, o + 2:o + 3], st_s[:, o:o + 1], 1.0 / D,
                              None, ALU.mult),
              reads=(sk,), writes=(sk,))
        kb.op(V, lambda e, o=o:
              e.tensor_tensor(st_s[:, o + 3:o + 4], st_s[:, o + 2:o + 3],
                              st_s[:, o + 2:o + 3], ALU.mult),
              reads=(sk,), writes=(sk,))
        kb.op(V, lambda e, o=o:
              e.tensor_scalar(st_s[:, o + 4:o + 5], st_s[:, o + 1:o + 2],
                              1.0 / D, st_s[:, o + 3:o + 4],
                              ALU.mult, ALU.subtract),
              reads=(sk,), writes=(sk,))
        kb.op(S, lambda e, o=o:
              e.activation(st_s[:, o + 5:o + 6], st_s[:, o + 4:o + 5], AF.Ln,
                           bias=epsc[:, 0:1]),
              reads=(sk, "epsc"), writes=(sk,))
        kb.op(S, lambda e, o=o:
              e.activation(st_s[:, o + 6:o + 7], st_s[:, o + 5:o + 6], AF.Exp,
                           scale=-0.5),
              reads=(sk,), writes=(sk,))
        kb.op(V, lambda e, p=p, o=o:
              e.tensor_scalar(xnt[p][:, :], xt[p][:, :],
                              st_s[:, o + 2:o + 3], st_s[:, o + 6:o + 7],
                              ALU.subtract, ALU.mult),
              reads=(f"xt{p}", sk), writes=(f"xnt{p}",))
        for c in range(8):
            bk = 6 + (c % 2)
            sl = ((c // 2) % 2) * 128
            kb.op(TE, lambda e, p=p, c=c, bk=bk, sl=sl:
                  e.transpose(pb[bk][:, sl:sl + 128],
                              xnt[p][:, c * 128:(c + 1) * 128], iden_f[:, :]),
                  reads=(f"xnt{p}", "iden_f"), writes=(f"pb{bk}",))
            kb.op(S, lambda e, tt=tt, c=c, bk=bk, sl=sl:
                  e.copy(XNF[c][:, tt * 128:(tt + 1) * 128],
                         pb[bk][:, sl:sl + 128]),
                  reads=(f"pb{bk}",), writes=(f"xnf{c}",))
    for c in range(8):
        kb.op(G, lambda e, c=c: e.tensor_copy(XN16[c][:, :], XNF[c][:, :]),
              reads=(f"xnf{c}",), writes=(f"xn16_{c}",))
    st_a2.close()
    kb.fence()

    st_b1 = ExitStack()
    gwS = sb("gwS", (NEXP, D), F32, st_b1)
    gwT = sb("gwT", (128, 80), F32, st_b1)
    lgS = sb("lgS", (NEXP, T), F32, st_b1)
    lgT = sb("lgT", (128, NTT * 16), F32, st_b1)
    cwTok = sb("cwTok", (128, NTT * 8), F32, st_b1)

    ld(gwS[:, :], w_d["gw"][:, :], writes=("gwS",))
    kb.finish_queue(MISC, ["gwS"])
    oh_i = sb("oh_i", (NTRU, NTRU * 128), F32, st_b1)
    ip8 = sb("ip8", (NTRU, 1), F32, st_b1)
    kb.op(G, lambda e: e.iota(oh_i[:, :], [[1, NTRU], [0, 128]],
                              channel_multiplier=0,
                              allow_small_or_imprecise_dtypes=True),
          writes=("oh_i",))
    kb.op(G, lambda e: e.iota(ip8[:, :], [[0, 1]], channel_multiplier=1,
                              allow_small_or_imprecise_dtypes=True),
          writes=("ip8",))
    kb.op(V, lambda e: e.tensor_scalar(ohE[:, :], oh_i[:, :], ip8[:, :], None,
                                       ALU.is_equal),
          reads=("oh_i", "ip8"), writes=("ohE",))
    for c in range(8):
        bk = c % 2
        kb.op(TE, lambda e, c=c, bk=bk:
              e.transpose(pb[bk][:, 0:NEXP], gwS[:, c * 128:(c + 1) * 128],
                          iden_f[0:NEXP, 0:NEXP]),
              reads=("gwS", "iden_f"), writes=(f"pb{bk}",))
        kb.op(V, lambda e, c=c, bk=bk:
              e.tensor_copy(gwT[:, c * 10:(c + 1) * 10], pb[bk][:, 0:NEXP]),
              reads=(f"pb{bk}",), writes=("gwT",))
    for pp in range(NP):
        for c in range(8):
            kb.op(TE, lambda e, pp=pp, c=c:
                  e.matmul(pb[4 + pp % 4][0:NEXP, 0:512],
                           gwT[:, c * 10:(c + 1) * 10],
                           XNF[c][:, pp * 512:(pp + 1) * 512],
                           start=(c == 0), stop=(c == 7)),
                  reads=("gwT", f"xnf{c}"), writes=(f"pb{4 + pp % 4}",),
                  inc=(c == 7))
        kb.op(V, lambda e, pp=pp:
              e.tensor_scalar(lgS[:, pp * 512:(pp + 1) * 512],
                              pb[4 + pp % 4][0:NEXP, 0:512], gbc[:, 0:1],
                              None, ALU.add),
              reads=(f"pb{4 + pp % 4}", "gbc"), writes=("lgS",))
    for tt in range(NTT):
        bk = tt % 2
        kb.op(TE, lambda e, tt=tt, bk=bk:
              e.transpose(pb[bk][:, 0:NEXP], lgS[:, tt * 128:(tt + 1) * 128],
                          iden_f[0:NEXP, 0:NEXP]),
              reads=("lgS", "iden_f"), writes=(f"pb{bk}",))
        kb.op(V, lambda e, tt=tt, bk=bk:
              e.tensor_copy(lgT[:, tt * 16:tt * 16 + NEXP], pb[bk][:, 0:NEXP]),
              reads=(f"pb{bk}",), writes=("lgT",))
    gcol = sb("gcol", (128, 16 * 8), F32, st_b1)
    gw1 = sb("gw1", (128, NTT * 16), F32, st_b1)
    gw2 = sb("gw2", (128, NTT * 16), F32, st_b1)
    gw3 = sb("gw3", (128, NTT * 16), F32, st_b1)

    def d3(h, width=16, n=10):
        return A(h, 0, [[NTT * width, 128], [width, NTT], [1, n]])

    def b3(slot):
        return A(gcol, slot * NTT, [[16 * 8, 128], [1, NTT], [0, 10]])

    def gcolf(slot):
        return gcol[:, slot * NTT:(slot + 1) * NTT]

    iota3 = A(iota10, 0, [[12, 128], [0, NTT], [1, 10]])
    iotaB3 = A(iotaB, 0, [[12, 128], [0, NTT], [1, 10]])

    def vg(fn, reads, writes):
        kb.op(V, fn, reads=reads, writes=writes)

    vg(lambda e: e.tensor_reduce(gcolf(0), d3(lgT), AX.X, ALU.max),
       ("lgT",), ("gcol",))
    vg(lambda e: e.tensor_tensor(d3(gw1), d3(lgT), b3(0), ALU.subtract),
       ("lgT", "gcol"), ("gw1",))
    kb.op(S, lambda e: e.activation(d3(gw1), d3(gw1), AF.Exp),
          reads=("gw1",), writes=("gw1",))
    vg(lambda e: e.tensor_reduce(gcolf(1), d3(gw1), AX.X, ALU.add),
       ("gw1",), ("gcol",))
    vg(lambda e: e.reciprocal(gcolf(1), gcolf(1)), ("gcol",), ("gcol",))
    vg(lambda e: e.tensor_tensor(d3(gw1), d3(gw1), b3(1), ALU.mult),
       ("gw1", "gcol"), ("gw1",))
    vg(lambda e: e.tensor_reduce(gcolf(2), d3(gw1), AX.X, ALU.max),
       ("gw1",), ("gcol",))
    vg(lambda e: e.tensor_tensor(d3(gw2), d3(gw1), b3(2), ALU.is_equal),
       ("gw1", "gcol"), ("gw2",))
    vg(lambda e: e.scalar_tensor_tensor(d3(gw3), d3(gw2), -1.0e4, iotaB3,
                                        ALU.mult, ALU.add),
       ("gw2", "iotaB"), ("gw3",))
    vg(lambda e: e.tensor_reduce(gcolf(3), d3(gw3), AX.X, ALU.min),
       ("gw3",), ("gcol",))
    vg(lambda e: e.tensor_tensor(d3(gw2), iota3, b3(3), ALU.is_equal),
       ("iota10", "gcol"), ("gw2",))
    vg(lambda e: e.scalar_tensor_tensor(d3(gw3), d3(gw2), -BIG, d3(gw1),
                                        ALU.mult, ALU.add),
       ("gw2", "gw1"), ("gw3",))
    vg(lambda e: e.tensor_reduce(gcolf(4), d3(gw3), AX.X, ALU.max),
       ("gw3",), ("gcol",))
    vg(lambda e: e.tensor_tensor(d3(gw1), d3(gw3), b3(4), ALU.is_equal),
       ("gw3", "gcol"), ("gw1",))
    vg(lambda e: e.scalar_tensor_tensor(d3(gw3), d3(gw1), -1.0e4, iotaB3,
                                        ALU.mult, ALU.add),
       ("gw1", "iotaB"), ("gw3",))
    vg(lambda e: e.tensor_reduce(gcolf(5), d3(gw3), AX.X, ALU.min),
       ("gw3",), ("gcol",))
    vg(lambda e: e.tensor_tensor(d3(gw3), iota3, b3(5), ALU.is_equal),
       ("iota10", "gcol"), ("gw3",))
    vg(lambda e: e.tensor_scalar(gcolf(6), gcolf(3), 7.5, None, ALU.is_lt),
       ("gcol",), ("gcol",))
    vg(lambda e: e.tensor_tensor(gcolf(6), gcolf(6), gcolf(2), ALU.mult),
       ("gcol",), ("gcol",))
    vg(lambda e: e.tensor_scalar(gcolf(7), gcolf(5), 7.5, None, ALU.is_lt),
       ("gcol",), ("gcol",))
    vg(lambda e: e.tensor_tensor(gcolf(7), gcolf(7), gcolf(4), ALU.mult),
       ("gcol",), ("gcol",))
    vg(lambda e: e.tensor_tensor(gcolf(2), gcolf(6), gcolf(7), ALU.add),
       ("gcol",), ("gcol",))
    vg(lambda e: e.tensor_scalar(gcolf(3), gcolf(2), 0.0, None, ALU.is_le),
       ("gcol",), ("gcol",))
    vg(lambda e: e.tensor_tensor(gcolf(2), gcolf(2), gcolf(3), ALU.add),
       ("gcol",), ("gcol",))
    vg(lambda e: e.reciprocal(gcolf(2), gcolf(2)), ("gcol",), ("gcol",))
    vg(lambda e: e.tensor_tensor(gcolf(6), gcolf(6), gcolf(2), ALU.mult),
       ("gcol",), ("gcol",))
    vg(lambda e: e.tensor_tensor(gcolf(7), gcolf(7), gcolf(2), ALU.mult),
       ("gcol",), ("gcol",))
    vg(lambda e: e.tensor_tensor(d3(gw2, n=8), d3(gw2, n=8),
                                 A(gcol, 6 * NTT, [[16 * 8, 128], [1, NTT], [0, 8]]),
                                 ALU.mult),
       ("gw2", "gcol"), ("gw2",))
    vg(lambda e: e.tensor_tensor(d3(gw3, n=8), d3(gw3, n=8),
                                 A(gcol, 7 * NTT, [[16 * 8, 128], [1, NTT], [0, 8]]),
                                 ALU.mult),
       ("gw3", "gcol"), ("gw3",))
    vg(lambda e: e.tensor_tensor(A(cwTok, 0, [[NTT * 8, 128], [8, NTT], [1, 8]]),
                                 d3(gw2, n=8), d3(gw3, n=8), ALU.add),
       ("gw2", "gw3"), ("cwTok",))
    for tt in range(NTT):
        bk = tt % 2
        kb.op(TE, lambda e, tt=tt, bk=bk:
              e.transpose(pb[bk][0:8, 0:128], cwTok[:, tt * 8:(tt + 1) * 8],
                          iden_f[:, :]),
              reads=("cwTok", "iden_f"), writes=(f"pb{bk}",))
        kb.op(V, lambda e, tt=tt, bk=bk:
              e.tensor_copy(cwT[:, tt * 128:(tt + 1) * 128], pb[bk][0:8, 0:128]),
              reads=(f"pb{bk}",), writes=("cwT",))
    st_b1.close()
    st_a.close()
    kb.fence()

    st_b2 = ExitStack()
    ACC = [sb(f"acc{c}", (128, T), F32, st_b2) for c in range(8)]
    cwbA = [sb(f"cwb{e}", (128, T), BF16, st_b2) for e in range(NTRU)]
    mtmp = [sb(f"mtmp{i}", (128, T), F32, st_b2) for i in range(2)]
    We16 = [sb(f"we16_{i}", (128, 4096), BF16, st_b2) for i in range(2)]

    for e_ in range(NTRU):
        for pp in range(NP):
            bk = pp % 2
            kb.op(TE, lambda e, e_=e_, pp=pp, bk=bk:
                  e.matmul(pb[bk][:, 0:512], ohE[:, e_ * 128:(e_ + 1) * 128],
                           cwT[:, pp * 512:(pp + 1) * 512],
                           start=True, stop=True),
                  reads=("ohE", "cwT"), writes=(f"pb{bk}",))
            kb.op(V, lambda e, e_=e_, pp=pp, bk=bk:
                  e.tensor_copy(cwbA[e_][:, pp * 512:(pp + 1) * 512],
                                pb[bk][:, 0:512]),
                  reads=(f"pb{bk}",), writes=(f"cwb{e_}",))

    def ew_load(e_, hlf, par):
        for k in range(8):
            ld(We16[par][:, k * 512:(k + 1) * 512],
               w_d["ew"][e_, k * 128:(k + 1) * 128, hlf * 512:(hlf + 1) * 512],
               cast=True, sem=f"qW{par}", writes=(f"we16_{par}",),
               silent=(k > 0))
        kb.finish_queue(f"qW{par}", [f"we16_{par}"])

    ew_load(0, 0, 0)
    for e_ in range(NTRU):
        for hlf in range(2):
            par = hlf
            nxt_e, nxt_h = (e_, 1) if hlf == 0 else (e_ + 1, 0)
            if nxt_e < NTRU:
                ew_load(nxt_e, nxt_h, 1 - hlf)
            for dj in range(4):
                dout = hlf * 4 + dj
                col = dj * 128
                pbase = (dout % 2) * 4
                for k in range(8):
                    for pp in range(NP):
                        bk = pbase + pp
                        kb.op(TE, lambda e, par=par, k=k, col=col, pp=pp, bk=bk:
                              e.matmul(pb[bk][:, 0:512],
                                       We16[par][:, k * 512 + col:
                                                 k * 512 + col + 128],
                                       XN16[k][:, pp * 512:(pp + 1) * 512],
                                       start=(k == 0), stop=(k == 7)),
                              reads=(f"we16_{par}", f"xn16_{k}"),
                              writes=(f"pb{bk}",), inc=(k == 7))
                mp = dout % 2
                for pp in range(NP):
                    bk = pbase + pp
                    if e_ == 0:
                        kb.op(V, lambda e, e_=e_, dout=dout, pp=pp, bk=bk:
                              e.scalar_tensor_tensor(
                                  ACC[dout][:, pp * 512:(pp + 1) * 512],
                                  pb[bk][:, 0:512],
                                  ebCol[:, e_ * 8 + dout:e_ * 8 + dout + 1],
                                  cwbA[e_][:, pp * 512:(pp + 1) * 512],
                                  ALU.add, ALU.mult),
                              reads=(f"pb{bk}", "ebCol", f"cwb{e_}"),
                              writes=(f"acc{dout}",))
                    else:
                        kb.op(V, lambda e, e_=e_, dout=dout, pp=pp, bk=bk,
                              mp=mp:
                              e.scalar_tensor_tensor(
                                  mtmp[mp][:, pp * 512:(pp + 1) * 512],
                                  pb[bk][:, 0:512],
                                  ebCol[:, e_ * 8 + dout:e_ * 8 + dout + 1],
                                  cwbA[e_][:, pp * 512:(pp + 1) * 512],
                                  ALU.add, ALU.mult),
                              reads=(f"pb{bk}", "ebCol", f"cwb{e_}"),
                              writes=(f"mtmp{mp}",))
                if e_ > 0:
                    kb.op(G, lambda e, dout=dout, mp=mp:
                          e.tensor_tensor(ACC[dout][:, :], ACC[dout][:, :],
                                          mtmp[mp][:, :], ALU.add),
                          reads=(f"acc{dout}", f"mtmp{mp}"),
                          writes=(f"acc{dout}",))
    for c in range(8):
        ld(acc_d[c * 128:(c + 1) * 128, :], ACC[c][:, :], sem="qA0",
           reads=(f"acc{c}",), writes=(f"accp{c}",))
    st_b2.close()
    st_cw.close()
    kb.fence()

    st_c_k = ExitStack()
    K16 = [sb(f"k16_{c}", (128, T), BF16, st_c_k) for c in range(8)]
    st_c_bc = ExitStack()
    Y16p = [sb(f"y16p_{c}", (128, T), BF16, st_c_bc) for c in range(8)]
    Q16 = [sb(f"q16_{c}", (128, T), BF16, st_c_bc) for c in range(8)]
    st_c_vt = ExitStack()
    VtL = [sb(f"vtl{t}", (128, H * 65), BF16, st_c_vt) for t in range(NTT)]
    st_c0 = ExitStack()
    sqP = [sb(f"sq{i}", (128, T), BF16, st_c0) for i in range(2)]
    Vf32 = sb("vf32", (128, T), F32, st_c0)
    rowscrP = sb("rowscrP", (2, 512), F32, st_c0)
    rowb16 = [sb(f"rowb16_{i}", (2, T), BF16, st_c0) for i in range(2)]
    Wst = [sb(f"wst{i}", (128, 1024), BF16, st_c0) for i in range(2)]

    XN2 = Y16p

    def wload(par, dram, r0, c0, c1, key, sem=None):
        for k in range(8):
            ld(Wst[par][:, k * (c1 - c0):(k + 1) * (c1 - c0)],
               dram[r0 + k * 128:r0 + (k + 1) * 128, c0:c1], cast=True,
               sem=sem or f"qW{par}", writes=(key,), silent=(k > 0))
        kb.finish_queue(sem or f"qW{par}", [key])

    wload(0, w_d["Wq"], 0, 0, 128, "wst0")
    wload(1, w_d["Wq"], 0, 128, 256, "wst1")

    for c in range(8):
        kb.op(V, lambda e, c=c: e.tensor_tensor(sqP[c % 2][:, :], XN16[c][:, :],
                                                XN16[c][:, :], ALU.mult),
              reads=(f"xn16_{c}",), writes=(f"sq{c % 2}",))
        for pp in range(NP):
            bk = 4 + pp
            kb.op(TE, lambda e, c=c, pp=pp, bk=bk:
                  e.matmul(pb[bk][0:1, 0:512], onesb[:, 0:1],
                           sqP[c % 2][:, pp * 512:(pp + 1) * 512],
                           start=(c == 0), stop=(c == 7)),
                  reads=(f"sq{c % 2}", "onesb"), writes=(f"pb{bk}",), inc=True)
    for pp in range(NP):
        kb.op(S, lambda e, pp=pp:
              e.activation(rowscrP[0:1, :], pb[4 + pp][0:1, 0:512], AF.Ln,
                           bias=epsc[0:1, 1:2], scale=1.0 / D),
              reads=(f"pb{4 + pp}", "epsc"), writes=("rowscrP",))
        kb.op(S, lambda e, pp=pp:
              e.activation(rowb16[0][0:1, pp * 512:(pp + 1) * 512],
                           rowscrP[0:1, :], AF.Exp, scale=-0.5),
              reads=("rowscrP",), writes=("rowb16_0",))
    for c in range(8):
        for pp in range(NP):
            bk = 4 + pp % 2
            kb.op(TE, lambda e, c=c, pp=pp, bk=bk:
                  e.matmul(pb[bk][:, 0:512], ln1b[:, c * 128:(c + 1) * 128],
                           rowb16[0][0:1, pp * 512:(pp + 1) * 512],
                           start=True, stop=True),
                  reads=("ln1b", "rowb16_0"), writes=(f"pb{bk}",))
            kb.op(V, lambda e, c=c, pp=pp, bk=bk:
                  e.tensor_tensor(XN2[c][:, pp * 512:(pp + 1) * 512],
                                  XN16[c][:, pp * 512:(pp + 1) * 512],
                                  pb[bk][:, 0:512], ALU.mult),
                  reads=(f"xn16_{c}", f"pb{bk}"), writes=(f"y16p_{c}",))
        pbase0 = 0 if NP > 2 else 0
        for pp in range(NP):
            kb.op(TE, lambda e, c=c, pp=pp:
                  e.matmul(pb[pp][:, 0:512], Wst[0][:, c * 128:(c + 1) * 128],
                           XN2[c][:, pp * 512:(pp + 1) * 512],
                           start=(c == 0), stop=(c == 7)),
                  reads=("wst0", f"y16p_{c}"), writes=(f"pb{pp}",),
                  inc=(c == 7))
    for pp in range(NP):
        kb.op(V, lambda e, pp=pp:
              e.tensor_copy(Q16[0][:, pp * 512:(pp + 1) * 512],
                            pb[pp][:, 0:512]),
              reads=(f"pb{pp}",), writes=("q16_0",))

    def proj_unit(wname, dst, dkey, dout, nxt):
        def go():
            par = dout % 2
            if nxt is not None:
                wload(1 - par, w_d[nxt[0]], 0, nxt[1] * 128,
                      (nxt[1] + 1) * 128, f"wst{1 - par}")
            pbase = (dout % 2) * 2 if NP <= 2 else 0
            for k in range(8):
                for pp in range(NP):
                    bk = pbase + pp if NP <= 2 else pp
                    kb.op(TE, lambda e, par=par, k=k, pp=pp, bk=bk:
                          e.matmul(pb[bk][:, 0:512],
                                   Wst[par][:, k * 128:(k + 1) * 128],
                                   XN2[k][:, pp * 512:(pp + 1) * 512],
                                   start=(k == 0), stop=(k == 7)),
                          reads=(f"wst{par}", f"y16p_{k}"),
                          writes=(f"pb{bk}",), inc=(k == 7))
            for pp in range(NP):
                bk = (pbase + pp) if NP <= 2 else pp
                if dst is None:
                    kb.op(V, lambda e, pp=pp, bk=bk:
                          e.tensor_copy(Vf32[:, pp * 512:(pp + 1) * 512],
                                        pb[bk][:, 0:512]),
                          reads=(f"pb{bk}",), writes=("vf32",))
                else:
                    kb.op(V, lambda e, dout=dout, pp=pp, bk=bk:
                          e.tensor_copy(dst[dout][:, pp * 512:(pp + 1) * 512],
                                        pb[bk][:, 0:512]),
                          reads=(f"pb{bk}",), writes=(dkey.format(dout),))
        return go

    def qn_sq(XB, keyfmt, c):
        pr = c % 2
        kb.op(V, lambda e, c=c, pr=pr:
              e.tensor_tensor(sqP[pr][:, :], XB[c][:, :], XB[c][:, :],
                              ALU.mult),
              reads=(keyfmt.format(c),), writes=(f"sq{pr}",))

    def qn_stats(XB, keyfmt, c, fold_scale, bank0):
        pr = c % 2
        for pp in range(NP):
            bk = bank0 + pp % 2
            kb.op(TE, lambda e, pr=pr, pp=pp, bk=bk:
                  e.matmul(pb[bk][0:2, 0:512], selq[:, :],
                           sqP[pr][:, pp * 512:(pp + 1) * 512],
                           start=True, stop=True),
                  reads=(f"sq{pr}", "selq"), writes=(f"pb{bk}",))
            kb.op(S, lambda e, bk=bk:
                  e.activation(rowscrP[0:2, :], pb[bk][0:2, 0:512], AF.Ln,
                               bias=epsc[0:2, 2:3]),
                  reads=(f"pb{bk}", "epsc"), writes=("rowscrP",))
            kb.op(S, lambda e, pr=pr, pp=pp:
                  e.activation(rowb16[pr][0:2, pp * 512:(pp + 1) * 512],
                               rowscrP[0:2, :], AF.Exp, scale=-0.5),
                  reads=("rowscrP",), writes=(f"rowb16_{pr}",))
            if fold_scale:
                kb.op(V, lambda e, c=c, pr=pr, pp=pp:
                      e.tensor_scalar(rowb16[pr][0:2, pp * 512:(pp + 1) * 512],
                                      rowb16[pr][0:2, pp * 512:(pp + 1) * 512],
                                      scaleP[:, c:c + 1], None, ALU.mult),
                      reads=(f"rowb16_{pr}", "scaleP"),
                      writes=(f"rowb16_{pr}",))

    def qn_apply_piece(XB, keyfmt, c, pp, bk):
        pr = c % 2
        kb.op(TE, lambda e, pr=pr, pp=pp, bk=bk:
              e.matmul(pb[bk][:, 0:512], sel2T[:, :],
                       rowb16[pr][0:2, pp * 512:(pp + 1) * 512],
                       start=True, stop=True),
              reads=(f"rowb16_{pr}", "sel2T"), writes=(f"pb{bk}",))
        kb.op(V, lambda e, c=c, pp=pp, bk=bk:
              e.tensor_tensor(XB[c][:, pp * 512:(pp + 1) * 512],
                              XB[c][:, pp * 512:(pp + 1) * 512],
                              pb[bk][:, 0:512], ALU.mult),
              reads=(keyfmt.format(c), f"pb{bk}"), writes=(keyfmt.format(c),))

    for d in range(1, 8):
        nxt = ("Wq", d + 1) if d < 7 else ("Wk", 0)
        proj_unit("Wq", Q16, "q16_{}", d, nxt)()
    for d in range(8):
        nxt = ("Wk", d + 1) if d < 7 else ("Wv", 0)
        qn_sq(Q16, "q16_{}", d)
        proj_unit("Wk", K16, "k16_{}", d, nxt)()
        qn_stats(Q16, "q16_{}", d, True, 4)
        if d > 0:
            for pp in range(NP):
                qn_apply_piece(Q16, "q16_{}", d - 1, pp, 6 + pp % 2)
    for pp in range(NP):
        qn_apply_piece(Q16, "q16_{}", 7, pp, 6 + pp % 2)

    def v_transpose(c, tt):
        reg = (tt % 4) * 128
        kb.op(TE, lambda e, tt=tt, reg=reg:
              e.transpose(pb[4][:, reg:reg + 128],
                          Vf32[:, tt * 128:(tt + 1) * 128], iden_f[:, :]),
              reads=("vf32", "iden_f"), writes=("pb4",))
        for hf in range(2):
            hh = 2 * c + hf
            if hf == 0:
                kb.op(V, lambda e, tt=tt, reg=reg, hh=hh:
                      e.tensor_copy(VtL[tt][:, hh * 65:hh * 65 + 64],
                                    pb[4][:, reg:reg + 64]),
                      reads=("pb4",), writes=(f"vtl{tt}",))
            else:
                kb.op(S, lambda e, tt=tt, reg=reg, hh=hh:
                      e.copy(VtL[tt][:, hh * 65:hh * 65 + 64],
                             pb[4][:, reg + 64:reg + 128]),
                      reads=("pb4",), writes=(f"vtl{tt}",))

    for d in range(8):
        par = d % 2
        qn_sq(K16, "k16_{}", d)
        if d < 7:
            wload(1 - par, w_d["Wv"], 0, (d + 1) * 128, (d + 2) * 128,
                  f"wst{1 - par}")
        pbase = (d % 2) * 2 if NP <= 2 else 0
        for k in range(8):
            for pp in range(NP):
                bk = pbase + pp if NP <= 2 else pp
                kb.op(TE, lambda e, par=par, k=k, pp=pp, bk=bk:
                      e.matmul(pb[bk][:, 0:512],
                               Wst[par][:, k * 128:(k + 1) * 128],
                               XN2[k][:, pp * 512:(pp + 1) * 512],
                               start=(k == 0), stop=(k == 7)),
                      reads=(f"wst{par}", f"y16p_{k}"),
                      writes=(f"pb{bk}",), inc=(k == 7))
            if d > 0 and 2 * k < NTT:
                v_transpose(d - 1, 2 * k)
                v_transpose(d - 1, 2 * k + 1)
            if d > 0 and k % 2 == 1 and (k - 1) // 2 < NP:
                qn_apply_piece(K16, "k16_{}", d - 1, (k - 1) // 2, 7)
        for pp in range(NP):
            bk = pbase + pp if NP <= 2 else pp
            kb.op(V, lambda e, pp=pp, bk=bk:
                  e.tensor_copy(Vf32[:, pp * 512:(pp + 1) * 512],
                                pb[bk][:, 0:512]),
                  reads=(f"pb{bk}",), writes=("vf32",))
        qn_stats(K16, "k16_{}", d, False, 5)
    for tt in range(NTT):
        v_transpose(7, tt)
    for pp in range(NP):
        qn_apply_piece(K16, "k16_{}", 7, pp, 7)
    for tt in range(NTT):
        kb.op(V, lambda e, tt=tt:
              e.memset(A(VtL[tt], 64, [[H * 65, 128], [65, H]]), 1.0),
              writes=(f"vtl{tt}",))
    st_c0.close()
    kb.fence()

    st_ps_a.close()
    st_ps_c1 = ExitStack()
    s16 = [st_ps_c1.enter_context(nc.psum_tensor(f"s16_{i}", [128, 512], F32))
           for i in range(3)]
    pbc = [st_ps_c1.enter_context(nc.psum_tensor("pbc0", [128, 512], F32))]
    st_c1 = ExitStack()
    Tst = [sb(f"tst{i}", (128, T), BF16, st_c1) for i in range(2)]
    Pt = [sb(f"pt{i}", (128, T), BF16, st_c1) for i in range(4)]
    denr = sb("denr", (1, 512), F32, st_c1)
    denb = sb("denb", (1, 1024), BF16, st_c1)
    BC16 = sb("bc16", (64, 1024), BF16, st_c1)
    O16t = sb("o16t", (64, T), BF16, st_c1)

    ld(Tst[0][:, :], A(scr_d, 0, [[1, 128], [1, T]]), cast=True, sem="qT0",
       reads=("cscr",), writes=("tst0",))

    NB = min(4, NP)
    vsum = sb("vsum", (1, H * 65), F32, st_c1)
    vsumT = sb("vsumT", (65, H), F32, st_c1)
    for g0, g1 in ((0, 512), (512, 1024), (1024, H * 65)):
        for tt in range(NTT):
            kb.op(TE, lambda e, g0=g0, g1=g1, tt=tt:
                  e.matmul(pbc[0][0:1, 0:g1 - g0], onesb[:, 0:1],
                           VtL[tt][:, g0:g1], start=(tt == 0),
                           stop=(tt == NTT - 1)),
                  reads=(f"vtl{tt}", "onesb"), writes=("pbc0",),
                  inc=(tt == NTT - 1))
        kb.op(V, lambda e, g0=g0, g1=g1:
              e.tensor_copy(vsum[:, g0:g1], pbc[0][0:1, 0:g1 - g0]),
              reads=("pbc0",), writes=("vsum",))
    for h in range(H):
        kb.op(TE, lambda e, h=h:
              e.transpose(pbc[0][0:65, h:h + 1], vsum[0:1, h * 65:(h + 1) * 65],
                          iden_f[0:1, 0:1]),
              reads=("vsum", "iden_f"), writes=("pbc0",), inc=(h == H - 1))
    kb.op(V, lambda e: e.tensor_copy(vsumT[:, :], pbc[0][0:65, 0:H]),
          reads=("pbc0",), writes=("vsumT",))

    def mk_post_bank(h, b):
        c, hf = h // 2, h % 2

        def post():
            kb.op(TE, lambda e, b=b:
                  e.matmul(pbc[0][0:64, 0:512], onesbr[:, :],
                           denb[:, (b % 2) * 512:(b % 2) * 512 + 512],
                           start=True, stop=True),
                  reads=("denb", "onesbr"), writes=("pbc0",))
            kb.op(V, lambda e, b=b:
                  e.tensor_copy(BC16[:, (b % 2) * 512:(b % 2) * 512 + 512],
                                pbc[0][0:64, 0:512]),
                  reads=("pbc0",), writes=("bc16",))
            dst = (lambda sl: Y16p[c][0:64, sl]) if hf == 0 else                   (lambda sl: O16t[:, sl])
            dkey = f"y16p_{c}" if hf == 0 else "o16t"
            kb.op(V, lambda e, b=b, dst=dst:
                  e.tensor_tensor(dst(slice(b * 512, (b + 1) * 512)),
                                  pb[b][0:64, 0:512],
                                  BC16[:, (b % 2) * 512:(b % 2) * 512 + 512],
                                  ALU.mult),
                  reads=(f"pb{b}", "bc16"), writes=(dkey,))
            if b == 0:
                kb.op(V, lambda e, h=h, dst=dst:
                      e.tensor_scalar(dst(slice(0, 1)), vsumT[0:64, h:h + 1],
                                      1.0 / T, None, ALU.mult),
                      reads=("vsumT",), writes=(dkey,))
            if b == NB - 1 and hf == 1:
                kb.op(G, lambda e, c=c:
                      e.dma_start(Y16p[c][64:128, :], O16t[:, :]),
                      reads=("o16t",), writes=(f"y16p_{c}",), dma_sem="qO")
        return post

    fifo = []
    postq = []
    cct = [0]

    def pop_one():
        h, b, q, w, j0, kt, pslot, ptk, lb = fifo.pop(0)
        if kt == 0:
            keep = []
            for ent in postq:
                if ent[2] == b:
                    ent[0]()
                else:
                    keep.append(ent)
            postq[:] = keep
        kb.op(TE, lambda e, h=h, b=b, q=q, w=w, j0=j0, kt=kt, pslot=pslot,
              lb=lb:
              e.matmul(pb[b][0:65, q % 512:q % 512 + w],
                       VtL[kt][:, h * 65:(h + 1) * 65],
                       Pt[pslot][:, j0:j0 + w],
                       start=(kt == 0), stop=lb),
              reads=(f"vtl{kt}", ptk), writes=(f"pb{b}",), inc=lb)
        if lb:
            kb.op(S, lambda e, b=b:
                  e.activation(denr[:, :], pb[b][64:65, 0:512], AF.Ln,
                               bias=epsc[0:1, 3:4]),
                  reads=(f"pb{b}", "epsc"), writes=("denr",))
            kb.op(S, lambda e, b=b:
                  e.activation(denb[:, (b % 2) * 512:(b % 2) * 512 + 512],
                               denr[:, :], AF.Exp, scale=-1.0),
                  reads=("denr",), writes=("denb",))
            postq.append((mk_post_bank(h, b), cct[0], b))

    def tflip(par, pp):
        kb.op(TE, lambda e, par=par, pp=pp:
              e.matmul(pbc[0][:, 0:512], Jmat[:, :],
                       Tst[par][:, pp * 512:(pp + 1) * 512],
                       start=True, stop=True),
              reads=("Jmat", f"tst{par}"), writes=("pbc0",))
        kb.op(V, lambda e, par=par, pp=pp:
              e.tensor_copy(Tst[par][:, pp * 512:(pp + 1) * 512],
                            pbc[0][:, 0:512]),
              reads=("pbc0",), writes=(f"tst{par}",))

    for pp in range(NP):
        tflip(0, pp)
    for h in range(H):
        par = h % 2
        c, hf = h // 2, h % 2
        po = 64 * hf
        if h + 1 < H:
            ld(Tst[1 - par][:, :], A(scr_d, (h + 1) * CP, [[1, 128], [1, T]]),
               cast=True, sem=f"qT{1 - par}", reads=("cscr",),
               writes=(f"tst{1 - par}",))
        for kt in range(NTT):
            if NTT // 2 <= kt < NTT // 2 + NP and h + 1 < H:
                tflip(1 - par, kt - NTT // 2)
            ktg = h * NTT + kt
            while fifo and fifo[0][5] + (fifo[0][0] - h) * NTT <= kt - 3:
                pop_one()
            q0 = kt * 128
            cs = q0
            ci = 0
            while cs < T:
                while postq and postq[0][1] <= cct[0] - 2:
                    postq.pop(0)[0]()
                ce = min((cs // 512 + 1) * 512, T)
                w2 = ce - cs
                sbk = cct[0] % 3
                pslot = ktg % 4
                ptk = f"pt{pslot}_c{ci}"
                jc = cs - q0
                kb.op(TE, lambda e, sbk=sbk, c=c, po=po, kt=kt, q=cs, w=w2:
                      e.matmul(s16[sbk][:, 0:w],
                               K16[c][po:po + 64, kt * 128:(kt + 1) * 128],
                               Q16[c][po:po + 64, q:q + w],
                               start=True, stop=True),
                      reads=(f"k16_{c}", f"q16_{c}"),
                      writes=(f"s16_{sbk}",), inc=True)
                kb.op(S, lambda e, sbk=sbk, pslot=pslot, jc=jc, w2=w2:
                      e.activation(Pt[pslot][:, jc:jc + w2], s16[sbk][:, 0:w2],
                                   AF.Exp),
                      reads=(f"s16_{sbk}",), writes=(ptk,))
                kb.op(V, lambda e, par=par, pslot=pslot, jc=jc, w2=w2:
                      e.tensor_tensor(Pt[pslot][:, jc:jc + w2],
                                      Pt[pslot][:, jc:jc + w2],
                                      Tst[par][:, jc:jc + w2], ALU.mult),
                      reads=(ptk, f"tst{par}"), writes=(ptk,))
                bb = cs // 512
                lb = (kt == min(4 * bb + 3, NTT - 1))
                fifo.append((h, bb, cs, w2, jc, kt, pslot, ptk, lb))
                cct[0] += 1
                while len(fifo) > 24:
                    pop_one()
                cs = ce
                ci += 1
    while fifo:
        pop_one()
    while postq:
        postq.pop(0)[0]()
    st_c1.close()
    st_c_vt.close()
    st_ps_c1.close()
    for i in (4, 5, 6, 7):
        pb[i] = ES.enter_context(nc.psum_tensor(f"pb{i}b", [128, 512], F32))
    kb.fence()

    st_c3 = ExitStack()
    NPH = max(1, NP // 2)

    st_c2 = ExitStack()
    accw = [sb(f"accw{i}", (128, T), F32, st_c2) for i in range(2)]
    rows2P = sb("rows2P", (1, 512), F32, st_c2)
    rowb2 = sb("rowb2", (1, T), BF16, st_c2)
    sq2 = sb("sq2", (128, T), BF16, st_c2)
    Wst2 = [sb(f"ws2_{i}", (128, 2048), BF16, st_c2) for i in range(2)]

    def wload2(par, dram, r0, c0, c1, key):
        for k in range(8):
            ld(Wst2[par][:, k * (c1 - c0):(k + 1) * (c1 - c0)],
               dram[r0 + k * 128:r0 + (k + 1) * 128, c0:c1], cast=True,
               sem=f"qW{par}", writes=(key,), silent=(k > 0))
        kb.finish_queue(f"qW{par}", [key])

    Y16o = Q16
    wload2(0, w_d["Wo"], 0, 0, 256, "ws2_0")
    for qr in range(4):
        par = qr % 2
        if qr + 1 < 4:
            wload2(1 - par, w_d["Wo"], 0, (qr + 1) * 256, (qr + 2) * 256,
                   f"ws2_{1 - par}")
        for dj in range(2):
            dout = qr * 2 + dj
            pbase = (dout % 2) * 4
            apar = dout % 2
            ld(accw[apar][:, :], acc_d[dout * 128:(dout + 1) * 128, :],
               sem=f"qA{apar}", reads=(f"accp{dout}",), writes=(f"accw{apar}",))
            for k in range(8):
                for pp in range(NP):
                    kb.op(TE, lambda e, par=par, k=k, dj=dj, pp=pp, pbase=pbase:
                          e.matmul(pb[pbase + pp][:, 0:512],
                                   Wst2[par][:, k * 256 + dj * 128:
                                            k * 256 + dj * 128 + 128],
                                   Y16p[k][:, pp * 512:(pp + 1) * 512],
                                   start=(k == 0), stop=(k == 7)),
                          reads=(f"ws2_{par}", f"y16p_{k}"),
                          writes=(f"pb{pbase + pp}",), inc=(k == 7))
            for pp in range(NP):
                kb.op(V, lambda e, dout=dout, pp=pp, pbase=pbase:
                      e.tensor_tensor(Y16o[dout][:, pp * 512:(pp + 1) * 512],
                                      pb[pbase + pp][:, 0:512],
                                      XN16[dout][:, pp * 512:(pp + 1) * 512],
                                      ALU.add),
                      reads=(f"pb{pbase + pp}", f"xn16_{dout}"),
                      writes=(f"q16_{dout}",))
                kb.op(V, lambda e, apar=apar, pp=pp, pbase=pbase:
                      e.tensor_tensor(accw[apar][:, pp * 512:(pp + 1) * 512],
                                      accw[apar][:, pp * 512:(pp + 1) * 512],
                                      pb[pbase + pp][:, 0:512], ALU.add),
                      reads=(f"pb{pbase + pp}", f"accw{apar}"),
                      writes=(f"accw{apar}",))
            kb.op(G, lambda e, apar=apar, dout=dout:
                  e.tensor_tensor(accw[apar][:, :], accw[apar][:, :],
                                  XN16[dout][:, :], ALU.add),
                  reads=(f"xn16_{dout}", f"accw{apar}"), writes=(f"accw{apar}",))
            ld(acc_d[dout * 128:(dout + 1) * 128, :], accw[apar][:, :],
               sem=f"qA{apar}", reads=(f"accw{apar}",), writes=(f"accp{dout}",))
    for c in range(8):
        kb.op(V, lambda e, c=c: e.tensor_tensor(sq2[:, :], Y16o[c][:, :],
                                                Y16o[c][:, :], ALU.mult),
              reads=(f"q16_{c}",), writes=("sq2",))
        for pp in range(NP):
            bk = 4 + pp
            kb.op(TE, lambda e, pp=pp, bk=bk, c=c:
                  e.matmul(pb[bk][0:1, 0:512], onesb[:, 0:1],
                           sq2[:, pp * 512:(pp + 1) * 512],
                           start=(c == 0), stop=(c == 7)),
                  reads=("sq2", "onesb"), writes=(f"pb{bk}",), inc=True)
    for pp in range(NP):
        kb.op(S, lambda e, pp=pp:
              e.activation(rows2P[0:1, :], pb[4 + pp][0:1, 0:512], AF.Ln,
                           bias=epsc[0:1, 1:2], scale=1.0 / D),
              reads=(f"pb{4 + pp}", "epsc"), writes=("rows2P",))
        kb.op(S, lambda e, pp=pp:
              e.activation(rowb2[0:1, pp * 512:(pp + 1) * 512],
                           rows2P[0:1, :], AF.Exp, scale=-0.5),
              reads=("rows2P",), writes=("rowb2",))
    YN16 = K16
    for c in range(8):
        for pp in range(NP):
            bk = pp % 2
            kb.op(TE, lambda e, c=c, pp=pp, bk=bk:
                  e.matmul(pb[bk][:, 0:512], ln2b[:, c * 128:(c + 1) * 128],
                           rowb2[0:1, pp * 512:(pp + 1) * 512],
                           start=True, stop=True),
                  reads=("ln2b", "rowb2"), writes=(f"pb{bk}",))
            kb.op(V, lambda e, c=c, pp=pp, bk=bk:
                  e.tensor_tensor(YN16[c][:, pp * 512:(pp + 1) * 512],
                                  Y16o[c][:, pp * 512:(pp + 1) * 512],
                                  pb[bk][:, 0:512], ALU.mult),
                  reads=(f"q16_{c}", f"pb{bk}"), writes=(f"k16_{c}",))
    st_c2.close()
    st_c_bc.close()
    kb.fence()

    H16 = [sb(f"h16_{i}", (128, T // 2), BF16, st_c3) for i in range(24)] + \
          [XN16[i] for i in range(8)]
    W2R = sb("w2r", (128, 32 * D), BF16, st_c3)
    Wst1 = [sb(f"ws1_{i}", (128, 2048), BF16, st_c3) for i in range(2)]
    accw2 = [sb(f"acw2_{i}", (128, T // 2), F32, st_c3) for i in range(2)]

    def w1load(par, e16):
        for k in range(8):
            ld(Wst1[par][:, k * 256:(k + 1) * 256],
               w_d["W1"][k * 128:(k + 1) * 128, e16 * 256:(e16 + 1) * 256],
               cast=True, sem=f"qW{par}", writes=(f"ws1_{par}",),
               silent=(k > 0))
        kb.finish_queue(f"qW{par}", [f"ws1_{par}"])

    def w2rload(h0, h1):
        for hk in range(h0, h1):
            ld(W2R[:, hk * D:(hk + 1) * D],
               w_d["W2"][hk * 128:(hk + 1) * 128, :],
               cast=True, sem="qF", silent=(hk % 4 != 0))
            if hk % 4 == 3 or hk == h1 - 1:
                kb.finish_queue("qF", [f"w2r_{k}" for k in
                                       range(hk - hk % 4, hk + 1)])
    w1load(0, 0)
    w2rload(0, 4)
    w1load(1, 1)

    def w1_emit(tp2, e8):
        t0c = tp2 * (T // 2)
        par = e8 % 2
        if e8 + 1 < 16:
            w1load(1 - par, e8 + 1)
        for hl in range(2):
            hk = e8 * 2 + hl
            pbase = (hk % 2) * 2
            for k in range(8):
                for pc in range(NPH):
                    kb.op(TE, lambda e, par=par, k=k, hl=hl, pc=pc,
                          pbase=pbase, t0c=t0c:
                          e.matmul(pb[pbase + pc][:, 0:512],
                                   Wst1[par][:, k * 256 + hl * 128:
                                            k * 256 + hl * 128 + 128],
                                   YN16[k][:, t0c + pc * 512:
                                           t0c + (pc + 1) * 512],
                                   start=(k == 0), stop=(k == 7)),
                          reads=(f"ws1_{par}", f"k16_{k}"),
                          writes=(f"pb{pbase + pc}",), inc=(k == 7))
            for pc in range(NPH):
                kb.op(S, lambda e, hk=hk, pc=pc, pbase=pbase:
                      e.activation(H16[hk][:, pc * 512:(pc + 1) * 512],
                                   pb[pbase + pc][:, 0:512], AF.Relu),
                      reads=(f"pb{pbase + pc}",), writes=(f"h16_{hk}",))

    def w2_emit(dp, hk):
        for j, dout in enumerate((2 * dp, 2 * dp + 1)):
            for pc in range(NPH):
                bk = 4 + 2 * j + pc if NPH == 2 else 4 + j
                kb.op(TE, lambda e, dout=dout, pc=pc, bk=bk, hk=hk:
                      e.matmul(pb[bk][:, 0:512],
                               W2R[:, hk * D + dout * 128:
                                    hk * D + (dout + 1) * 128],
                               H16[hk][:, pc * 512:(pc + 1) * 512],
                               start=(hk == 0), stop=(hk == 31)),
                      reads=(f"w2r_{hk}", f"h16_{hk}"),
                      writes=(f"pb{bk}",),
                      inc=(j == 1 and pc == NPH - 1))

    def w2_evac_pre(tp2, dp):
        t0c = tp2 * (T // 2)
        for j, dout in enumerate((2 * dp, 2 * dp + 1)):
            ld(accw2[j][:, :], acc_d[dout * 128:(dout + 1) * 128,
                                     t0c:t0c + T // 2],
               sem=f"qA{j}", reads=(f"accp{dout}",), writes=(f"acw2_{j}",))

    def w2_evac(tp2, dp):
        t0c = tp2 * (T // 2)
        for j, dout in enumerate((2 * dp, 2 * dp + 1)):
            for pc in range(NPH):
                bk = 4 + 2 * j + pc if NPH == 2 else 4 + j
                kb.op(V, lambda e, j=j, pc=pc, bk=bk:
                      e.tensor_tensor(accw2[j][:, pc * 512:(pc + 1) * 512],
                                      accw2[j][:, pc * 512:(pc + 1) * 512],
                                      pb[bk][:, 0:512], ALU.add),
                      reads=(f"pb{bk}", f"acw2_{j}"), writes=(f"acw2_{j}",))
            ld(acc_d[dout * 128:(dout + 1) * 128, t0c:t0c + T // 2],
               accw2[j][:, :], sem=f"qA{j}", reads=(f"acw2_{j}",),
               writes=(f"accp{dout}_{tp2}",))

    for tp2 in range(2):
        if tp2 == 1:
            w1load(0, 0)
        w2_evac_pre(tp2, 0)
        w2lag = 0
        for e8 in range(16):
            if tp2 == 0:
                lo, hi = 4 + 2 * e8, min(6 + 2 * e8, 32)
                if lo < 32:
                    w2rload(lo, hi)
            w1_emit(tp2, e8)
            while w2lag <= 2 * e8 - 2:
                w2_emit(0, w2lag)
                w2lag += 1
        while w2lag < 32:
            w2_emit(0, w2lag)
            w2lag += 1
        w2_evac(tp2, 0)
        for dp in range(1, 4):
            w2_evac_pre(tp2, dp)
            for hk in range(32):
                w2_emit(dp, hk)
            w2_evac(tp2, dp)
    st_c3.close()
    st_c_k.close()
    kb.fence()

    st_d = ExitStack()
    ND = 8
    xo = [sb(f"xo{i}", (128, D), F32, st_d) for i in range(ND)]
    ot = [sb(f"ot{i}", (128, D), F32, st_d) for i in range(ND)]
    asl = [sb(f"asl{i}", (128, D), F32, st_d) for i in range(ND)]
    for tt in range(NTT):
        p = tt % ND
        hv = 0 if tt * 128 < T // 2 else 1
        ld(xo[p][:, :], x_d[tt * 128:(tt + 1) * 128, :], sem=f"qX{p % 4}",
           writes=(f"xo{p}",))
        kb.finish_queue(f"qX{p % 4}", [f"xo{p}"])
        src = A(acc_d, tt * 128,
                [[T, 128], [128 * T, 8], [1, 128]])
        dst = A(asl[p], 0, [[D, 128], [128, 8], [1, 128]])
        kb.op("sync", lambda e, d=dst, s=src: e.dma_start(d, s),
              reads=tuple(f"accp{c}_{hv}" for c in range(8)),
              writes=(f"asl{p}",), dma_sem=f"qD{p % 4}")
        kb.finish_queue(f"qD{p % 4}", [f"asl{p}"])
        for c in range(8):
            bk = 6 + (c % 2)
            sl = ((c // 2) % 2) * 128
            kb.op(TE, lambda e, p=p, c=c, bk=bk, sl=sl:
                  e.transpose(pb[bk][:, sl:sl + 128],
                              asl[p][:, c * 128:(c + 1) * 128], iden_f[:, :]),
                  reads=(f"asl{p}", "iden_f"), writes=(f"pb{bk}",))
            kb.op(V, lambda e, p=p, c=c, bk=bk, sl=sl:
                  e.tensor_tensor(ot[p][:, c * 128:(c + 1) * 128],
                                  pb[bk][:, sl:sl + 128],
                                  xo[p][:, c * 128:(c + 1) * 128], ALU.add),
                  reads=(f"pb{bk}", f"xo{p}"), writes=(f"ot{p}",))
        kb.op("sync", lambda e, tt=tt, p=p:
              e.dma_start(out_d[tt * 128:(tt + 1) * 128, :], ot[p][:, :]),
              reads=(f"ot{p}",), writes=("outd",), dma_sem="qOut")
    st_d.close()
    kb.fence()
    kb.wait("sync", "qOut", kb.cnt["qOut"])

    kb.emit()
    kb.stack.close()
    return nc


def kernel(**inputs):
    x = np.ascontiguousarray(np.asarray(inputs["L0_attn_x"], dtype=np.float32))
    T = x.shape[1]
    base = {
        "ln1": inputs["L0_attn_ln1_w"], "Wq": inputs["L0_attn_Wq"],
        "Wk": inputs["L0_attn_Wk"], "Wv": inputs["L0_attn_Wv"],
        "relb": inputs["L0_attn_rel_bias"], "scale": inputs["L0_attn_scale"],
        "Wo": inputs["L0_attn_Wo"], "ln2": inputs["L0_attn_ln2_w"],
        "W1": inputs["L0_attn_W1"], "W2": inputs["L0_attn_W2"],
        "gw": inputs["L0_ffn_gate_w"], "gb": inputs["L0_ffn_gate_b"],
        "ew": inputs["L0_ffn_expert_w"], "eb": inputs["L0_ffn_expert_b"],
    }
    base = {k: np.ascontiguousarray(np.asarray(v, dtype=np.float32))
            for k, v in base.items()}
    base["E"] = _bucket_onehot(T)
    nc = build_nc(T=T)
    in_maps = [dict(base, x=np.ascontiguousarray(x[i]))
               for i in range(x.shape[0])]
    res = run_bass_kernel_spmd(nc, in_maps, core_ids=list(range(len(in_maps))))
    out = np.stack([res.results[i]["out"] for i in range(len(in_maps))], axis=0)
    return out.astype(np.float32)


# version 30
# speedup vs baseline: 1.0142x; 1.0112x over previous
import math
import numpy as np
from contextlib import ExitStack

import concourse.bass as bass
import concourse.mybir as mybir
import concourse.bass_utils as _bass_utils
from concourse.bass_utils import run_bass_kernel_spmd

if not getattr(_bass_utils, "_ldw_opt_patched", False):
    _orig_run_command = _bass_utils.run_command

    def _run_command_ldw(cmd, *a, **kw):
        if isinstance(cmd, (list, tuple)):
            cmd = ["--enable-ldw-opt=true" if c == "--enable-ldw-opt=false"
                   else c for c in cmd]
        return _orig_run_command(cmd, *a, **kw)

    _bass_utils.run_command = _run_command_ldw
    _bass_utils._ldw_opt_patched = True

F32 = mybir.dt.float32
BF16 = mybir.dt.bfloat16
AF = mybir.ActivationFunctionType
ALU = mybir.AluOpType
AX = mybir.AxisListType

D = 1024
H = 16
HID = 4096
NEXP = 10
NTRU = 8
LN_EPS = 1e-5
RMS_EPS = 1e-6
KQ_EPS = 1e-6
NEG = -1e30
BIG = 1e9
QK_BOUND = 1.05


def _bucket_onehot(S):
    n = S + 128
    E = np.zeros((32, n), dtype=np.float32)
    i = np.arange(128, n)
    d = (i - 127).astype(np.int64)
    safe = np.maximum(d, 1).astype(np.float32)
    t = np.log(safe / np.float32(16.0))
    t = t / np.float32(math.log(128.0 / 16.0))
    t = t * np.float32(16.0)
    large = np.minimum(16 + t.astype(np.int32), 31)
    bucket = np.where(d < 16, d, large).astype(np.int64)
    E[bucket, i] = 1.0
    return E


class KB:

    ENGINES = ("sync", "gpsimd", "tensor", "vector", "scalar")

    def __init__(self, nc):
        self.nc = nc
        self.plan = {e: [] for e in self.ENGINES}
        self.sems = {}
        self.cnt = {}
        self.waited = {e: {} for e in self.ENGINES}
        self.bufs = {}
        self.pending = {e: [] for e in self.ENGINES}
        self.eng_sem = {"gpsimd": "G", "tensor": "T", "vector": "V",
                        "scalar": "S", "sync": None}
        self.fence_waits = {}
        self.stack = ExitStack()

    def sem(self, name):
        if name not in self.sems:
            self.sems[name] = self.stack.enter_context(self.nc.semaphore(name))
            self.cnt[name] = 0
        return self.sems[name]

    def op(self, eng, fn, reads=(), writes=(), inc=True, dma_sem=None,
           extra_waits=()):
        waits = list(extra_waits)

        def need(w):
            if w is None:
                return
            if w[0] == "PEND":
                if w[1] != eng:
                    raise RuntimeError(f"cross-engine access to pending buf {w}")
                return
            waits.append(w)

        for b in reads:
            ent = self.bufs.get(b)
            if ent:
                need(ent.get("w"))
        for b in writes:
            ent = self.bufs.get(b)
            if ent:
                need(ent.get("w"))
                for r in ent.get("r", ()):
                    need(r)
                for peng in ent.get("pr", ()):
                    if peng != eng:
                        raise RuntimeError(
                            f"write to {b} with pending readers on {peng}")
        ww = {}
        for s, v in list(waits) + list(self.fence_waits.items()):
            ww[s] = max(ww.get(s, 0), v)
        w_em = []
        for s, v in ww.items():
            if self.waited[eng].get(s, -1) >= v:
                continue
            self.waited[eng][s] = v
            self.sem(s)
            w_em.append((s, v))

        if dma_sem is not None:
            self.sem(dma_sem)
            self.cnt[dma_sem] += 16
            eff = (dma_sem, self.cnt[dma_sem])
        elif inc and self.eng_sem[eng]:
            s = self.eng_sem[eng]
            self.sem(s)
            self.cnt[s] += 1
            eff = (s, self.cnt[s])
        else:
            eff = ("PEND", eng)

        if eff[0] != "PEND":
            for key, kind in self.pending[eng]:
                ent = self.bufs.setdefault(key, {"w": None, "r": [], "pr": set()})
                if kind == "w":
                    ent["w"] = eff
                    ent["r"] = []
                else:
                    ent["r"].append(eff)
                ent["pr"].discard(eng)
            self.pending[eng] = []

        for b in reads:
            ent = self.bufs.setdefault(b, {"w": None, "r": [], "pr": set()})
            if eff[0] == "PEND":
                self.pending[eng].append((b, "r"))
                ent["pr"].add(eng)
            else:
                ent["r"].append(eff)
        for b in writes:
            ent = self.bufs.setdefault(b, {"w": None, "r": [], "pr": set()})
            if eff[0] == "PEND":
                ent["w"] = ("PEND", eng)
                self.pending[eng].append((b, "w"))
                ent["pr"].add(eng)
            else:
                ent["w"] = eff
                ent["r"] = []
        self.plan[eng].append((w_em, fn, None if eff[0] == "PEND" else eff,
                               dma_sem is not None))

    def fence(self):
        for eng in self.ENGINES:
            if self.pending[eng]:
                raise RuntimeError(f"fence with pending ops on {eng}")
        self.fence_waits = dict(self.cnt)

    def finish_queue(self, sem, keys):
        for k in keys:
            ent = self.bufs.setdefault(k, {"w": None, "r": [], "pr": set()})
            ent["w"] = (sem, self.cnt[sem])
            ent["r"] = []

    def wait(self, eng, sem, val):
        self.sem(sem)
        self.plan[eng].append(([(sem, val)], None, None, False))

    def emit(self):
        kb = self
        with kb.nc.Block() as block:
            for eng in self.ENGINES:
                steps = self.plan[eng]
                if not steps:
                    continue

                def body(e, steps=steps):
                    for waits, fn, eff, is_dma in steps:
                        for s, v in waits:
                            e.wait_ge(kb.sems[s], v)
                        if fn is None:
                            continue
                        inst = fn(e)
                        if eff is not None:
                            inst.then_inc(kb.sems[eff[0]], 16 if is_dma else 1)
                getattr(block, eng)(body)


def build_nc(T=2048):
    nc = bass.Bass("TRN2", target_bir_lowering=False)
    NTT = T // 128
    NP = T // 512
    CP = T + 128

    x_d = nc.dram_tensor("x", [T, D], F32, kind="ExternalInput")
    w_d = {}
    for nm, shp in [("ln1", [D]), ("Wq", [D, D]), ("Wk", [D, D]), ("Wv", [D, D]),
                    ("relb", [32, H]), ("scale", [H]), ("Wo", [D, D]),
                    ("ln2", [D]), ("W1", [D, HID]), ("W2", [HID, D]),
                    ("gw", [NEXP, D]), ("gb", [NEXP]),
                    ("ew", [NTRU, D, D]), ("eb", [NTRU, D]),
                    ("E", [32, CP])]:
        w_d[nm] = nc.dram_tensor(nm, shp, F32, kind="ExternalInput")
    out_d = nc.dram_tensor("out", [T, D], F32, kind="ExternalOutput")
    scr_d = nc.dram_tensor("cscr", [H, CP], F32, kind="Internal")
    acc_d = nc.dram_tensor("accscr", [8 * 128, T], F32, kind="Internal")

    kb = KB(nc)
    A = bass.AP
    ES = kb.stack

    def sb(name, shape, dt, st=None):
        return (st or ES).enter_context(nc.sbuf_tensor(name, list(shape), dt))

    pb = [None] * 8
    for i in (0, 1, 2, 3):
        pb[i] = ES.enter_context(nc.psum_tensor(f"pb{i}", [128, 512], F32))
    st_ps_a = ExitStack()
    for i in (4, 5, 6, 7):
        pb[i] = st_ps_a.enter_context(nc.psum_tensor(f"pb{i}a", [128, 512], F32))

    iden_f = sb("iden_f", (128, 128), F32)
    Jmat = sb("Jmat", (128, 128), BF16)
    onesb = sb("onesb", (128, 2), BF16)
    onesf = sb("onesf", (1, 64), F32)
    onesbr = sb("onesbr", (1, 64), BF16)
    iota_f = sb("iota_f", (128, 128), F32)
    iota_p = sb("iota_p", (128, 1), F32)
    iota10 = sb("iota10", (128, 12), F32)
    iotaB = sb("iotaB", (128, 12), F32)
    epsc = sb("epsc", (128, 4), F32)
    selq = sb("selq", (128, 2), BF16)
    sel2T = sb("sel2T", (2, 128), BF16)
    relbS = sb("relbS", (32, H), F32)
    relbT = sb("relbT", (H, 32), F32)
    scale16 = sb("scale16", (H, 2), F32)
    scaleP = sb("scaleP", (2, 8), F32)
    gmaxc = sb("gmaxc", (H, 2), F32)
    gbc = sb("gbc", (NEXP, 1), F32)
    ln1b = sb("ln1b", (1, D), BF16)
    ln2b = sb("ln2b", (1, D), BF16)
    ebCol = sb("ebCol", (128, NTRU * 8), F32)
    g1 = sb("g1", (128, 12), F32)
    g2 = sb("g2", (128, 12), F32)
    g3 = sb("g3", (128, 12), F32)
    gs = sb("gs", (128, 16), F32)

    XN16 = [sb(f"xn16_{c}", (128, T), BF16) for c in range(8)]

    MISC = "qM"

    def ld(dst, src, cast=False, sem=MISC, reads=(), writes=(), silent=False,
           slow=False):
        eng = "gpsimd" if cast else "sync"
        ew = () if silent else (((sem, kb.cnt.get(sem, 0)),)
                                if kb.cnt.get(sem, 0) else ())
        kb.op(eng, lambda e, d=dst, s=src, sl=slow:
              e.dma_start(d, s, allow_slow_non_contiguous=sl),
              reads=() if silent else reads, writes=() if silent else writes,
              dma_sem=sem, extra_waits=ew)

    ld(relbS[:, :], w_d["relb"][:, :], writes=("relbS",))
    ld(relbT[:, :], A(w_d["relb"], 0, [[1, H], [H, 32]]), writes=("relbT",), slow=True)
    ld(scale16[:, 0:1], A(w_d["scale"], 0, [[1, H], [1, 1]]), writes=("scale16",))
    ld(scaleP[:, :], A(w_d["scale"], 0, [[1, 2], [2, 8]]), writes=("scaleP",), slow=True)
    ld(gbc[:, :], A(w_d["gb"], 0, [[1, NEXP], [1, 1]]), writes=("gbc",))
    ld(ln1b[:, :], A(w_d["ln1"], 0, [[D, 1], [1, D]]), cast=True, sem="qC",
       writes=("ln1b",))
    ld(ln2b[:, :], A(w_d["ln2"], 0, [[D, 1], [1, D]]), cast=True, sem="qC",
       writes=("ln2b",))
    for e_ in range(NTRU):
        ld(ebCol[:, e_ * 8:(e_ + 1) * 8], A(w_d["eb"], e_ * D, [[1, 128], [128, 8]]),
           writes=("ebCol",), slow=True)

    G, V, S, TE = "gpsimd", "vector", "scalar", "tensor"
    kb.op(G, lambda e: e.iota(iota_f[:, :], [[1, 128]], channel_multiplier=0,
                              allow_small_or_imprecise_dtypes=True),
          writes=("iota_f",))
    kb.op(G, lambda e: e.iota(iota_p[:, :], [[0, 1]], channel_multiplier=1,
                              allow_small_or_imprecise_dtypes=True),
          writes=("iota_p",))
    kb.op(G, lambda e: e.iota(iota10[:, 0:NEXP], [[1, NEXP]], channel_multiplier=0,
                              allow_small_or_imprecise_dtypes=True),
          writes=("iota10",))
    kb.op(V, lambda e: e.tensor_scalar(iotaB[:, 0:NEXP], iota10[:, 0:NEXP],
                                       1.0e4, None, ALU.add),
          reads=("iota10",), writes=("iotaB",))
    kb.op(G, lambda e: e.memset(onesb[:, :], 1.0), writes=("onesb",))
    kb.op(G, lambda e: e.memset(onesf[:, :], 1.0), writes=("onesf",))
    kb.op(G, lambda e: e.memset(onesbr[:, :], 1.0), writes=("onesbr",))
    kb.op(G, lambda e: e.memset(epsc[:, 0:1], LN_EPS), writes=("epsc",))
    kb.op(G, lambda e: e.memset(epsc[:, 1:2], RMS_EPS), writes=("epsc",))
    kb.op(G, lambda e: e.memset(epsc[:, 2:3], KQ_EPS), writes=("epsc",))
    kb.op(G, lambda e: e.memset(epsc[:, 3:4], 1e-38), writes=("epsc",))
    kb.op(V, lambda e: e.tensor_scalar(iden_f[:, :], iota_f[:, :], iota_p[:, :],
                                       None, ALU.is_equal),
          reads=("iota_f", "iota_p"), writes=("iden_f",))
    kb.op(V, lambda e: e.tensor_scalar(Jmat[:, :], iota_f[:, :], iota_p[:, :],
                                       127.0, ALU.add, ALU.is_equal),
          reads=("iota_f", "iota_p"), writes=("Jmat",))
    s2i = sb("s2i", (2, 128), F32)
    sel2F = sb("sel2F", (2, 128), F32)
    ip2 = sb("ip2", (2, 1), F32)
    kb.op(G, lambda e: e.iota(s2i[:, :], [[1, 2], [0, 64]], channel_multiplier=0,
                              allow_small_or_imprecise_dtypes=True),
          writes=("s2i",))
    kb.op(G, lambda e: e.iota(ip2[:, :], [[0, 1]], channel_multiplier=1,
                              allow_small_or_imprecise_dtypes=True),
          writes=("ip2",))
    kb.op(V, lambda e: e.tensor_scalar(sel2T[:, :], s2i[:, :], ip2[:, :], None,
                                       ALU.is_equal),
          reads=("s2i", "ip2"), writes=("sel2T",))
    kb.op(V, lambda e: e.tensor_scalar(sel2F[:, :], s2i[:, :], ip2[:, :], None,
                                       ALU.is_equal),
          reads=("s2i", "ip2"), writes=("sel2F",))
    kb.op(TE, lambda e: e.transpose(pb[0][:, 0:2], sel2F[:, :],
                                    iden_f[0:2, 0:2]),
          reads=("sel2F", "iden_f"), writes=("pb0",))
    kb.op(V, lambda e: e.tensor_copy(selq[:, :], pb[0][:, 0:2]),
          reads=("pb0",), writes=("selq",))

    st_cw = ExitStack()
    cwT = sb("cwT", (NTRU, T), BF16, st_cw)
    ohE = sb("ohE", (NTRU, NTRU * 128), BF16, st_cw)
    st_a = ExitStack()
    XNF = [sb(f"xnf{c}", (128, T), F32, st_a) for c in range(8)]
    st_a2 = ExitStack()
    EsS = sb("EsS", (32, CP), F32, st_a2)
    cpadS = sb("cpadS", (H, CP), F32, st_a2)
    xt = [sb(f"xt{i}", (128, D), F32, st_a2) for i in range(2)]
    xnt = [sb(f"xnt{i}", (128, D), F32, st_a2) for i in range(2)]
    sqt = [sb(f"sqt{i}", (128, D), F32, st_a2) for i in range(2)]
    st_s = sb("st_s", (128, 16), F32, st_a2)

    ld(EsS[:, :], w_d["E"][:, :], writes=("EsS",))
    kb.finish_queue(MISC, ["relbS", "relbT", "scale16", "scaleP", "gbc",
                           "ebCol", "EsS"])
    kb.finish_queue("qC", ["ln1b", "ln2b"])
    kb.op(S, lambda e: e.activation(scale16[:, 1:2], scale16[:, 0:1], AF.Abs),
          reads=("scale16",), writes=("scale16",))
    kb.op(V, lambda e: e.reduce_max(gmaxc[:, 0:1], relbT[:, :], axis=AX.X,
                                    apply_absolute_value=True),
          reads=("relbT",), writes=("gmaxc",))
    kb.op(V, lambda e: e.tensor_scalar(gmaxc[:, 0:1], gmaxc[:, 0:1], QK_BOUND,
                                       None, ALU.add),
          reads=("gmaxc",), writes=("gmaxc",))
    kb.op(V, lambda e: e.tensor_tensor(gmaxc[:, 1:2], gmaxc[:, 0:1],
                                       scale16[:, 1:2], ALU.mult),
          reads=("gmaxc", "scale16"), writes=("gmaxc",))
    ofs = 0
    while ofs < CP:
        w = min(512, CP - ofs)
        bk = (ofs // 512) % 2
        kb.op(TE, lambda e, o=ofs, w=w, bk=bk:
              e.matmul(pb[bk][0:H, 0:w], relbS[:, :], EsS[:, o:o + w],
                       start=True, stop=True),
              reads=("relbS", "EsS"), writes=(f"pb{bk}",))
        kb.op(V, lambda e, o=ofs, w=w, bk=bk:
              e.tensor_scalar(cpadS[:, o:o + w], pb[bk][0:H, 0:w],
                              scale16[:, 0:1], gmaxc[:, 1:2],
                              ALU.mult, ALU.subtract),
              reads=(f"pb{bk}", "scale16", "gmaxc"), writes=("cpadS",))
        ofs += w
    kb.op(V, lambda e: e.memset(cpadS[:, 0:128], NEG), writes=("cpadS",))
    kb.op(S, lambda e: e.activation(cpadS[:, :], cpadS[:, :], AF.Exp),
          reads=("cpadS",), writes=("cpadS",))
    ld(scr_d[:, :], cpadS[:, :], sem="qS", reads=("cpadS",), writes=("cscr",))

    for tt in range(NTT):
        p = tt % 2
        o = p * 8
        sk = f"st_s{p}"
        ld(xt[p][:, :], x_d[tt * 128:(tt + 1) * 128, :], sem=f"qX{p}",
           writes=(f"xt{p}",))
        kb.op(V, lambda e, p=p, o=o:
              e.reduce_sum(st_s[:, o:o + 1], xt[p][:, :], axis=AX.X),
              reads=(f"xt{p}",), writes=(sk,))
        kb.op(V, lambda e, p=p: e.tensor_tensor(sqt[p][:, :], xt[p][:, :],
                                                xt[p][:, :], ALU.mult),
              reads=(f"xt{p}",), writes=(f"sqt{p}",))
        kb.op(V, lambda e, p=p, o=o:
              e.reduce_sum(st_s[:, o + 1:o + 2], sqt[p][:, :], axis=AX.X),
              reads=(f"sqt{p}",), writes=(sk,))
        kb.op(V, lambda e, o=o:
              e.tensor_scalar(st_s[:, o + 2:o + 3], st_s[:, o:o + 1], 1.0 / D,
                              None, ALU.mult),
              reads=(sk,), writes=(sk,))
        kb.op(V, lambda e, o=o:
              e.tensor_tensor(st_s[:, o + 3:o + 4], st_s[:, o + 2:o + 3],
                              st_s[:, o + 2:o + 3], ALU.mult),
              reads=(sk,), writes=(sk,))
        kb.op(V, lambda e, o=o:
              e.tensor_scalar(st_s[:, o + 4:o + 5], st_s[:, o + 1:o + 2],
                              1.0 / D, st_s[:, o + 3:o + 4],
                              ALU.mult, ALU.subtract),
              reads=(sk,), writes=(sk,))
        kb.op(S, lambda e, o=o:
              e.activation(st_s[:, o + 5:o + 6], st_s[:, o + 4:o + 5], AF.Ln,
                           bias=epsc[:, 0:1]),
              reads=(sk, "epsc"), writes=(sk,))
        kb.op(S, lambda e, o=o:
              e.activation(st_s[:, o + 6:o + 7], st_s[:, o + 5:o + 6], AF.Exp,
                           scale=-0.5),
              reads=(sk,), writes=(sk,))
        kb.op(V, lambda e, p=p, o=o:
              e.tensor_scalar(xnt[p][:, :], xt[p][:, :],
                              st_s[:, o + 2:o + 3], st_s[:, o + 6:o + 7],
                              ALU.subtract, ALU.mult),
              reads=(f"xt{p}", sk), writes=(f"xnt{p}",))
        for c in range(8):
            bk = 6 + (c % 2)
            sl = ((c // 2) % 2) * 128
            kb.op(TE, lambda e, p=p, c=c, bk=bk, sl=sl:
                  e.transpose(pb[bk][:, sl:sl + 128],
                              xnt[p][:, c * 128:(c + 1) * 128], iden_f[:, :]),
                  reads=(f"xnt{p}", "iden_f"), writes=(f"pb{bk}",))
            kb.op(S, lambda e, tt=tt, c=c, bk=bk, sl=sl:
                  e.copy(XNF[c][:, tt * 128:(tt + 1) * 128],
                         pb[bk][:, sl:sl + 128]),
                  reads=(f"pb{bk}",), writes=(f"xnf{c}",))
    for c in range(8):
        kb.op(G, lambda e, c=c: e.tensor_copy(XN16[c][:, :], XNF[c][:, :]),
              reads=(f"xnf{c}",), writes=(f"xn16_{c}",))
    st_a2.close()
    kb.fence()

    st_b1 = ExitStack()
    gwS = sb("gwS", (NEXP, D), F32, st_b1)
    gwT = sb("gwT", (128, 80), F32, st_b1)
    lgS = sb("lgS", (NEXP, T), F32, st_b1)
    lgT = sb("lgT", (128, NTT * 16), F32, st_b1)
    cwTok = sb("cwTok", (128, NTT * 8), F32, st_b1)

    ld(gwS[:, :], w_d["gw"][:, :], writes=("gwS",))
    kb.finish_queue(MISC, ["gwS"])
    oh_i = sb("oh_i", (NTRU, NTRU * 128), F32, st_b1)
    ip8 = sb("ip8", (NTRU, 1), F32, st_b1)
    kb.op(G, lambda e: e.iota(oh_i[:, :], [[1, NTRU], [0, 128]],
                              channel_multiplier=0,
                              allow_small_or_imprecise_dtypes=True),
          writes=("oh_i",))
    kb.op(G, lambda e: e.iota(ip8[:, :], [[0, 1]], channel_multiplier=1,
                              allow_small_or_imprecise_dtypes=True),
          writes=("ip8",))
    kb.op(V, lambda e: e.tensor_scalar(ohE[:, :], oh_i[:, :], ip8[:, :], None,
                                       ALU.is_equal),
          reads=("oh_i", "ip8"), writes=("ohE",))
    for c in range(8):
        bk = c % 2
        kb.op(TE, lambda e, c=c, bk=bk:
              e.transpose(pb[bk][:, 0:NEXP], gwS[:, c * 128:(c + 1) * 128],
                          iden_f[0:NEXP, 0:NEXP]),
              reads=("gwS", "iden_f"), writes=(f"pb{bk}",))
        kb.op(V, lambda e, c=c, bk=bk:
              e.tensor_copy(gwT[:, c * 10:(c + 1) * 10], pb[bk][:, 0:NEXP]),
              reads=(f"pb{bk}",), writes=("gwT",))
    for pp in range(NP):
        for c in range(8):
            kb.op(TE, lambda e, pp=pp, c=c:
                  e.matmul(pb[4 + pp % 4][0:NEXP, 0:512],
                           gwT[:, c * 10:(c + 1) * 10],
                           XNF[c][:, pp * 512:(pp + 1) * 512],
                           start=(c == 0), stop=(c == 7)),
                  reads=("gwT", f"xnf{c}"), writes=(f"pb{4 + pp % 4}",),
                  inc=(c == 7))
        kb.op(V, lambda e, pp=pp:
              e.tensor_scalar(lgS[:, pp * 512:(pp + 1) * 512],
                              pb[4 + pp % 4][0:NEXP, 0:512], gbc[:, 0:1],
                              None, ALU.add),
              reads=(f"pb{4 + pp % 4}", "gbc"), writes=("lgS",))
    for tt in range(NTT):
        bk = tt % 2
        kb.op(TE, lambda e, tt=tt, bk=bk:
              e.transpose(pb[bk][:, 0:NEXP], lgS[:, tt * 128:(tt + 1) * 128],
                          iden_f[0:NEXP, 0:NEXP]),
              reads=("lgS", "iden_f"), writes=(f"pb{bk}",))
        kb.op(V, lambda e, tt=tt, bk=bk:
              e.tensor_copy(lgT[:, tt * 16:tt * 16 + NEXP], pb[bk][:, 0:NEXP]),
              reads=(f"pb{bk}",), writes=("lgT",))
    gcol = sb("gcol", (128, 16 * 8), F32, st_b1)
    gw1 = sb("gw1", (128, NTT * 16), F32, st_b1)
    gw2 = sb("gw2", (128, NTT * 16), F32, st_b1)
    gw3 = sb("gw3", (128, NTT * 16), F32, st_b1)

    def d3(h, width=16, n=10):
        return A(h, 0, [[NTT * width, 128], [width, NTT], [1, n]])

    def b3(slot):
        return A(gcol, slot * NTT, [[16 * 8, 128], [1, NTT], [0, 10]])

    def gcolf(slot):
        return gcol[:, slot * NTT:(slot + 1) * NTT]

    iota3 = A(iota10, 0, [[12, 128], [0, NTT], [1, 10]])
    iotaB3 = A(iotaB, 0, [[12, 128], [0, NTT], [1, 10]])

    def vg(fn, reads, writes):
        kb.op(V, fn, reads=reads, writes=writes)

    vg(lambda e: e.tensor_reduce(gcolf(0), d3(lgT), AX.X, ALU.max),
       ("lgT",), ("gcol",))
    vg(lambda e: e.tensor_tensor(d3(gw1), d3(lgT), b3(0), ALU.subtract),
       ("lgT", "gcol"), ("gw1",))
    kb.op(S, lambda e: e.activation(d3(gw1), d3(gw1), AF.Exp),
          reads=("gw1",), writes=("gw1",))
    vg(lambda e: e.tensor_reduce(gcolf(1), d3(gw1), AX.X, ALU.add),
       ("gw1",), ("gcol",))
    vg(lambda e: e.reciprocal(gcolf(1), gcolf(1)), ("gcol",), ("gcol",))
    vg(lambda e: e.tensor_tensor(d3(gw1), d3(gw1), b3(1), ALU.mult),
       ("gw1", "gcol"), ("gw1",))
    vg(lambda e: e.tensor_reduce(gcolf(2), d3(gw1), AX.X, ALU.max),
       ("gw1",), ("gcol",))
    vg(lambda e: e.tensor_tensor(d3(gw2), d3(gw1), b3(2), ALU.is_equal),
       ("gw1", "gcol"), ("gw2",))
    vg(lambda e: e.scalar_tensor_tensor(d3(gw3), d3(gw2), -1.0e4, iotaB3,
                                        ALU.mult, ALU.add),
       ("gw2", "iotaB"), ("gw3",))
    vg(lambda e: e.tensor_reduce(gcolf(3), d3(gw3), AX.X, ALU.min),
       ("gw3",), ("gcol",))
    vg(lambda e: e.tensor_tensor(d3(gw2), iota3, b3(3), ALU.is_equal),
       ("iota10", "gcol"), ("gw2",))
    vg(lambda e: e.scalar_tensor_tensor(d3(gw3), d3(gw2), -BIG, d3(gw1),
                                        ALU.mult, ALU.add),
       ("gw2", "gw1"), ("gw3",))
    vg(lambda e: e.tensor_reduce(gcolf(4), d3(gw3), AX.X, ALU.max),
       ("gw3",), ("gcol",))
    vg(lambda e: e.tensor_tensor(d3(gw1), d3(gw3), b3(4), ALU.is_equal),
       ("gw3", "gcol"), ("gw1",))
    vg(lambda e: e.scalar_tensor_tensor(d3(gw3), d3(gw1), -1.0e4, iotaB3,
                                        ALU.mult, ALU.add),
       ("gw1", "iotaB"), ("gw3",))
    vg(lambda e: e.tensor_reduce(gcolf(5), d3(gw3), AX.X, ALU.min),
       ("gw3",), ("gcol",))
    vg(lambda e: e.tensor_tensor(d3(gw3), iota3, b3(5), ALU.is_equal),
       ("iota10", "gcol"), ("gw3",))
    vg(lambda e: e.tensor_scalar(gcolf(6), gcolf(3), 7.5, None, ALU.is_lt),
       ("gcol",), ("gcol",))
    vg(lambda e: e.tensor_tensor(gcolf(6), gcolf(6), gcolf(2), ALU.mult),
       ("gcol",), ("gcol",))
    vg(lambda e: e.tensor_scalar(gcolf(7), gcolf(5), 7.5, None, ALU.is_lt),
       ("gcol",), ("gcol",))
    vg(lambda e: e.tensor_tensor(gcolf(7), gcolf(7), gcolf(4), ALU.mult),
       ("gcol",), ("gcol",))
    vg(lambda e: e.tensor_tensor(gcolf(2), gcolf(6), gcolf(7), ALU.add),
       ("gcol",), ("gcol",))
    vg(lambda e: e.tensor_scalar(gcolf(3), gcolf(2), 0.0, None, ALU.is_le),
       ("gcol",), ("gcol",))
    vg(lambda e: e.tensor_tensor(gcolf(2), gcolf(2), gcolf(3), ALU.add),
       ("gcol",), ("gcol",))
    vg(lambda e: e.reciprocal(gcolf(2), gcolf(2)), ("gcol",), ("gcol",))
    vg(lambda e: e.tensor_tensor(gcolf(6), gcolf(6), gcolf(2), ALU.mult),
       ("gcol",), ("gcol",))
    vg(lambda e: e.tensor_tensor(gcolf(7), gcolf(7), gcolf(2), ALU.mult),
       ("gcol",), ("gcol",))
    vg(lambda e: e.tensor_tensor(d3(gw2, n=8), d3(gw2, n=8),
                                 A(gcol, 6 * NTT, [[16 * 8, 128], [1, NTT], [0, 8]]),
                                 ALU.mult),
       ("gw2", "gcol"), ("gw2",))
    vg(lambda e: e.tensor_tensor(d3(gw3, n=8), d3(gw3, n=8),
                                 A(gcol, 7 * NTT, [[16 * 8, 128], [1, NTT], [0, 8]]),
                                 ALU.mult),
       ("gw3", "gcol"), ("gw3",))
    vg(lambda e: e.tensor_tensor(A(cwTok, 0, [[NTT * 8, 128], [8, NTT], [1, 8]]),
                                 d3(gw2, n=8), d3(gw3, n=8), ALU.add),
       ("gw2", "gw3"), ("cwTok",))
    for tt in range(NTT):
        bk = tt % 2
        kb.op(TE, lambda e, tt=tt, bk=bk:
              e.transpose(pb[bk][0:8, 0:128], cwTok[:, tt * 8:(tt + 1) * 8],
                          iden_f[:, :]),
              reads=("cwTok", "iden_f"), writes=(f"pb{bk}",))
        kb.op(V, lambda e, tt=tt, bk=bk:
              e.tensor_copy(cwT[:, tt * 128:(tt + 1) * 128], pb[bk][0:8, 0:128]),
              reads=(f"pb{bk}",), writes=("cwT",))
    st_b1.close()
    st_a.close()
    kb.fence()

    st_b2 = ExitStack()
    ACC = [sb(f"acc{c}", (128, T), F32, st_b2) for c in range(8)]
    cwbA = [sb(f"cwb{e}", (128, T), BF16, st_b2) for e in range(NTRU)]
    mtmp = [sb(f"mtmp{i}", (128, T), F32, st_b2) for i in range(2)]
    We16 = [sb(f"we16_{i}", (128, 4096), BF16, st_b2) for i in range(2)]

    for e_ in range(NTRU):
        for pp in range(NP):
            bk = pp % 2
            kb.op(TE, lambda e, e_=e_, pp=pp, bk=bk:
                  e.matmul(pb[bk][:, 0:512], ohE[:, e_ * 128:(e_ + 1) * 128],
                           cwT[:, pp * 512:(pp + 1) * 512],
                           start=True, stop=True),
                  reads=("ohE", "cwT"), writes=(f"pb{bk}",))
            kb.op(V, lambda e, e_=e_, pp=pp, bk=bk:
                  e.tensor_copy(cwbA[e_][:, pp * 512:(pp + 1) * 512],
                                pb[bk][:, 0:512]),
                  reads=(f"pb{bk}",), writes=(f"cwb{e_}",))

    def ew_load(e_, hlf, par):
        for k in range(8):
            ld(We16[par][:, k * 512:(k + 1) * 512],
               w_d["ew"][e_, k * 128:(k + 1) * 128, hlf * 512:(hlf + 1) * 512],
               cast=True, sem=f"qW{par}", writes=(f"we16_{par}",),
               silent=(k > 0))
        kb.finish_queue(f"qW{par}", [f"we16_{par}"])

    ew_load(0, 0, 0)
    for e_ in range(NTRU):
        for hlf in range(2):
            par = hlf
            nxt_e, nxt_h = (e_, 1) if hlf == 0 else (e_ + 1, 0)
            if nxt_e < NTRU:
                ew_load(nxt_e, nxt_h, 1 - hlf)
            for dj in range(4):
                dout = hlf * 4 + dj
                col = dj * 128
                pbase = (dout % 2) * 4
                for k in range(8):
                    for pp in range(NP):
                        bk = pbase + pp
                        kb.op(TE, lambda e, par=par, k=k, col=col, pp=pp, bk=bk:
                              e.matmul(pb[bk][:, 0:512],
                                       We16[par][:, k * 512 + col:
                                                 k * 512 + col + 128],
                                       XN16[k][:, pp * 512:(pp + 1) * 512],
                                       start=(k == 0), stop=(k == 7)),
                              reads=(f"we16_{par}", f"xn16_{k}"),
                              writes=(f"pb{bk}",), inc=(k == 7))
                mp = dout % 2
                for pp in range(NP):
                    bk = pbase + pp
                    if e_ == 0:
                        kb.op(V, lambda e, e_=e_, dout=dout, pp=pp, bk=bk:
                              e.scalar_tensor_tensor(
                                  ACC[dout][:, pp * 512:(pp + 1) * 512],
                                  pb[bk][:, 0:512],
                                  ebCol[:, e_ * 8 + dout:e_ * 8 + dout + 1],
                                  cwbA[e_][:, pp * 512:(pp + 1) * 512],
                                  ALU.add, ALU.mult),
                              reads=(f"pb{bk}", "ebCol", f"cwb{e_}"),
                              writes=(f"acc{dout}",))
                    else:
                        kb.op(V, lambda e, e_=e_, dout=dout, pp=pp, bk=bk,
                              mp=mp:
                              e.scalar_tensor_tensor(
                                  mtmp[mp][:, pp * 512:(pp + 1) * 512],
                                  pb[bk][:, 0:512],
                                  ebCol[:, e_ * 8 + dout:e_ * 8 + dout + 1],
                                  cwbA[e_][:, pp * 512:(pp + 1) * 512],
                                  ALU.add, ALU.mult),
                              reads=(f"pb{bk}", "ebCol", f"cwb{e_}"),
                              writes=(f"mtmp{mp}",))
                if e_ > 0:
                    kb.op(G, lambda e, dout=dout, mp=mp:
                          e.tensor_tensor(ACC[dout][:, :], ACC[dout][:, :],
                                          mtmp[mp][:, :], ALU.add),
                          reads=(f"acc{dout}", f"mtmp{mp}"),
                          writes=(f"acc{dout}",))
    for c in range(8):
        ld(acc_d[c * 128:(c + 1) * 128, :], ACC[c][:, :], sem="qA0",
           reads=(f"acc{c}",), writes=(f"accp{c}",))
    st_b2.close()
    st_cw.close()
    kb.fence()

    st_c_k = ExitStack()
    K16 = [sb(f"k16_{c}", (128, T), BF16, st_c_k) for c in range(8)]
    st_c_bc = ExitStack()
    Y16p = [sb(f"y16p_{c}", (128, T), BF16, st_c_bc) for c in range(8)]
    Q16 = [sb(f"q16_{c}", (128, T), BF16, st_c_bc) for c in range(8)]
    st_c_vt = ExitStack()
    VtL = [sb(f"vtl{t}", (128, H * 65), BF16, st_c_vt) for t in range(NTT)]
    st_c0 = ExitStack()
    sqP = [sb(f"sq{i}", (128, T), BF16, st_c0) for i in range(2)]
    Vf32 = sb("vf32", (128, T), F32, st_c0)
    rowscrP = sb("rowscrP", (2, 512), F32, st_c0)
    rowb16 = [sb(f"rowb16_{i}", (2, T), BF16, st_c0) for i in range(2)]
    Wst = [sb(f"wst{i}", (128, 1024), BF16, st_c0) for i in range(2)]

    XN2 = Y16p

    def wload(par, dram, r0, c0, c1, key, sem=None):
        for k in range(8):
            ld(Wst[par][:, k * (c1 - c0):(k + 1) * (c1 - c0)],
               dram[r0 + k * 128:r0 + (k + 1) * 128, c0:c1], cast=True,
               sem=sem or f"qW{par}", writes=(key,), silent=(k > 0))
        kb.finish_queue(sem or f"qW{par}", [key])

    wload(0, w_d["Wq"], 0, 0, 128, "wst0")
    wload(1, w_d["Wq"], 0, 128, 256, "wst1")

    for c in range(8):
        kb.op(V, lambda e, c=c: e.tensor_tensor(sqP[c % 2][:, :], XN16[c][:, :],
                                                XN16[c][:, :], ALU.mult),
              reads=(f"xn16_{c}",), writes=(f"sq{c % 2}",))
        for pp in range(NP):
            bk = 4 + pp
            kb.op(TE, lambda e, c=c, pp=pp, bk=bk:
                  e.matmul(pb[bk][0:1, 0:512], onesb[:, 0:1],
                           sqP[c % 2][:, pp * 512:(pp + 1) * 512],
                           start=(c == 0), stop=(c == 7)),
                  reads=(f"sq{c % 2}", "onesb"), writes=(f"pb{bk}",), inc=True)
    for pp in range(NP):
        kb.op(S, lambda e, pp=pp:
              e.activation(rowscrP[0:1, :], pb[4 + pp][0:1, 0:512], AF.Ln,
                           bias=epsc[0:1, 1:2], scale=1.0 / D),
              reads=(f"pb{4 + pp}", "epsc"), writes=("rowscrP",))
        kb.op(S, lambda e, pp=pp:
              e.activation(rowb16[0][0:1, pp * 512:(pp + 1) * 512],
                           rowscrP[0:1, :], AF.Exp, scale=-0.5),
              reads=("rowscrP",), writes=("rowb16_0",))
    for c in range(8):
        for pp in range(NP):
            bk = 4 + pp % 2
            kb.op(TE, lambda e, c=c, pp=pp, bk=bk:
                  e.matmul(pb[bk][:, 0:512], ln1b[:, c * 128:(c + 1) * 128],
                           rowb16[0][0:1, pp * 512:(pp + 1) * 512],
                           start=True, stop=True),
                  reads=("ln1b", "rowb16_0"), writes=(f"pb{bk}",))
            kb.op(V, lambda e, c=c, pp=pp, bk=bk:
                  e.tensor_tensor(XN2[c][:, pp * 512:(pp + 1) * 512],
                                  XN16[c][:, pp * 512:(pp + 1) * 512],
                                  pb[bk][:, 0:512], ALU.mult),
                  reads=(f"xn16_{c}", f"pb{bk}"), writes=(f"y16p_{c}",))
        pbase0 = 0 if NP > 2 else 0
        for pp in range(NP):
            kb.op(TE, lambda e, c=c, pp=pp:
                  e.matmul(pb[pp][:, 0:512], Wst[0][:, c * 128:(c + 1) * 128],
                           XN2[c][:, pp * 512:(pp + 1) * 512],
                           start=(c == 0), stop=(c == 7)),
                  reads=("wst0", f"y16p_{c}"), writes=(f"pb{pp}",),
                  inc=(c == 7))
    for pp in range(NP):
        kb.op(V, lambda e, pp=pp:
              e.tensor_copy(Q16[0][:, pp * 512:(pp + 1) * 512],
                            pb[pp][:, 0:512]),
              reads=(f"pb{pp}",), writes=("q16_0",))

    def proj_unit(wname, dst, dkey, dout, nxt):
        def go():
            par = dout % 2
            if nxt is not None:
                wload(1 - par, w_d[nxt[0]], 0, nxt[1] * 128,
                      (nxt[1] + 1) * 128, f"wst{1 - par}")
            pbase = (dout % 2) * 2 if NP <= 2 else 0
            for k in range(8):
                for pp in range(NP):
                    bk = pbase + pp if NP <= 2 else pp
                    kb.op(TE, lambda e, par=par, k=k, pp=pp, bk=bk:
                          e.matmul(pb[bk][:, 0:512],
                                   Wst[par][:, k * 128:(k + 1) * 128],
                                   XN2[k][:, pp * 512:(pp + 1) * 512],
                                   start=(k == 0), stop=(k == 7)),
                          reads=(f"wst{par}", f"y16p_{k}"),
                          writes=(f"pb{bk}",), inc=(k == 7))
            for pp in range(NP):
                bk = (pbase + pp) if NP <= 2 else pp
                if dst is None:
                    kb.op(V, lambda e, pp=pp, bk=bk:
                          e.tensor_copy(Vf32[:, pp * 512:(pp + 1) * 512],
                                        pb[bk][:, 0:512]),
                          reads=(f"pb{bk}",), writes=("vf32",))
                else:
                    kb.op(V, lambda e, dout=dout, pp=pp, bk=bk:
                          e.tensor_copy(dst[dout][:, pp * 512:(pp + 1) * 512],
                                        pb[bk][:, 0:512]),
                          reads=(f"pb{bk}",), writes=(dkey.format(dout),))
        return go

    def qn_sq(XB, keyfmt, c):
        pr = c % 2
        kb.op(V, lambda e, c=c, pr=pr:
              e.tensor_tensor(sqP[pr][:, :], XB[c][:, :], XB[c][:, :],
                              ALU.mult),
              reads=(keyfmt.format(c),), writes=(f"sq{pr}",))

    def qn_stats(XB, keyfmt, c, fold_scale, bank0):
        pr = c % 2
        for pp in range(NP):
            bk = bank0 + pp % 2
            kb.op(TE, lambda e, pr=pr, pp=pp, bk=bk:
                  e.matmul(pb[bk][0:2, 0:512], selq[:, :],
                           sqP[pr][:, pp * 512:(pp + 1) * 512],
                           start=True, stop=True),
                  reads=(f"sq{pr}", "selq"), writes=(f"pb{bk}",))
            kb.op(S, lambda e, bk=bk:
                  e.activation(rowscrP[0:2, :], pb[bk][0:2, 0:512], AF.Ln,
                               bias=epsc[0:2, 2:3]),
                  reads=(f"pb{bk}", "epsc"), writes=("rowscrP",))
            kb.op(S, lambda e, pr=pr, pp=pp:
                  e.activation(rowb16[pr][0:2, pp * 512:(pp + 1) * 512],
                               rowscrP[0:2, :], AF.Exp, scale=-0.5),
                  reads=("rowscrP",), writes=(f"rowb16_{pr}",))
            if fold_scale:
                kb.op(V, lambda e, c=c, pr=pr, pp=pp:
                      e.tensor_scalar(rowb16[pr][0:2, pp * 512:(pp + 1) * 512],
                                      rowb16[pr][0:2, pp * 512:(pp + 1) * 512],
                                      scaleP[:, c:c + 1], None, ALU.mult),
                      reads=(f"rowb16_{pr}", "scaleP"),
                      writes=(f"rowb16_{pr}",))

    def qn_apply_piece(XB, keyfmt, c, pp, bk):
        pr = c % 2
        kb.op(TE, lambda e, pr=pr, pp=pp, bk=bk:
              e.matmul(pb[bk][:, 0:512], sel2T[:, :],
                       rowb16[pr][0:2, pp * 512:(pp + 1) * 512],
                       start=True, stop=True),
              reads=(f"rowb16_{pr}", "sel2T"), writes=(f"pb{bk}",))
        kb.op(V, lambda e, c=c, pp=pp, bk=bk:
              e.tensor_tensor(XB[c][:, pp * 512:(pp + 1) * 512],
                              XB[c][:, pp * 512:(pp + 1) * 512],
                              pb[bk][:, 0:512], ALU.mult),
              reads=(keyfmt.format(c), f"pb{bk}"), writes=(keyfmt.format(c),))

    for d in range(1, 8):
        nxt = ("Wq", d + 1) if d < 7 else ("Wk", 0)
        proj_unit("Wq", Q16, "q16_{}", d, nxt)()
    for d in range(8):
        nxt = ("Wk", d + 1) if d < 7 else ("Wv", 0)
        qn_sq(Q16, "q16_{}", d)
        proj_unit("Wk", K16, "k16_{}", d, nxt)()
        qn_stats(Q16, "q16_{}", d, True, 4)
        if d > 0:
            for pp in range(NP):
                qn_apply_piece(Q16, "q16_{}", d - 1, pp, 6 + pp % 2)
    for pp in range(NP):
        qn_apply_piece(Q16, "q16_{}", 7, pp, 6 + pp % 2)

    def v_transpose(c, tt):
        reg = (tt % 4) * 128
        kb.op(TE, lambda e, tt=tt, reg=reg:
              e.transpose(pb[4][:, reg:reg + 128],
                          Vf32[:, tt * 128:(tt + 1) * 128], iden_f[:, :]),
              reads=("vf32", "iden_f"), writes=("pb4",))
        for hf, eng in ((0, V), (1, V)):
            hh = 2 * c + hf
            kb.op(eng, lambda e, tt=tt, reg=reg, hf=hf, hh=hh:
                  e.tensor_copy(VtL[tt][:, hh * 65:hh * 65 + 64],
                                pb[4][:, reg + hf * 64:reg + hf * 64 + 64]),
                  reads=("pb4",), writes=(f"vtl{tt}",))

    for d in range(8):
        par = d % 2
        qn_sq(K16, "k16_{}", d)
        if d < 7:
            wload(1 - par, w_d["Wv"], 0, (d + 1) * 128, (d + 2) * 128,
                  f"wst{1 - par}")
        pbase = (d % 2) * 2 if NP <= 2 else 0
        for k in range(8):
            for pp in range(NP):
                bk = pbase + pp if NP <= 2 else pp
                kb.op(TE, lambda e, par=par, k=k, pp=pp, bk=bk:
                      e.matmul(pb[bk][:, 0:512],
                               Wst[par][:, k * 128:(k + 1) * 128],
                               XN2[k][:, pp * 512:(pp + 1) * 512],
                               start=(k == 0), stop=(k == 7)),
                      reads=(f"wst{par}", f"y16p_{k}"),
                      writes=(f"pb{bk}",), inc=(k == 7))
            if d > 0 and 2 * k < NTT:
                v_transpose(d - 1, 2 * k)
                v_transpose(d - 1, 2 * k + 1)
            if d > 0 and k % 2 == 1 and (k - 1) // 2 < NP:
                qn_apply_piece(K16, "k16_{}", d - 1, (k - 1) // 2, 7)
        for pp in range(NP):
            bk = pbase + pp if NP <= 2 else pp
            kb.op(V, lambda e, pp=pp, bk=bk:
                  e.tensor_copy(Vf32[:, pp * 512:(pp + 1) * 512],
                                pb[bk][:, 0:512]),
                  reads=(f"pb{bk}",), writes=("vf32",))
        qn_stats(K16, "k16_{}", d, False, 5)
    for tt in range(NTT):
        v_transpose(7, tt)
    for pp in range(NP):
        qn_apply_piece(K16, "k16_{}", 7, pp, 7)
    for tt in range(NTT):
        kb.op(V, lambda e, tt=tt:
              e.memset(A(VtL[tt], 64, [[H * 65, 128], [65, H]]), 1.0),
              writes=(f"vtl{tt}",))
    st_c0.close()
    kb.fence()

    st_ps_a.close()
    st_ps_c1 = ExitStack()
    s16 = [st_ps_c1.enter_context(nc.psum_tensor(f"s16_{i}", [128, 512], F32))
           for i in range(3)]
    pbc = [st_ps_c1.enter_context(nc.psum_tensor("pbc0", [128, 512], F32))]
    st_c1 = ExitStack()
    Tst = [sb(f"tst{i}", (128, T), BF16, st_c1) for i in range(2)]
    Pt = [sb(f"pt{i}", (128, T), BF16, st_c1) for i in range(4)]
    denr = sb("denr", (1, 512), F32, st_c1)
    denb = sb("denb", (1, 1024), BF16, st_c1)
    BC16 = sb("bc16", (64, 1024), BF16, st_c1)
    O16t = sb("o16t", (64, T), BF16, st_c1)

    ld(Tst[0][:, :], A(scr_d, 0, [[1, 128], [1, T]]), cast=True, sem="qT0",
       reads=("cscr",), writes=("tst0",))

    NB = min(4, NP)
    vsum = sb("vsum", (1, H * 65), F32, st_c1)
    vsumT = sb("vsumT", (65, H), F32, st_c1)
    for g0, g1 in ((0, 512), (512, 1024), (1024, H * 65)):
        for tt in range(NTT):
            kb.op(TE, lambda e, g0=g0, g1=g1, tt=tt:
                  e.matmul(pbc[0][0:1, 0:g1 - g0], onesb[:, 0:1],
                           VtL[tt][:, g0:g1], start=(tt == 0),
                           stop=(tt == NTT - 1)),
                  reads=(f"vtl{tt}", "onesb"), writes=("pbc0",),
                  inc=(tt == NTT - 1))
        kb.op(V, lambda e, g0=g0, g1=g1:
              e.tensor_copy(vsum[:, g0:g1], pbc[0][0:1, 0:g1 - g0]),
              reads=("pbc0",), writes=("vsum",))
    for h in range(H):
        kb.op(TE, lambda e, h=h:
              e.transpose(pbc[0][0:65, h:h + 1], vsum[0:1, h * 65:(h + 1) * 65],
                          iden_f[0:1, 0:1]),
              reads=("vsum", "iden_f"), writes=("pbc0",), inc=(h == H - 1))
    kb.op(V, lambda e: e.tensor_copy(vsumT[:, :], pbc[0][0:65, 0:H]),
          reads=("pbc0",), writes=("vsumT",))

    def mk_post_bank(h, b):
        c, hf = h // 2, h % 2

        def post():
            kb.op(TE, lambda e, b=b:
                  e.matmul(pbc[0][0:64, 0:512], onesbr[:, :],
                           denb[:, (b % 2) * 512:(b % 2) * 512 + 512],
                           start=True, stop=True),
                  reads=("denb", "onesbr"), writes=("pbc0",))
            kb.op(V, lambda e, b=b:
                  e.tensor_copy(BC16[:, (b % 2) * 512:(b % 2) * 512 + 512],
                                pbc[0][0:64, 0:512]),
                  reads=("pbc0",), writes=("bc16",))
            dst = (lambda sl: Y16p[c][0:64, sl]) if hf == 0 else                   (lambda sl: O16t[:, sl])
            dkey = f"y16p_{c}" if hf == 0 else "o16t"
            kb.op(V, lambda e, b=b, dst=dst:
                  e.tensor_tensor(dst(slice(b * 512, (b + 1) * 512)),
                                  pb[b][0:64, 0:512],
                                  BC16[:, (b % 2) * 512:(b % 2) * 512 + 512],
                                  ALU.mult),
                  reads=(f"pb{b}", "bc16"), writes=(dkey,))
            if b == 0:
                kb.op(V, lambda e, h=h, dst=dst:
                      e.tensor_scalar(dst(slice(0, 1)), vsumT[0:64, h:h + 1],
                                      1.0 / T, None, ALU.mult),
                      reads=("vsumT",), writes=(dkey,))
            if b == NB - 1 and hf == 1:
                kb.op(G, lambda e, c=c:
                      e.dma_start(Y16p[c][64:128, :], O16t[:, :]),
                      reads=("o16t",), writes=(f"y16p_{c}",), dma_sem="qO")
        return post

    fifo = []
    postq = []
    cct = [0]

    def pop_one():
        h, b, q, w, j0, kt, pslot, ptk, lb = fifo.pop(0)
        if kt == 0:
            keep = []
            for ent in postq:
                if ent[2] == b:
                    ent[0]()
                else:
                    keep.append(ent)
            postq[:] = keep
        kb.op(TE, lambda e, h=h, b=b, q=q, w=w, j0=j0, kt=kt, pslot=pslot,
              lb=lb:
              e.matmul(pb[b][0:65, q % 512:q % 512 + w],
                       VtL[kt][:, h * 65:(h + 1) * 65],
                       Pt[pslot][:, j0:j0 + w],
                       start=(kt == 0), stop=lb),
              reads=(f"vtl{kt}", ptk), writes=(f"pb{b}",), inc=lb)
        if lb:
            kb.op(S, lambda e, b=b:
                  e.activation(denr[:, :], pb[b][64:65, 0:512], AF.Ln,
                               bias=epsc[0:1, 3:4]),
                  reads=(f"pb{b}", "epsc"), writes=("denr",))
            kb.op(S, lambda e, b=b:
                  e.activation(denb[:, (b % 2) * 512:(b % 2) * 512 + 512],
                               denr[:, :], AF.Exp, scale=-1.0),
                  reads=("denr",), writes=("denb",))
            postq.append((mk_post_bank(h, b), cct[0], b))

    def tflip(par, pp):
        kb.op(TE, lambda e, par=par, pp=pp:
              e.matmul(pbc[0][:, 0:512], Jmat[:, :],
                       Tst[par][:, pp * 512:(pp + 1) * 512],
                       start=True, stop=True),
              reads=("Jmat", f"tst{par}"), writes=("pbc0",))
        kb.op(V, lambda e, par=par, pp=pp:
              e.tensor_copy(Tst[par][:, pp * 512:(pp + 1) * 512],
                            pbc[0][:, 0:512]),
              reads=("pbc0",), writes=(f"tst{par}",))

    for pp in range(NP):
        tflip(0, pp)
    for h in range(H):
        par = h % 2
        c, hf = h // 2, h % 2
        po = 64 * hf
        if h + 1 < H:
            ld(Tst[1 - par][:, :], A(scr_d, (h + 1) * CP, [[1, 128], [1, T]]),
               cast=True, sem=f"qT{1 - par}", reads=("cscr",),
               writes=(f"tst{1 - par}",))
        for kt in range(NTT):
            if NTT // 2 <= kt < NTT // 2 + NP and h + 1 < H:
                tflip(1 - par, kt - NTT // 2)
            ktg = h * NTT + kt
            while fifo and fifo[0][5] + (fifo[0][0] - h) * NTT <= kt - 3:
                pop_one()
            q0 = kt * 128
            cs = q0
            ci = 0
            while cs < T:
                while postq and postq[0][1] <= cct[0] - 2:
                    postq.pop(0)[0]()
                ce = min((cs // 512 + 1) * 512, T)
                w2 = ce - cs
                sbk = cct[0] % 3
                pslot = ktg % 4
                ptk = f"pt{pslot}_c{ci}"
                jc = cs - q0
                kb.op(TE, lambda e, sbk=sbk, c=c, po=po, kt=kt, q=cs, w=w2:
                      e.matmul(s16[sbk][:, 0:w],
                               K16[c][po:po + 64, kt * 128:(kt + 1) * 128],
                               Q16[c][po:po + 64, q:q + w],
                               start=True, stop=True),
                      reads=(f"k16_{c}", f"q16_{c}"),
                      writes=(f"s16_{sbk}",), inc=True)
                kb.op(S, lambda e, sbk=sbk, pslot=pslot, jc=jc, w2=w2:
                      e.activation(Pt[pslot][:, jc:jc + w2], s16[sbk][:, 0:w2],
                                   AF.Exp),
                      reads=(f"s16_{sbk}",), writes=(ptk,))
                kb.op(V, lambda e, par=par, pslot=pslot, jc=jc, w2=w2:
                      e.tensor_tensor(Pt[pslot][:, jc:jc + w2],
                                      Pt[pslot][:, jc:jc + w2],
                                      Tst[par][:, jc:jc + w2], ALU.mult),
                      reads=(ptk, f"tst{par}"), writes=(ptk,))
                bb = cs // 512
                lb = (kt == min(4 * bb + 3, NTT - 1))
                fifo.append((h, bb, cs, w2, jc, kt, pslot, ptk, lb))
                cct[0] += 1
                while len(fifo) > 24:
                    pop_one()
                cs = ce
                ci += 1
    while fifo:
        pop_one()
    while postq:
        postq.pop(0)[0]()
    st_c1.close()
    st_c_vt.close()
    st_ps_c1.close()
    for i in (4, 5, 6, 7):
        pb[i] = ES.enter_context(nc.psum_tensor(f"pb{i}b", [128, 512], F32))
    kb.fence()

    st_c3 = ExitStack()
    NPH = max(1, NP // 2)

    st_c2 = ExitStack()
    accw = [sb(f"accw{i}", (128, T), F32, st_c2) for i in range(2)]
    rows2P = sb("rows2P", (1, 512), F32, st_c2)
    rowb2 = sb("rowb2", (1, T), BF16, st_c2)
    sq2 = [sb(f"sq2_{i}", (128, T), BF16, st_c2) for i in range(2)]
    Wst2 = [sb(f"ws2_{i}", (128, 2048), BF16, st_c2) for i in range(2)]

    def wload2(par, dram, r0, c0, c1, key):
        for k in range(8):
            ld(Wst2[par][:, k * (c1 - c0):(k + 1) * (c1 - c0)],
               dram[r0 + k * 128:r0 + (k + 1) * 128, c0:c1], cast=True,
               sem=f"qW{par}", writes=(key,), silent=(k > 0))
        kb.finish_queue(f"qW{par}", [key])

    Y16o = Q16
    wload2(0, w_d["Wo"], 0, 0, 256, "ws2_0")
    for qr in range(4):
        par = qr % 2
        if qr + 1 < 4:
            wload2(1 - par, w_d["Wo"], 0, (qr + 1) * 256, (qr + 2) * 256,
                   f"ws2_{1 - par}")
        for dj in range(2):
            dout = qr * 2 + dj
            pbase = (dout % 2) * 4
            apar = dout % 2
            ld(accw[apar][:, :], acc_d[dout * 128:(dout + 1) * 128, :],
               sem=f"qA{apar}", reads=(f"accp{dout}",), writes=(f"accw{apar}",))
            for k in range(8):
                for pp in range(NP):
                    kb.op(TE, lambda e, par=par, k=k, dj=dj, pp=pp, pbase=pbase:
                          e.matmul(pb[pbase + pp][:, 0:512],
                                   Wst2[par][:, k * 256 + dj * 128:
                                            k * 256 + dj * 128 + 128],
                                   Y16p[k][:, pp * 512:(pp + 1) * 512],
                                   start=(k == 0), stop=(k == 7)),
                          reads=(f"ws2_{par}", f"y16p_{k}"),
                          writes=(f"pb{pbase + pp}",), inc=(k == 7))
            for pp in range(NP):
                kb.op(V, lambda e, dout=dout, pp=pp, pbase=pbase:
                      e.tensor_tensor(Y16o[dout][:, pp * 512:(pp + 1) * 512],
                                      pb[pbase + pp][:, 0:512],
                                      XN16[dout][:, pp * 512:(pp + 1) * 512],
                                      ALU.add),
                      reads=(f"pb{pbase + pp}", f"xn16_{dout}"),
                      writes=(f"q16_{dout}",))
                kb.op(V, lambda e, apar=apar, pp=pp, pbase=pbase:
                      e.tensor_tensor(accw[apar][:, pp * 512:(pp + 1) * 512],
                                      accw[apar][:, pp * 512:(pp + 1) * 512],
                                      pb[pbase + pp][:, 0:512], ALU.add),
                      reads=(f"pb{pbase + pp}", f"accw{apar}"),
                      writes=(f"accw{apar}",))
            kb.op(G, lambda e, apar=apar, dout=dout:
                  e.tensor_tensor(accw[apar][:, :], accw[apar][:, :],
                                  XN16[dout][:, :], ALU.add),
                  reads=(f"xn16_{dout}", f"accw{apar}"), writes=(f"accw{apar}",))
            ld(acc_d[dout * 128:(dout + 1) * 128, :], accw[apar][:, :],
               sem=f"qA{apar}", reads=(f"accw{apar}",), writes=(f"accp{dout}",))
    for c in range(8):
        kb.op(V, lambda e, c=c: e.tensor_tensor(sq2[c % 2][:, :],
                                                Y16o[c][:, :],
                                                Y16o[c][:, :], ALU.mult),
              reads=(f"q16_{c}",), writes=(f"sq2_{c % 2}",))
        for pp in range(NP):
            bk = 4 + pp
            kb.op(TE, lambda e, pp=pp, bk=bk, c=c:
                  e.matmul(pb[bk][0:1, 0:512], onesb[:, 0:1],
                           sq2[c % 2][:, pp * 512:(pp + 1) * 512],
                           start=(c == 0), stop=(c == 7)),
                  reads=(f"sq2_{c % 2}", "onesb"), writes=(f"pb{bk}",),
                  inc=True)
    for pp in range(NP):
        kb.op(S, lambda e, pp=pp:
              e.activation(rows2P[0:1, :], pb[4 + pp][0:1, 0:512], AF.Ln,
                           bias=epsc[0:1, 1:2], scale=1.0 / D),
              reads=(f"pb{4 + pp}", "epsc"), writes=("rows2P",))
        kb.op(S, lambda e, pp=pp:
              e.activation(rowb2[0:1, pp * 512:(pp + 1) * 512],
                           rows2P[0:1, :], AF.Exp, scale=-0.5),
              reads=("rows2P",), writes=("rowb2",))
    YN16 = K16
    for c in range(8):
        for pp in range(NP):
            bk = pp % 2
            kb.op(TE, lambda e, c=c, pp=pp, bk=bk:
                  e.matmul(pb[bk][:, 0:512], ln2b[:, c * 128:(c + 1) * 128],
                           rowb2[0:1, pp * 512:(pp + 1) * 512],
                           start=True, stop=True),
                  reads=("ln2b", "rowb2"), writes=(f"pb{bk}",))
            kb.op(V, lambda e, c=c, pp=pp, bk=bk:
                  e.tensor_tensor(YN16[c][:, pp * 512:(pp + 1) * 512],
                                  Y16o[c][:, pp * 512:(pp + 1) * 512],
                                  pb[bk][:, 0:512], ALU.mult),
                  reads=(f"q16_{c}", f"pb{bk}"), writes=(f"k16_{c}",))
    st_c2.close()
    st_c_bc.close()
    kb.fence()

    H16 = [sb(f"h16_{i}", (128, T // 2), BF16, st_c3) for i in range(24)] + \
          [XN16[i] for i in range(8)]
    W2R = sb("w2r", (128, 32 * D), BF16, st_c3)
    Wst1 = [sb(f"ws1_{i}", (128, 2048), BF16, st_c3) for i in range(2)]
    accw2 = [sb(f"acw2_{i}", (128, T // 2), F32, st_c3) for i in range(2)]

    def w1load(par, e16):
        for k in range(8):
            ld(Wst1[par][:, k * 256:(k + 1) * 256],
               w_d["W1"][k * 128:(k + 1) * 128, e16 * 256:(e16 + 1) * 256],
               cast=True, sem=f"qW{par}", writes=(f"ws1_{par}",),
               silent=(k > 0))
        kb.finish_queue(f"qW{par}", [f"ws1_{par}"])

    def w2rload(h0, h1):
        for hk in range(h0, h1):
            ld(W2R[:, hk * D:(hk + 1) * D],
               w_d["W2"][hk * 128:(hk + 1) * 128, :],
               cast=True, sem="qF", silent=(hk % 4 != 0))
            if hk % 4 == 3 or hk == h1 - 1:
                kb.finish_queue("qF", [f"w2r_{k}" for k in
                                       range(hk - hk % 4, hk + 1)])
    w1load(0, 0)
    w2rload(0, 4)
    w1load(1, 1)

    def w1_emit(tp2, e8):
        t0c = tp2 * (T // 2)
        par = e8 % 2
        if e8 + 1 < 16:
            w1load(1 - par, e8 + 1)
        for hl in range(2):
            hk = e8 * 2 + hl
            pbase = (hk % 2) * 2
            for k in range(8):
                for pc in range(NPH):
                    kb.op(TE, lambda e, par=par, k=k, hl=hl, pc=pc,
                          pbase=pbase, t0c=t0c:
                          e.matmul(pb[pbase + pc][:, 0:512],
                                   Wst1[par][:, k * 256 + hl * 128:
                                            k * 256 + hl * 128 + 128],
                                   YN16[k][:, t0c + pc * 512:
                                           t0c + (pc + 1) * 512],
                                   start=(k == 0), stop=(k == 7)),
                          reads=(f"ws1_{par}", f"k16_{k}"),
                          writes=(f"pb{pbase + pc}",), inc=(k == 7))
            for pc in range(NPH):
                kb.op(S, lambda e, hk=hk, pc=pc, pbase=pbase:
                      e.activation(H16[hk][:, pc * 512:(pc + 1) * 512],
                                   pb[pbase + pc][:, 0:512], AF.Relu),
                      reads=(f"pb{pbase + pc}",), writes=(f"h16_{hk}",))

    def w2_emit(dp, hk):
        for j, dout in enumerate((2 * dp, 2 * dp + 1)):
            for pc in range(NPH):
                bk = 4 + 2 * j + pc if NPH == 2 else 4 + j
                kb.op(TE, lambda e, dout=dout, pc=pc, bk=bk, hk=hk:
                      e.matmul(pb[bk][:, 0:512],
                               W2R[:, hk * D + dout * 128:
                                    hk * D + (dout + 1) * 128],
                               H16[hk][:, pc * 512:(pc + 1) * 512],
                               start=(hk == 0), stop=(hk == 31)),
                      reads=(f"w2r_{hk}", f"h16_{hk}"),
                      writes=(f"pb{bk}",),
                      inc=(j == 1 and pc == NPH - 1))

    def w2_evac_pre(tp2, dp):
        t0c = tp2 * (T // 2)
        for j, dout in enumerate((2 * dp, 2 * dp + 1)):
            ld(accw2[j][:, :], acc_d[dout * 128:(dout + 1) * 128,
                                     t0c:t0c + T // 2],
               sem=f"qA{j}", reads=(f"accp{dout}",), writes=(f"acw2_{j}",))

    def w2_evac(tp2, dp):
        t0c = tp2 * (T // 2)
        for j, dout in enumerate((2 * dp, 2 * dp + 1)):
            for pc in range(NPH):
                bk = 4 + 2 * j + pc if NPH == 2 else 4 + j
                kb.op(V, lambda e, j=j, pc=pc, bk=bk:
                      e.tensor_tensor(accw2[j][:, pc * 512:(pc + 1) * 512],
                                      accw2[j][:, pc * 512:(pc + 1) * 512],
                                      pb[bk][:, 0:512], ALU.add),
                      reads=(f"pb{bk}", f"acw2_{j}"), writes=(f"acw2_{j}",))
            ld(acc_d[dout * 128:(dout + 1) * 128, t0c:t0c + T // 2],
               accw2[j][:, :], sem=f"qA{j}", reads=(f"acw2_{j}",),
               writes=(f"accp{dout}_{tp2}",))

    for tp2 in range(2):
        if tp2 == 1:
            w1load(0, 0)
        w2_evac_pre(tp2, 0)
        w2lag = 0
        for e8 in range(16):
            if tp2 == 0:
                lo, hi = 4 + 2 * e8, min(6 + 2 * e8, 32)
                if lo < 32:
                    w2rload(lo, hi)
            w1_emit(tp2, e8)
            while w2lag <= 2 * e8 - 2:
                w2_emit(0, w2lag)
                w2lag += 1
        while w2lag < 32:
            w2_emit(0, w2lag)
            w2lag += 1
        w2_evac(tp2, 0)
        for dp in range(1, 4):
            w2_evac_pre(tp2, dp)
            for hk in range(32):
                w2_emit(dp, hk)
            w2_evac(tp2, dp)
    st_c3.close()
    st_c_k.close()
    kb.fence()

    st_d = ExitStack()
    ND = 8
    xo = [sb(f"xo{i}", (128, D), F32, st_d) for i in range(ND)]
    ot = [sb(f"ot{i}", (128, D), F32, st_d) for i in range(ND)]
    asl = [sb(f"asl{i}", (128, D), F32, st_d) for i in range(ND)]
    for tt in range(NTT):
        p = tt % ND
        hv = 0 if tt * 128 < T // 2 else 1
        ld(xo[p][:, :], x_d[tt * 128:(tt + 1) * 128, :], sem=f"qX{p % 4}",
           writes=(f"xo{p}",))
        kb.finish_queue(f"qX{p % 4}", [f"xo{p}"])
        src = A(acc_d, tt * 128,
                [[T, 128], [128 * T, 8], [1, 128]])
        dst = A(asl[p], 0, [[D, 128], [128, 8], [1, 128]])
        kb.op("sync", lambda e, d=dst, s=src: e.dma_start(d, s),
              reads=tuple(f"accp{c}_{hv}" for c in range(8)),
              writes=(f"asl{p}",), dma_sem=f"qD{p % 4}")
        kb.finish_queue(f"qD{p % 4}", [f"asl{p}"])
        for c in range(8):
            bk = 6 + (c % 2)
            sl = ((c // 2) % 2) * 128
            kb.op(TE, lambda e, p=p, c=c, bk=bk, sl=sl:
                  e.transpose(pb[bk][:, sl:sl + 128],
                              asl[p][:, c * 128:(c + 1) * 128], iden_f[:, :]),
                  reads=(f"asl{p}", "iden_f"), writes=(f"pb{bk}",))
            kb.op(V, lambda e, p=p, c=c, bk=bk, sl=sl:
                  e.tensor_tensor(ot[p][:, c * 128:(c + 1) * 128],
                                  pb[bk][:, sl:sl + 128],
                                  xo[p][:, c * 128:(c + 1) * 128], ALU.add),
                  reads=(f"pb{bk}", f"xo{p}"), writes=(f"ot{p}",))
        kb.op("sync", lambda e, tt=tt, p=p:
              e.dma_start(out_d[tt * 128:(tt + 1) * 128, :], ot[p][:, :]),
              reads=(f"ot{p}",), writes=("outd",), dma_sem="qOut")
    st_d.close()
    kb.fence()
    kb.wait("sync", "qOut", kb.cnt["qOut"])

    kb.emit()
    kb.stack.close()
    return nc


def kernel(**inputs):
    x = np.ascontiguousarray(np.asarray(inputs["L0_attn_x"], dtype=np.float32))
    T = x.shape[1]
    base = {
        "ln1": inputs["L0_attn_ln1_w"], "Wq": inputs["L0_attn_Wq"],
        "Wk": inputs["L0_attn_Wk"], "Wv": inputs["L0_attn_Wv"],
        "relb": inputs["L0_attn_rel_bias"], "scale": inputs["L0_attn_scale"],
        "Wo": inputs["L0_attn_Wo"], "ln2": inputs["L0_attn_ln2_w"],
        "W1": inputs["L0_attn_W1"], "W2": inputs["L0_attn_W2"],
        "gw": inputs["L0_ffn_gate_w"], "gb": inputs["L0_ffn_gate_b"],
        "ew": inputs["L0_ffn_expert_w"], "eb": inputs["L0_ffn_expert_b"],
    }
    base = {k: np.ascontiguousarray(np.asarray(v, dtype=np.float32))
            for k, v in base.items()}
    base["E"] = _bucket_onehot(T)
    nc = build_nc(T=T)
    in_maps = [dict(base, x=np.ascontiguousarray(x[i]))
               for i in range(x.shape[0])]
    res = run_bass_kernel_spmd(nc, in_maps, core_ids=list(range(len(in_maps))))
    out = np.stack([res.results[i]["out"] for i in range(len(in_maps))], axis=0)
    return out.astype(np.float32)


# version 31
# speedup vs baseline: 1.0249x; 1.0106x over previous
import math
import numpy as np
from contextlib import ExitStack

import concourse.bass as bass
import concourse.mybir as mybir
import concourse.bass_utils as _bass_utils
from concourse.bass_utils import run_bass_kernel_spmd

if not getattr(_bass_utils, "_ldw_opt_patched", False):
    _orig_run_command = _bass_utils.run_command

    def _run_command_ldw(cmd, *a, **kw):
        if isinstance(cmd, (list, tuple)):
            cmd = ["--enable-ldw-opt=true" if c == "--enable-ldw-opt=false"
                   else c for c in cmd]
        return _orig_run_command(cmd, *a, **kw)

    _bass_utils.run_command = _run_command_ldw
    _bass_utils._ldw_opt_patched = True

F32 = mybir.dt.float32
BF16 = mybir.dt.bfloat16
AF = mybir.ActivationFunctionType
ALU = mybir.AluOpType
AX = mybir.AxisListType

D = 1024
H = 16
HID = 4096
NEXP = 10
NTRU = 8
LN_EPS = 1e-5
RMS_EPS = 1e-6
KQ_EPS = 1e-6
NEG = -1e30
BIG = 1e9
QK_BOUND = 1.05


def _bucket_onehot(S):
    n = S + 128
    E = np.zeros((32, n), dtype=np.float32)
    i = np.arange(128, n)
    d = (i - 127).astype(np.int64)
    safe = np.maximum(d, 1).astype(np.float32)
    t = np.log(safe / np.float32(16.0))
    t = t / np.float32(math.log(128.0 / 16.0))
    t = t * np.float32(16.0)
    large = np.minimum(16 + t.astype(np.int32), 31)
    bucket = np.where(d < 16, d, large).astype(np.int64)
    E[bucket, i] = 1.0
    return E


class KB:

    ENGINES = ("sync", "gpsimd", "tensor", "vector", "scalar")

    def __init__(self, nc):
        self.nc = nc
        self.plan = {e: [] for e in self.ENGINES}
        self.sems = {}
        self.cnt = {}
        self.waited = {e: {} for e in self.ENGINES}
        self.bufs = {}
        self.pending = {e: [] for e in self.ENGINES}
        self.eng_sem = {"gpsimd": "G", "tensor": "T", "vector": "V",
                        "scalar": "S", "sync": None}
        self.fence_waits = {}
        self.stack = ExitStack()

    def sem(self, name):
        if name not in self.sems:
            self.sems[name] = self.stack.enter_context(self.nc.semaphore(name))
            self.cnt[name] = 0
        return self.sems[name]

    def op(self, eng, fn, reads=(), writes=(), inc=True, dma_sem=None,
           extra_waits=()):
        waits = list(extra_waits)

        def need(w):
            if w is None:
                return
            if w[0] == "PEND":
                if w[1] != eng:
                    raise RuntimeError(f"cross-engine access to pending buf {w}")
                return
            waits.append(w)

        for b in reads:
            ent = self.bufs.get(b)
            if ent:
                need(ent.get("w"))
        for b in writes:
            ent = self.bufs.get(b)
            if ent:
                need(ent.get("w"))
                for r in ent.get("r", ()):
                    need(r)
                for peng in ent.get("pr", ()):
                    if peng != eng:
                        raise RuntimeError(
                            f"write to {b} with pending readers on {peng}")
        ww = {}
        for s, v in list(waits) + list(self.fence_waits.items()):
            ww[s] = max(ww.get(s, 0), v)
        w_em = []
        for s, v in ww.items():
            if self.waited[eng].get(s, -1) >= v:
                continue
            self.waited[eng][s] = v
            self.sem(s)
            w_em.append((s, v))

        if dma_sem is not None:
            self.sem(dma_sem)
            self.cnt[dma_sem] += 16
            eff = (dma_sem, self.cnt[dma_sem])
        elif inc and self.eng_sem[eng]:
            s = self.eng_sem[eng]
            self.sem(s)
            self.cnt[s] += 1
            eff = (s, self.cnt[s])
        else:
            eff = ("PEND", eng)

        if eff[0] != "PEND":
            for key, kind in self.pending[eng]:
                ent = self.bufs.setdefault(key, {"w": None, "r": [], "pr": set()})
                if kind == "w":
                    ent["w"] = eff
                    ent["r"] = []
                else:
                    ent["r"].append(eff)
                ent["pr"].discard(eng)
            self.pending[eng] = []

        for b in reads:
            ent = self.bufs.setdefault(b, {"w": None, "r": [], "pr": set()})
            if eff[0] == "PEND":
                self.pending[eng].append((b, "r"))
                ent["pr"].add(eng)
            else:
                ent["r"].append(eff)
        for b in writes:
            ent = self.bufs.setdefault(b, {"w": None, "r": [], "pr": set()})
            if eff[0] == "PEND":
                ent["w"] = ("PEND", eng)
                self.pending[eng].append((b, "w"))
                ent["pr"].add(eng)
            else:
                ent["w"] = eff
                ent["r"] = []
        self.plan[eng].append((w_em, fn, None if eff[0] == "PEND" else eff,
                               dma_sem is not None))

    def fence(self):
        for eng in self.ENGINES:
            if self.pending[eng]:
                raise RuntimeError(f"fence with pending ops on {eng}")
        self.fence_waits = dict(self.cnt)

    def finish_queue(self, sem, keys):
        for k in keys:
            ent = self.bufs.setdefault(k, {"w": None, "r": [], "pr": set()})
            ent["w"] = (sem, self.cnt[sem])
            ent["r"] = []

    def wait(self, eng, sem, val):
        self.sem(sem)
        self.plan[eng].append(([(sem, val)], None, None, False))

    def emit(self):
        kb = self
        with kb.nc.Block() as block:
            for eng in self.ENGINES:
                steps = self.plan[eng]
                if not steps:
                    continue

                def body(e, steps=steps):
                    for waits, fn, eff, is_dma in steps:
                        for s, v in waits:
                            e.wait_ge(kb.sems[s], v)
                        if fn is None:
                            continue
                        inst = fn(e)
                        if eff is not None:
                            inst.then_inc(kb.sems[eff[0]], 16 if is_dma else 1)
                getattr(block, eng)(body)


def build_nc(T=2048):
    nc = bass.Bass("TRN2", target_bir_lowering=False)
    NTT = T // 128
    NP = T // 512
    CP = T + 128

    x_d = nc.dram_tensor("x", [T, D], F32, kind="ExternalInput")
    w_d = {}
    for nm, shp in [("ln1", [D]), ("Wq", [D, D]), ("Wk", [D, D]), ("Wv", [D, D]),
                    ("relb", [32, H]), ("scale", [H]), ("Wo", [D, D]),
                    ("ln2", [D]), ("W1", [D, HID]), ("W2", [HID, D]),
                    ("gw", [NEXP, D]), ("gb", [NEXP]),
                    ("ew", [NTRU, D, D]), ("eb", [NTRU, D]),
                    ("E", [32, CP])]:
        w_d[nm] = nc.dram_tensor(nm, shp, F32, kind="ExternalInput")
    out_d = nc.dram_tensor("out", [T, D], F32, kind="ExternalOutput")
    scr_d = nc.dram_tensor("cscr", [H, CP], F32, kind="Internal")
    acc_d = nc.dram_tensor("accscr", [8 * 128, T], F32, kind="Internal")

    kb = KB(nc)
    A = bass.AP
    ES = kb.stack

    def sb(name, shape, dt, st=None):
        return (st or ES).enter_context(nc.sbuf_tensor(name, list(shape), dt))

    pb = [None] * 8
    for i in (0, 1, 2, 3):
        pb[i] = ES.enter_context(nc.psum_tensor(f"pb{i}", [128, 512], F32))
    st_ps_a = ExitStack()
    for i in (4, 5, 6, 7):
        pb[i] = st_ps_a.enter_context(nc.psum_tensor(f"pb{i}a", [128, 512], F32))

    iden_f = sb("iden_f", (128, 128), F32)
    Jmat = sb("Jmat", (128, 128), BF16)
    onesb = sb("onesb", (128, 2), BF16)
    onesf = sb("onesf", (1, 64), F32)
    onesbr = sb("onesbr", (1, 64), BF16)
    iota_f = sb("iota_f", (128, 128), F32)
    iota_p = sb("iota_p", (128, 1), F32)
    iota10 = sb("iota10", (128, 12), F32)
    iotaB = sb("iotaB", (128, 12), F32)
    epsc = sb("epsc", (128, 4), F32)
    selq = sb("selq", (128, 2), BF16)
    sel2T = sb("sel2T", (2, 128), BF16)
    relbS = sb("relbS", (32, H), F32)
    relbT = sb("relbT", (H, 32), F32)
    scale16 = sb("scale16", (H, 2), F32)
    scaleP = sb("scaleP", (2, 8), F32)
    gmaxc = sb("gmaxc", (H, 2), F32)
    gbc = sb("gbc", (NEXP, 1), F32)
    ln1b = sb("ln1b", (1, D), BF16)
    ln2b = sb("ln2b", (1, D), BF16)
    ebCol = sb("ebCol", (128, NTRU * 8), F32)
    g1 = sb("g1", (128, 12), F32)
    g2 = sb("g2", (128, 12), F32)
    g3 = sb("g3", (128, 12), F32)
    gs = sb("gs", (128, 16), F32)

    XN16 = [sb(f"xn16_{c}", (128, T), BF16) for c in range(8)]

    MISC = "qM"

    def ld(dst, src, cast=False, sem=MISC, reads=(), writes=(), silent=False,
           slow=False):
        eng = "gpsimd" if cast else "sync"
        ew = () if silent else (((sem, kb.cnt.get(sem, 0)),)
                                if kb.cnt.get(sem, 0) else ())
        kb.op(eng, lambda e, d=dst, s=src, sl=slow:
              e.dma_start(d, s, allow_slow_non_contiguous=sl),
              reads=() if silent else reads, writes=() if silent else writes,
              dma_sem=sem, extra_waits=ew)

    ld(relbS[:, :], w_d["relb"][:, :], writes=("relbS",))
    ld(relbT[:, :], A(w_d["relb"], 0, [[1, H], [H, 32]]), writes=("relbT",), slow=True)
    ld(scale16[:, 0:1], A(w_d["scale"], 0, [[1, H], [1, 1]]), writes=("scale16",))
    ld(scaleP[:, :], A(w_d["scale"], 0, [[1, 2], [2, 8]]), writes=("scaleP",), slow=True)
    ld(gbc[:, :], A(w_d["gb"], 0, [[1, NEXP], [1, 1]]), writes=("gbc",))
    ld(ln1b[:, :], A(w_d["ln1"], 0, [[D, 1], [1, D]]), cast=True, sem="qC",
       writes=("ln1b",))
    ld(ln2b[:, :], A(w_d["ln2"], 0, [[D, 1], [1, D]]), cast=True, sem="qC",
       writes=("ln2b",))
    for e_ in range(NTRU):
        ld(ebCol[:, e_ * 8:(e_ + 1) * 8], A(w_d["eb"], e_ * D, [[1, 128], [128, 8]]),
           writes=("ebCol",), slow=True)

    G, V, S, TE = "gpsimd", "vector", "scalar", "tensor"
    kb.op(G, lambda e: e.iota(iota_f[:, :], [[1, 128]], channel_multiplier=0,
                              allow_small_or_imprecise_dtypes=True),
          writes=("iota_f",))
    kb.op(G, lambda e: e.iota(iota_p[:, :], [[0, 1]], channel_multiplier=1,
                              allow_small_or_imprecise_dtypes=True),
          writes=("iota_p",))
    kb.op(G, lambda e: e.iota(iota10[:, 0:NEXP], [[1, NEXP]], channel_multiplier=0,
                              allow_small_or_imprecise_dtypes=True),
          writes=("iota10",))
    kb.op(V, lambda e: e.tensor_scalar(iotaB[:, 0:NEXP], iota10[:, 0:NEXP],
                                       1.0e4, None, ALU.add),
          reads=("iota10",), writes=("iotaB",))
    kb.op(G, lambda e: e.memset(onesb[:, :], 1.0), writes=("onesb",))
    kb.op(G, lambda e: e.memset(onesf[:, :], 1.0), writes=("onesf",))
    kb.op(G, lambda e: e.memset(onesbr[:, :], 1.0), writes=("onesbr",))
    kb.op(G, lambda e: e.memset(epsc[:, 0:1], LN_EPS), writes=("epsc",))
    kb.op(G, lambda e: e.memset(epsc[:, 1:2], RMS_EPS), writes=("epsc",))
    kb.op(G, lambda e: e.memset(epsc[:, 2:3], KQ_EPS), writes=("epsc",))
    kb.op(G, lambda e: e.memset(epsc[:, 3:4], 1e-38), writes=("epsc",))
    kb.op(V, lambda e: e.tensor_scalar(iden_f[:, :], iota_f[:, :], iota_p[:, :],
                                       None, ALU.is_equal),
          reads=("iota_f", "iota_p"), writes=("iden_f",))
    kb.op(V, lambda e: e.tensor_scalar(Jmat[:, :], iota_f[:, :], iota_p[:, :],
                                       127.0, ALU.add, ALU.is_equal),
          reads=("iota_f", "iota_p"), writes=("Jmat",))
    s2i = sb("s2i", (2, 128), F32)
    sel2F = sb("sel2F", (2, 128), F32)
    ip2 = sb("ip2", (2, 1), F32)
    kb.op(G, lambda e: e.iota(s2i[:, :], [[1, 2], [0, 64]], channel_multiplier=0,
                              allow_small_or_imprecise_dtypes=True),
          writes=("s2i",))
    kb.op(G, lambda e: e.iota(ip2[:, :], [[0, 1]], channel_multiplier=1,
                              allow_small_or_imprecise_dtypes=True),
          writes=("ip2",))
    kb.op(V, lambda e: e.tensor_scalar(sel2T[:, :], s2i[:, :], ip2[:, :], None,
                                       ALU.is_equal),
          reads=("s2i", "ip2"), writes=("sel2T",))
    kb.op(V, lambda e: e.tensor_scalar(sel2F[:, :], s2i[:, :], ip2[:, :], None,
                                       ALU.is_equal),
          reads=("s2i", "ip2"), writes=("sel2F",))
    kb.op(TE, lambda e: e.transpose(pb[0][:, 0:2], sel2F[:, :],
                                    iden_f[0:2, 0:2]),
          reads=("sel2F", "iden_f"), writes=("pb0",))
    kb.op(V, lambda e: e.tensor_copy(selq[:, :], pb[0][:, 0:2]),
          reads=("pb0",), writes=("selq",))

    st_cw = ExitStack()
    cwT = sb("cwT", (NTRU, T), BF16, st_cw)
    ohE = sb("ohE", (NTRU, NTRU * 128), BF16, st_cw)
    st_a = ExitStack()
    XNF = [sb(f"xnf{c}", (128, T), F32, st_a) for c in range(8)]
    st_a2 = ExitStack()
    EsS = sb("EsS", (32, CP), F32, st_a2)
    cpadS = sb("cpadS", (H, CP), F32, st_a2)
    xt = [sb(f"xt{i}", (128, D), F32, st_a2) for i in range(2)]
    xnt = [sb(f"xnt{i}", (128, D), F32, st_a2) for i in range(2)]
    sqt = [sb(f"sqt{i}", (128, D), F32, st_a2) for i in range(2)]
    st_s = sb("st_s", (128, 16), F32, st_a2)

    ld(EsS[:, :], w_d["E"][:, :], writes=("EsS",))
    kb.finish_queue(MISC, ["relbS", "relbT", "scale16", "scaleP", "gbc",
                           "ebCol", "EsS"])
    kb.finish_queue("qC", ["ln1b", "ln2b"])
    kb.op(S, lambda e: e.activation(scale16[:, 1:2], scale16[:, 0:1], AF.Abs),
          reads=("scale16",), writes=("scale16",))
    kb.op(V, lambda e: e.reduce_max(gmaxc[:, 0:1], relbT[:, :], axis=AX.X,
                                    apply_absolute_value=True),
          reads=("relbT",), writes=("gmaxc",))
    kb.op(V, lambda e: e.tensor_scalar(gmaxc[:, 0:1], gmaxc[:, 0:1], QK_BOUND,
                                       None, ALU.add),
          reads=("gmaxc",), writes=("gmaxc",))
    kb.op(V, lambda e: e.tensor_tensor(gmaxc[:, 1:2], gmaxc[:, 0:1],
                                       scale16[:, 1:2], ALU.mult),
          reads=("gmaxc", "scale16"), writes=("gmaxc",))
    ofs = 0
    while ofs < CP:
        w = min(512, CP - ofs)
        bk = (ofs // 512) % 2
        kb.op(TE, lambda e, o=ofs, w=w, bk=bk:
              e.matmul(pb[bk][0:H, 0:w], relbS[:, :], EsS[:, o:o + w],
                       start=True, stop=True),
              reads=("relbS", "EsS"), writes=(f"pb{bk}",))
        kb.op(V, lambda e, o=ofs, w=w, bk=bk:
              e.tensor_scalar(cpadS[:, o:o + w], pb[bk][0:H, 0:w],
                              scale16[:, 0:1], gmaxc[:, 1:2],
                              ALU.mult, ALU.subtract),
              reads=(f"pb{bk}", "scale16", "gmaxc"), writes=("cpadS",))
        ofs += w
    kb.op(V, lambda e: e.memset(cpadS[:, 0:128], NEG), writes=("cpadS",))
    kb.op(S, lambda e: e.activation(cpadS[:, :], cpadS[:, :], AF.Exp),
          reads=("cpadS",), writes=("cpadS",))
    ld(scr_d[:, :], cpadS[:, :], sem="qS", reads=("cpadS",), writes=("cscr",))

    for tt in range(NTT):
        p = tt % 2
        o = p * 8
        sk = f"st_s{p}"
        ld(xt[p][:, :], x_d[tt * 128:(tt + 1) * 128, :], sem=f"qX{p}",
           writes=(f"xt{p}",))
        kb.op(V, lambda e, p=p, o=o:
              e.reduce_sum(st_s[:, o:o + 1], xt[p][:, :], axis=AX.X),
              reads=(f"xt{p}",), writes=(sk,))
        kb.op(V, lambda e, p=p: e.tensor_tensor(sqt[p][:, :], xt[p][:, :],
                                                xt[p][:, :], ALU.mult),
              reads=(f"xt{p}",), writes=(f"sqt{p}",))
        kb.op(V, lambda e, p=p, o=o:
              e.reduce_sum(st_s[:, o + 1:o + 2], sqt[p][:, :], axis=AX.X),
              reads=(f"sqt{p}",), writes=(sk,))
        kb.op(V, lambda e, o=o:
              e.tensor_scalar(st_s[:, o + 2:o + 3], st_s[:, o:o + 1], 1.0 / D,
                              None, ALU.mult),
              reads=(sk,), writes=(sk,))
        kb.op(V, lambda e, o=o:
              e.tensor_tensor(st_s[:, o + 3:o + 4], st_s[:, o + 2:o + 3],
                              st_s[:, o + 2:o + 3], ALU.mult),
              reads=(sk,), writes=(sk,))
        kb.op(V, lambda e, o=o:
              e.tensor_scalar(st_s[:, o + 4:o + 5], st_s[:, o + 1:o + 2],
                              1.0 / D, st_s[:, o + 3:o + 4],
                              ALU.mult, ALU.subtract),
              reads=(sk,), writes=(sk,))
        kb.op(S, lambda e, o=o:
              e.activation(st_s[:, o + 5:o + 6], st_s[:, o + 4:o + 5], AF.Ln,
                           bias=epsc[:, 0:1]),
              reads=(sk, "epsc"), writes=(sk,))
        kb.op(S, lambda e, o=o:
              e.activation(st_s[:, o + 6:o + 7], st_s[:, o + 5:o + 6], AF.Exp,
                           scale=-0.5),
              reads=(sk,), writes=(sk,))
        kb.op(V, lambda e, p=p, o=o:
              e.tensor_scalar(xnt[p][:, :], xt[p][:, :],
                              st_s[:, o + 2:o + 3], st_s[:, o + 6:o + 7],
                              ALU.subtract, ALU.mult),
              reads=(f"xt{p}", sk), writes=(f"xnt{p}",))
        for c in range(8):
            bk = 6 + (c % 2)
            sl = ((c // 2) % 2) * 128
            kb.op(TE, lambda e, p=p, c=c, bk=bk, sl=sl:
                  e.transpose(pb[bk][:, sl:sl + 128],
                              xnt[p][:, c * 128:(c + 1) * 128], iden_f[:, :]),
                  reads=(f"xnt{p}", "iden_f"), writes=(f"pb{bk}",))
            kb.op(S, lambda e, tt=tt, c=c, bk=bk, sl=sl:
                  e.copy(XNF[c][:, tt * 128:(tt + 1) * 128],
                         pb[bk][:, sl:sl + 128]),
                  reads=(f"pb{bk}",), writes=(f"xnf{c}",))
    for c in range(8):
        kb.op(G, lambda e, c=c: e.tensor_copy(XN16[c][:, :], XNF[c][:, :]),
              reads=(f"xnf{c}",), writes=(f"xn16_{c}",))
    st_a2.close()
    kb.fence()

    st_b1 = ExitStack()
    gwS = sb("gwS", (NEXP, D), F32, st_b1)
    gwT = sb("gwT", (128, 80), F32, st_b1)
    lgS = sb("lgS", (NEXP, T), F32, st_b1)
    lgT = sb("lgT", (128, NTT * 16), F32, st_b1)
    cwTok = sb("cwTok", (128, NTT * 8), F32, st_b1)

    ld(gwS[:, :], w_d["gw"][:, :], writes=("gwS",))
    kb.finish_queue(MISC, ["gwS"])
    oh_i = sb("oh_i", (NTRU, NTRU * 128), F32, st_b1)
    ip8 = sb("ip8", (NTRU, 1), F32, st_b1)
    kb.op(G, lambda e: e.iota(oh_i[:, :], [[1, NTRU], [0, 128]],
                              channel_multiplier=0,
                              allow_small_or_imprecise_dtypes=True),
          writes=("oh_i",))
    kb.op(G, lambda e: e.iota(ip8[:, :], [[0, 1]], channel_multiplier=1,
                              allow_small_or_imprecise_dtypes=True),
          writes=("ip8",))
    kb.op(V, lambda e: e.tensor_scalar(ohE[:, :], oh_i[:, :], ip8[:, :], None,
                                       ALU.is_equal),
          reads=("oh_i", "ip8"), writes=("ohE",))
    for c in range(8):
        bk = c % 2
        kb.op(TE, lambda e, c=c, bk=bk:
              e.transpose(pb[bk][:, 0:NEXP], gwS[:, c * 128:(c + 1) * 128],
                          iden_f[0:NEXP, 0:NEXP]),
              reads=("gwS", "iden_f"), writes=(f"pb{bk}",))
        kb.op(V, lambda e, c=c, bk=bk:
              e.tensor_copy(gwT[:, c * 10:(c + 1) * 10], pb[bk][:, 0:NEXP]),
              reads=(f"pb{bk}",), writes=("gwT",))
    for pp in range(NP):
        for c in range(8):
            kb.op(TE, lambda e, pp=pp, c=c:
                  e.matmul(pb[4 + pp % 4][0:NEXP, 0:512],
                           gwT[:, c * 10:(c + 1) * 10],
                           XNF[c][:, pp * 512:(pp + 1) * 512],
                           start=(c == 0), stop=(c == 7)),
                  reads=("gwT", f"xnf{c}"), writes=(f"pb{4 + pp % 4}",),
                  inc=(c == 7))
        kb.op(V, lambda e, pp=pp:
              e.tensor_scalar(lgS[:, pp * 512:(pp + 1) * 512],
                              pb[4 + pp % 4][0:NEXP, 0:512], gbc[:, 0:1],
                              None, ALU.add),
              reads=(f"pb{4 + pp % 4}", "gbc"), writes=("lgS",))
    for tt in range(NTT):
        bk = tt % 2
        kb.op(TE, lambda e, tt=tt, bk=bk:
              e.transpose(pb[bk][:, 0:NEXP], lgS[:, tt * 128:(tt + 1) * 128],
                          iden_f[0:NEXP, 0:NEXP]),
              reads=("lgS", "iden_f"), writes=(f"pb{bk}",))
        kb.op(V, lambda e, tt=tt, bk=bk:
              e.tensor_copy(lgT[:, tt * 16:tt * 16 + NEXP], pb[bk][:, 0:NEXP]),
              reads=(f"pb{bk}",), writes=("lgT",))
    gcol = sb("gcol", (128, 16 * 8), F32, st_b1)
    gw1 = sb("gw1", (128, NTT * 16), F32, st_b1)
    gw2 = sb("gw2", (128, NTT * 16), F32, st_b1)
    gw3 = sb("gw3", (128, NTT * 16), F32, st_b1)

    def d3(h, width=16, n=10):
        return A(h, 0, [[NTT * width, 128], [width, NTT], [1, n]])

    def b3(slot):
        return A(gcol, slot * NTT, [[16 * 8, 128], [1, NTT], [0, 10]])

    def gcolf(slot):
        return gcol[:, slot * NTT:(slot + 1) * NTT]

    iota3 = A(iota10, 0, [[12, 128], [0, NTT], [1, 10]])
    iotaB3 = A(iotaB, 0, [[12, 128], [0, NTT], [1, 10]])

    def vg(fn, reads, writes):
        kb.op(V, fn, reads=reads, writes=writes)

    vg(lambda e: e.tensor_reduce(gcolf(0), d3(lgT), AX.X, ALU.max),
       ("lgT",), ("gcol",))
    vg(lambda e: e.tensor_tensor(d3(gw1), d3(lgT), b3(0), ALU.subtract),
       ("lgT", "gcol"), ("gw1",))
    kb.op(S, lambda e: e.activation(d3(gw1), d3(gw1), AF.Exp),
          reads=("gw1",), writes=("gw1",))
    vg(lambda e: e.tensor_reduce(gcolf(1), d3(gw1), AX.X, ALU.add),
       ("gw1",), ("gcol",))
    vg(lambda e: e.reciprocal(gcolf(1), gcolf(1)), ("gcol",), ("gcol",))
    vg(lambda e: e.tensor_tensor(d3(gw1), d3(gw1), b3(1), ALU.mult),
       ("gw1", "gcol"), ("gw1",))
    vg(lambda e: e.tensor_reduce(gcolf(2), d3(gw1), AX.X, ALU.max),
       ("gw1",), ("gcol",))
    vg(lambda e: e.tensor_tensor(d3(gw2), d3(gw1), b3(2), ALU.is_equal),
       ("gw1", "gcol"), ("gw2",))
    vg(lambda e: e.scalar_tensor_tensor(d3(gw3), d3(gw2), -1.0e4, iotaB3,
                                        ALU.mult, ALU.add),
       ("gw2", "iotaB"), ("gw3",))
    vg(lambda e: e.tensor_reduce(gcolf(3), d3(gw3), AX.X, ALU.min),
       ("gw3",), ("gcol",))
    vg(lambda e: e.tensor_tensor(d3(gw2), iota3, b3(3), ALU.is_equal),
       ("iota10", "gcol"), ("gw2",))
    vg(lambda e: e.scalar_tensor_tensor(d3(gw3), d3(gw2), -BIG, d3(gw1),
                                        ALU.mult, ALU.add),
       ("gw2", "gw1"), ("gw3",))
    vg(lambda e: e.tensor_reduce(gcolf(4), d3(gw3), AX.X, ALU.max),
       ("gw3",), ("gcol",))
    vg(lambda e: e.tensor_tensor(d3(gw1), d3(gw3), b3(4), ALU.is_equal),
       ("gw3", "gcol"), ("gw1",))
    vg(lambda e: e.scalar_tensor_tensor(d3(gw3), d3(gw1), -1.0e4, iotaB3,
                                        ALU.mult, ALU.add),
       ("gw1", "iotaB"), ("gw3",))
    vg(lambda e: e.tensor_reduce(gcolf(5), d3(gw3), AX.X, ALU.min),
       ("gw3",), ("gcol",))
    vg(lambda e: e.tensor_tensor(d3(gw3), iota3, b3(5), ALU.is_equal),
       ("iota10", "gcol"), ("gw3",))
    vg(lambda e: e.tensor_scalar(gcolf(6), gcolf(3), 7.5, None, ALU.is_lt),
       ("gcol",), ("gcol",))
    vg(lambda e: e.tensor_tensor(gcolf(6), gcolf(6), gcolf(2), ALU.mult),
       ("gcol",), ("gcol",))
    vg(lambda e: e.tensor_scalar(gcolf(7), gcolf(5), 7.5, None, ALU.is_lt),
       ("gcol",), ("gcol",))
    vg(lambda e: e.tensor_tensor(gcolf(7), gcolf(7), gcolf(4), ALU.mult),
       ("gcol",), ("gcol",))
    vg(lambda e: e.tensor_tensor(gcolf(2), gcolf(6), gcolf(7), ALU.add),
       ("gcol",), ("gcol",))
    vg(lambda e: e.tensor_scalar(gcolf(3), gcolf(2), 0.0, None, ALU.is_le),
       ("gcol",), ("gcol",))
    vg(lambda e: e.tensor_tensor(gcolf(2), gcolf(2), gcolf(3), ALU.add),
       ("gcol",), ("gcol",))
    vg(lambda e: e.reciprocal(gcolf(2), gcolf(2)), ("gcol",), ("gcol",))
    vg(lambda e: e.tensor_tensor(gcolf(6), gcolf(6), gcolf(2), ALU.mult),
       ("gcol",), ("gcol",))
    vg(lambda e: e.tensor_tensor(gcolf(7), gcolf(7), gcolf(2), ALU.mult),
       ("gcol",), ("gcol",))
    vg(lambda e: e.tensor_tensor(d3(gw2, n=8), d3(gw2, n=8),
                                 A(gcol, 6 * NTT, [[16 * 8, 128], [1, NTT], [0, 8]]),
                                 ALU.mult),
       ("gw2", "gcol"), ("gw2",))
    vg(lambda e: e.tensor_tensor(d3(gw3, n=8), d3(gw3, n=8),
                                 A(gcol, 7 * NTT, [[16 * 8, 128], [1, NTT], [0, 8]]),
                                 ALU.mult),
       ("gw3", "gcol"), ("gw3",))
    vg(lambda e: e.tensor_tensor(A(cwTok, 0, [[NTT * 8, 128], [8, NTT], [1, 8]]),
                                 d3(gw2, n=8), d3(gw3, n=8), ALU.add),
       ("gw2", "gw3"), ("cwTok",))
    for tt in range(NTT):
        bk = tt % 2
        kb.op(TE, lambda e, tt=tt, bk=bk:
              e.transpose(pb[bk][0:8, 0:128], cwTok[:, tt * 8:(tt + 1) * 8],
                          iden_f[:, :]),
              reads=("cwTok", "iden_f"), writes=(f"pb{bk}",))
        kb.op(V, lambda e, tt=tt, bk=bk:
              e.tensor_copy(cwT[:, tt * 128:(tt + 1) * 128], pb[bk][0:8, 0:128]),
              reads=(f"pb{bk}",), writes=("cwT",))
    st_b1.close()
    st_a.close()
    kb.fence()

    st_b2 = ExitStack()
    ACC = [sb(f"acc{c}", (128, T), F32, st_b2) for c in range(8)]
    cwbA = [sb(f"cwb{e}", (128, T), BF16, st_b2) for e in range(NTRU)]
    mtmp = [sb(f"mtmp{i}", (128, T), F32, st_b2) for i in range(2)]
    We16 = [sb(f"we16_{i}", (128, 4096), BF16, st_b2) for i in range(2)]

    for e_ in range(NTRU):
        for pp in range(NP):
            bk = pp % 2
            kb.op(TE, lambda e, e_=e_, pp=pp, bk=bk:
                  e.matmul(pb[bk][:, 0:512], ohE[:, e_ * 128:(e_ + 1) * 128],
                           cwT[:, pp * 512:(pp + 1) * 512],
                           start=True, stop=True),
                  reads=("ohE", "cwT"), writes=(f"pb{bk}",))
            kb.op(V, lambda e, e_=e_, pp=pp, bk=bk:
                  e.tensor_copy(cwbA[e_][:, pp * 512:(pp + 1) * 512],
                                pb[bk][:, 0:512]),
                  reads=(f"pb{bk}",), writes=(f"cwb{e_}",))

    def ew_load(e_, hlf, par):
        for k in range(8):
            ld(We16[par][:, k * 512:(k + 1) * 512],
               w_d["ew"][e_, k * 128:(k + 1) * 128, hlf * 512:(hlf + 1) * 512],
               cast=True, sem=f"qW{par}", writes=(f"we16_{par}",),
               silent=(k > 0))
        kb.finish_queue(f"qW{par}", [f"we16_{par}"])

    ew_load(0, 0, 0)
    for e_ in range(NTRU):
        for hlf in range(2):
            par = hlf
            nxt_e, nxt_h = (e_, 1) if hlf == 0 else (e_ + 1, 0)
            if nxt_e < NTRU:
                ew_load(nxt_e, nxt_h, 1 - hlf)
            for dj in range(4):
                dout = hlf * 4 + dj
                col = dj * 128
                pbase = (dout % 2) * 4
                for k in range(8):
                    for pp in range(NP):
                        bk = pbase + pp
                        kb.op(TE, lambda e, par=par, k=k, col=col, pp=pp, bk=bk:
                              e.matmul(pb[bk][:, 0:512],
                                       We16[par][:, k * 512 + col:
                                                 k * 512 + col + 128],
                                       XN16[k][:, pp * 512:(pp + 1) * 512],
                                       start=(k == 0), stop=(k == 7)),
                              reads=(f"we16_{par}", f"xn16_{k}"),
                              writes=(f"pb{bk}",), inc=(k == 7))
                mp = dout % 2
                for pp in range(NP):
                    bk = pbase + pp
                    if e_ == 0:
                        kb.op(V, lambda e, e_=e_, dout=dout, pp=pp, bk=bk:
                              e.scalar_tensor_tensor(
                                  ACC[dout][:, pp * 512:(pp + 1) * 512],
                                  pb[bk][:, 0:512],
                                  ebCol[:, e_ * 8 + dout:e_ * 8 + dout + 1],
                                  cwbA[e_][:, pp * 512:(pp + 1) * 512],
                                  ALU.add, ALU.mult),
                              reads=(f"pb{bk}", "ebCol", f"cwb{e_}"),
                              writes=(f"acc{dout}",))
                    else:
                        kb.op(V, lambda e, e_=e_, dout=dout, pp=pp, bk=bk,
                              mp=mp:
                              e.scalar_tensor_tensor(
                                  mtmp[mp][:, pp * 512:(pp + 1) * 512],
                                  pb[bk][:, 0:512],
                                  ebCol[:, e_ * 8 + dout:e_ * 8 + dout + 1],
                                  cwbA[e_][:, pp * 512:(pp + 1) * 512],
                                  ALU.add, ALU.mult),
                              reads=(f"pb{bk}", "ebCol", f"cwb{e_}"),
                              writes=(f"mtmp{mp}",))
                if e_ > 0:
                    kb.op(G, lambda e, dout=dout, mp=mp:
                          e.tensor_tensor(ACC[dout][:, :], ACC[dout][:, :],
                                          mtmp[mp][:, :], ALU.add),
                          reads=(f"acc{dout}", f"mtmp{mp}"),
                          writes=(f"acc{dout}",))
    for c in range(8):
        ld(acc_d[c * 128:(c + 1) * 128, :], ACC[c][:, :], sem="qA0",
           reads=(f"acc{c}",), writes=(f"accp{c}",))
    st_b2.close()
    st_cw.close()
    kb.fence()

    st_c_k = ExitStack()
    K16 = [sb(f"k16_{c}", (128, T), BF16, st_c_k) for c in range(8)]
    st_c_bc = ExitStack()
    Y16p = [sb(f"y16p_{c}", (128, T), BF16, st_c_bc) for c in range(8)]
    Q16 = [sb(f"q16_{c}", (128, T), BF16, st_c_bc) for c in range(8)]
    st_c_vt = ExitStack()
    VtL = [sb(f"vtl{t}", (128, H * 65), BF16, st_c_vt) for t in range(NTT)]
    st_c0 = ExitStack()
    sqP = [sb(f"sq{i}", (128, T), BF16, st_c0) for i in range(2)]
    Vf32 = sb("vf32", (128, T), F32, st_c0)
    rowscrP = sb("rowscrP", (2, 512), F32, st_c0)
    rowb16 = [sb(f"rowb16_{i}", (2, T), BF16, st_c0) for i in range(2)]
    Wst = [sb(f"wst{i}", (128, 1024), BF16, st_c0) for i in range(2)]

    XN2 = Y16p

    def wload(par, dram, r0, c0, c1, key, sem=None):
        for k in range(8):
            ld(Wst[par][:, k * (c1 - c0):(k + 1) * (c1 - c0)],
               dram[r0 + k * 128:r0 + (k + 1) * 128, c0:c1], cast=True,
               sem=sem or f"qW{par}", writes=(key,), silent=(k > 0))
        kb.finish_queue(sem or f"qW{par}", [key])

    wload(0, w_d["Wq"], 0, 0, 128, "wst0")
    wload(1, w_d["Wq"], 0, 128, 256, "wst1")

    for c in range(8):
        kb.op(V, lambda e, c=c: e.tensor_tensor(sqP[c % 2][:, :], XN16[c][:, :],
                                                XN16[c][:, :], ALU.mult),
              reads=(f"xn16_{c}",), writes=(f"sq{c % 2}",))
        for pp in range(NP):
            bk = 4 + pp
            kb.op(TE, lambda e, c=c, pp=pp, bk=bk:
                  e.matmul(pb[bk][0:1, 0:512], onesb[:, 0:1],
                           sqP[c % 2][:, pp * 512:(pp + 1) * 512],
                           start=(c == 0), stop=(c == 7)),
                  reads=(f"sq{c % 2}", "onesb"), writes=(f"pb{bk}",), inc=True)
    for pp in range(NP):
        kb.op(S, lambda e, pp=pp:
              e.activation(rowscrP[0:1, :], pb[4 + pp][0:1, 0:512], AF.Ln,
                           bias=epsc[0:1, 1:2], scale=1.0 / D),
              reads=(f"pb{4 + pp}", "epsc"), writes=("rowscrP",))
        kb.op(S, lambda e, pp=pp:
              e.activation(rowb16[0][0:1, pp * 512:(pp + 1) * 512],
                           rowscrP[0:1, :], AF.Exp, scale=-0.5),
              reads=("rowscrP",), writes=("rowb16_0",))
    for c in range(8):
        for pp in range(NP):
            bk = 4 + pp % 2
            kb.op(TE, lambda e, c=c, pp=pp, bk=bk:
                  e.matmul(pb[bk][:, 0:512], ln1b[:, c * 128:(c + 1) * 128],
                           rowb16[0][0:1, pp * 512:(pp + 1) * 512],
                           start=True, stop=True),
                  reads=("ln1b", "rowb16_0"), writes=(f"pb{bk}",))
            kb.op(V, lambda e, c=c, pp=pp, bk=bk:
                  e.tensor_tensor(XN2[c][:, pp * 512:(pp + 1) * 512],
                                  XN16[c][:, pp * 512:(pp + 1) * 512],
                                  pb[bk][:, 0:512], ALU.mult),
                  reads=(f"xn16_{c}", f"pb{bk}"), writes=(f"y16p_{c}",))
        pbase0 = 0 if NP > 2 else 0
        for pp in range(NP):
            kb.op(TE, lambda e, c=c, pp=pp:
                  e.matmul(pb[pp][:, 0:512], Wst[0][:, c * 128:(c + 1) * 128],
                           XN2[c][:, pp * 512:(pp + 1) * 512],
                           start=(c == 0), stop=(c == 7)),
                  reads=("wst0", f"y16p_{c}"), writes=(f"pb{pp}",),
                  inc=(c == 7))
    for pp in range(NP):
        kb.op(V, lambda e, pp=pp:
              e.tensor_copy(Q16[0][:, pp * 512:(pp + 1) * 512],
                            pb[pp][:, 0:512]),
              reads=(f"pb{pp}",), writes=("q16_0",))

    def proj_unit(wname, dst, dkey, dout, nxt):
        def go():
            par = dout % 2
            if nxt is not None:
                wload(1 - par, w_d[nxt[0]], 0, nxt[1] * 128,
                      (nxt[1] + 1) * 128, f"wst{1 - par}")
            pbase = (dout % 2) * 2 if NP <= 2 else 0
            for k in range(8):
                for pp in range(NP):
                    bk = pbase + pp if NP <= 2 else pp
                    kb.op(TE, lambda e, par=par, k=k, pp=pp, bk=bk:
                          e.matmul(pb[bk][:, 0:512],
                                   Wst[par][:, k * 128:(k + 1) * 128],
                                   XN2[k][:, pp * 512:(pp + 1) * 512],
                                   start=(k == 0), stop=(k == 7)),
                          reads=(f"wst{par}", f"y16p_{k}"),
                          writes=(f"pb{bk}",), inc=(k == 7))
            for pp in range(NP):
                bk = (pbase + pp) if NP <= 2 else pp
                if dst is None:
                    kb.op(V, lambda e, pp=pp, bk=bk:
                          e.tensor_copy(Vf32[:, pp * 512:(pp + 1) * 512],
                                        pb[bk][:, 0:512]),
                          reads=(f"pb{bk}",), writes=("vf32",))
                else:
                    kb.op(V, lambda e, dout=dout, pp=pp, bk=bk:
                          e.tensor_copy(dst[dout][:, pp * 512:(pp + 1) * 512],
                                        pb[bk][:, 0:512]),
                          reads=(f"pb{bk}",), writes=(dkey.format(dout),))
        return go

    def qn_sq(XB, keyfmt, c):
        pr = c % 2
        kb.op(V, lambda e, c=c, pr=pr:
              e.tensor_tensor(sqP[pr][:, :], XB[c][:, :], XB[c][:, :],
                              ALU.mult),
              reads=(keyfmt.format(c),), writes=(f"sq{pr}",))

    def qn_stats(XB, keyfmt, c, fold_scale, bank0):
        pr = c % 2
        for pp in range(NP):
            bk = bank0 + pp % 2
            kb.op(TE, lambda e, pr=pr, pp=pp, bk=bk:
                  e.matmul(pb[bk][0:2, 0:512], selq[:, :],
                           sqP[pr][:, pp * 512:(pp + 1) * 512],
                           start=True, stop=True),
                  reads=(f"sq{pr}", "selq"), writes=(f"pb{bk}",))
            kb.op(S, lambda e, bk=bk:
                  e.activation(rowscrP[0:2, :], pb[bk][0:2, 0:512], AF.Ln,
                               bias=epsc[0:2, 2:3]),
                  reads=(f"pb{bk}", "epsc"), writes=("rowscrP",))
            kb.op(S, lambda e, pr=pr, pp=pp:
                  e.activation(rowb16[pr][0:2, pp * 512:(pp + 1) * 512],
                               rowscrP[0:2, :], AF.Exp, scale=-0.5),
                  reads=("rowscrP",), writes=(f"rowb16_{pr}",))
            if fold_scale:
                kb.op(V, lambda e, c=c, pr=pr, pp=pp:
                      e.tensor_scalar(rowb16[pr][0:2, pp * 512:(pp + 1) * 512],
                                      rowb16[pr][0:2, pp * 512:(pp + 1) * 512],
                                      scaleP[:, c:c + 1], None, ALU.mult),
                      reads=(f"rowb16_{pr}", "scaleP"),
                      writes=(f"rowb16_{pr}",))

    def qn_apply_piece(XB, keyfmt, c, pp, bk):
        pr = c % 2
        kb.op(TE, lambda e, pr=pr, pp=pp, bk=bk:
              e.matmul(pb[bk][:, 0:512], sel2T[:, :],
                       rowb16[pr][0:2, pp * 512:(pp + 1) * 512],
                       start=True, stop=True),
              reads=(f"rowb16_{pr}", "sel2T"), writes=(f"pb{bk}",))
        kb.op(V, lambda e, c=c, pp=pp, bk=bk:
              e.tensor_tensor(XB[c][:, pp * 512:(pp + 1) * 512],
                              XB[c][:, pp * 512:(pp + 1) * 512],
                              pb[bk][:, 0:512], ALU.mult),
              reads=(keyfmt.format(c), f"pb{bk}"), writes=(keyfmt.format(c),))

    for d in range(1, 8):
        nxt = ("Wq", d + 1) if d < 7 else ("Wk", 0)
        proj_unit("Wq", Q16, "q16_{}", d, nxt)()
    for d in range(8):
        nxt = ("Wk", d + 1) if d < 7 else ("Wv", 0)
        qn_sq(Q16, "q16_{}", d)
        proj_unit("Wk", K16, "k16_{}", d, nxt)()
        qn_stats(Q16, "q16_{}", d, True, 4)
        if d > 0:
            for pp in range(NP):
                qn_apply_piece(Q16, "q16_{}", d - 1, pp, 6 + pp % 2)
    for pp in range(NP):
        qn_apply_piece(Q16, "q16_{}", 7, pp, 6 + pp % 2)

    def v_transpose(c, tt):
        reg = (tt % 4) * 128
        kb.op(TE, lambda e, tt=tt, reg=reg:
              e.transpose(pb[4][:, reg:reg + 128],
                          Vf32[:, tt * 128:(tt + 1) * 128], iden_f[:, :]),
              reads=("vf32", "iden_f"), writes=("pb4",))
        for hf, eng in ((0, V), (1, V)):
            hh = 2 * c + hf
            kb.op(eng, lambda e, tt=tt, reg=reg, hf=hf, hh=hh:
                  e.tensor_copy(VtL[tt][:, hh * 65:hh * 65 + 64],
                                pb[4][:, reg + hf * 64:reg + hf * 64 + 64]),
                  reads=("pb4",), writes=(f"vtl{tt}",))

    for d in range(8):
        par = d % 2
        qn_sq(K16, "k16_{}", d)
        if d < 7:
            wload(1 - par, w_d["Wv"], 0, (d + 1) * 128, (d + 2) * 128,
                  f"wst{1 - par}")
        pbase = (d % 2) * 2 if NP <= 2 else 0
        for k in range(8):
            for pp in range(NP):
                bk = pbase + pp if NP <= 2 else pp
                kb.op(TE, lambda e, par=par, k=k, pp=pp, bk=bk:
                      e.matmul(pb[bk][:, 0:512],
                               Wst[par][:, k * 128:(k + 1) * 128],
                               XN2[k][:, pp * 512:(pp + 1) * 512],
                               start=(k == 0), stop=(k == 7)),
                      reads=(f"wst{par}", f"y16p_{k}"),
                      writes=(f"pb{bk}",), inc=(k == 7))
            if d > 0 and 2 * k < NTT:
                v_transpose(d - 1, 2 * k)
                v_transpose(d - 1, 2 * k + 1)
            if d > 0 and k % 2 == 1 and (k - 1) // 2 < NP:
                qn_apply_piece(K16, "k16_{}", d - 1, (k - 1) // 2, 7)
        for pp in range(NP):
            bk = pbase + pp if NP <= 2 else pp
            kb.op(V, lambda e, pp=pp, bk=bk:
                  e.tensor_copy(Vf32[:, pp * 512:(pp + 1) * 512],
                                pb[bk][:, 0:512]),
                  reads=(f"pb{bk}",), writes=("vf32",))
        qn_stats(K16, "k16_{}", d, False, 5)
    for tt in range(NTT):
        v_transpose(7, tt)
    for pp in range(NP):
        qn_apply_piece(K16, "k16_{}", 7, pp, 7)
    for tt in range(NTT):
        kb.op(V, lambda e, tt=tt:
              e.memset(A(VtL[tt], 64, [[H * 65, 128], [65, H]]), 1.0),
              writes=(f"vtl{tt}",))
    st_c0.close()
    kb.fence()

    st_ps_a.close()
    st_ps_c1 = ExitStack()
    s16 = [st_ps_c1.enter_context(nc.psum_tensor(f"s16_{i}", [128, 512], F32))
           for i in range(3)]
    pbc = [st_ps_c1.enter_context(nc.psum_tensor("pbc0", [128, 512], F32))]
    st_c1 = ExitStack()
    Tst = [sb(f"tst{i}", (128, T), BF16, st_c1) for i in range(2)]
    Pt = [sb(f"pt{i}", (128, T), BF16, st_c1) for i in range(4)]
    denr = sb("denr", (1, 512), F32, st_c1)
    denb = sb("denb", (1, 1024), BF16, st_c1)
    BC16 = sb("bc16", (64, 1024), BF16, st_c1)
    O16t = sb("o16t", (64, T), BF16, st_c1)

    ld(Tst[0][:, :], A(scr_d, 0, [[1, 128], [1, T]]), cast=True, sem="qT0",
       reads=("cscr",), writes=("tst0",))

    NB = min(4, NP)
    vsum = sb("vsum", (1, H * 65), F32, st_c1)
    vsumT = sb("vsumT", (65, H), F32, st_c1)
    for g0, g1 in ((0, 512), (512, 1024), (1024, H * 65)):
        for tt in range(NTT):
            kb.op(TE, lambda e, g0=g0, g1=g1, tt=tt:
                  e.matmul(pbc[0][0:1, 0:g1 - g0], onesb[:, 0:1],
                           VtL[tt][:, g0:g1], start=(tt == 0),
                           stop=(tt == NTT - 1)),
                  reads=(f"vtl{tt}", "onesb"), writes=("pbc0",),
                  inc=(tt == NTT - 1))
        kb.op(V, lambda e, g0=g0, g1=g1:
              e.tensor_copy(vsum[:, g0:g1], pbc[0][0:1, 0:g1 - g0]),
              reads=("pbc0",), writes=("vsum",))
    for h in range(H):
        kb.op(TE, lambda e, h=h:
              e.transpose(pbc[0][0:65, h:h + 1], vsum[0:1, h * 65:(h + 1) * 65],
                          iden_f[0:1, 0:1]),
              reads=("vsum", "iden_f"), writes=("pbc0",), inc=(h == H - 1))
    kb.op(V, lambda e: e.tensor_copy(vsumT[:, :], pbc[0][0:65, 0:H]),
          reads=("pbc0",), writes=("vsumT",))

    def mk_post_bank(h, b):
        c, hf = h // 2, h % 2

        def post():
            kb.op(TE, lambda e, b=b:
                  e.matmul(pbc[0][0:64, 0:512], onesbr[:, :],
                           denb[:, (b % 2) * 512:(b % 2) * 512 + 512],
                           start=True, stop=True),
                  reads=("denb", "onesbr"), writes=("pbc0",))
            kb.op(V, lambda e, b=b:
                  e.tensor_copy(BC16[:, (b % 2) * 512:(b % 2) * 512 + 512],
                                pbc[0][0:64, 0:512]),
                  reads=("pbc0",), writes=("bc16",))
            dst = (lambda sl: Y16p[c][0:64, sl]) if hf == 0 else                   (lambda sl: O16t[:, sl])
            dkey = f"y16p_{c}" if hf == 0 else "o16t"
            kb.op(V, lambda e, b=b, dst=dst:
                  e.tensor_tensor(dst(slice(b * 512, (b + 1) * 512)),
                                  pb[b][0:64, 0:512],
                                  BC16[:, (b % 2) * 512:(b % 2) * 512 + 512],
                                  ALU.mult),
                  reads=(f"pb{b}", "bc16"), writes=(dkey,))
            if b == 0:
                kb.op(V, lambda e, h=h, dst=dst:
                      e.tensor_scalar(dst(slice(0, 1)), vsumT[0:64, h:h + 1],
                                      1.0 / T, None, ALU.mult),
                      reads=("vsumT",), writes=(dkey,))
            if b == NB - 1 and hf == 1:
                kb.op(G, lambda e, c=c:
                      e.dma_start(Y16p[c][64:128, :], O16t[:, :]),
                      reads=("o16t",), writes=(f"y16p_{c}",), dma_sem="qO")
        return post

    fifo = []
    postq = []
    cct = [0]

    def pop_one():
        h, b, q, w, j0, kt, pslot, ptk, lb = fifo.pop(0)
        if kt == 0:
            keep = []
            for ent in postq:
                if ent[2] == b:
                    ent[0]()
                else:
                    keep.append(ent)
            postq[:] = keep
        kb.op(TE, lambda e, h=h, b=b, q=q, w=w, j0=j0, kt=kt, pslot=pslot,
              lb=lb:
              e.matmul(pb[b][0:65, q % 512:q % 512 + w],
                       VtL[kt][:, h * 65:(h + 1) * 65],
                       Pt[pslot][:, j0:j0 + w],
                       start=(kt == 0), stop=lb),
              reads=(f"vtl{kt}", ptk), writes=(f"pb{b}",), inc=lb)
        if lb:
            kb.op(S, lambda e, b=b:
                  e.activation(denr[:, :], pb[b][64:65, 0:512], AF.Ln,
                               bias=epsc[0:1, 3:4]),
                  reads=(f"pb{b}", "epsc"), writes=("denr",))
            kb.op(S, lambda e, b=b:
                  e.activation(denb[:, (b % 2) * 512:(b % 2) * 512 + 512],
                               denr[:, :], AF.Exp, scale=-1.0),
                  reads=("denr",), writes=("denb",))
            postq.append((mk_post_bank(h, b), cct[0], b))

    def tflip(par, pp):
        kb.op(TE, lambda e, par=par, pp=pp:
              e.matmul(pbc[0][:, 0:512], Jmat[:, :],
                       Tst[par][:, pp * 512:(pp + 1) * 512],
                       start=True, stop=True),
              reads=("Jmat", f"tst{par}"), writes=("pbc0",))
        kb.op(V, lambda e, par=par, pp=pp:
              e.tensor_copy(Tst[par][:, pp * 512:(pp + 1) * 512],
                            pbc[0][:, 0:512]),
              reads=("pbc0",), writes=(f"tst{par}",))

    for pp in range(NP):
        tflip(0, pp)
    for h in range(H):
        par = h % 2
        c, hf = h // 2, h % 2
        po = 64 * hf
        if h + 1 < H:
            ld(Tst[1 - par][:, :], A(scr_d, (h + 1) * CP, [[1, 128], [1, T]]),
               cast=True, sem=f"qT{1 - par}", reads=("cscr",),
               writes=(f"tst{1 - par}",))
        for kt in range(NTT):
            if NTT // 2 <= kt < NTT // 2 + NP and h + 1 < H:
                tflip(1 - par, kt - NTT // 2)
            ktg = h * NTT + kt
            while fifo and fifo[0][5] + (fifo[0][0] - h) * NTT <= kt - 3:
                pop_one()
            q0 = kt * 128
            cs = q0
            ci = 0
            while cs < T:
                while postq and postq[0][1] <= cct[0] - 2:
                    postq.pop(0)[0]()
                ce = min((cs // 512 + 1) * 512, T)
                w2 = ce - cs
                sbk = cct[0] % 3
                pslot = ktg % 4
                ptk = f"pt{pslot}_c{ci}"
                jc = cs - q0
                kb.op(TE, lambda e, sbk=sbk, c=c, po=po, kt=kt, q=cs, w=w2:
                      e.matmul(s16[sbk][:, 0:w],
                               K16[c][po:po + 64, kt * 128:(kt + 1) * 128],
                               Q16[c][po:po + 64, q:q + w],
                               start=True, stop=True),
                      reads=(f"k16_{c}", f"q16_{c}"),
                      writes=(f"s16_{sbk}",), inc=True)
                kb.op(S, lambda e, sbk=sbk, pslot=pslot, jc=jc, w2=w2:
                      e.activation(Pt[pslot][:, jc:jc + w2], s16[sbk][:, 0:w2],
                                   AF.Exp),
                      reads=(f"s16_{sbk}",), writes=(ptk,))
                kb.op(V, lambda e, par=par, pslot=pslot, jc=jc, w2=w2:
                      e.tensor_tensor(Pt[pslot][:, jc:jc + w2],
                                      Pt[pslot][:, jc:jc + w2],
                                      Tst[par][:, jc:jc + w2], ALU.mult),
                      reads=(ptk, f"tst{par}"), writes=(ptk,))
                bb = cs // 512
                lb = (kt == min(4 * bb + 3, NTT - 1))
                fifo.append((h, bb, cs, w2, jc, kt, pslot, ptk, lb))
                cct[0] += 1
                while len(fifo) > 32:
                    pop_one()
                cs = ce
                ci += 1
    while fifo:
        pop_one()
    while postq:
        postq.pop(0)[0]()
    st_c1.close()
    st_c_vt.close()
    st_ps_c1.close()
    for i in (4, 5, 6, 7):
        pb[i] = ES.enter_context(nc.psum_tensor(f"pb{i}b", [128, 512], F32))
    kb.fence()

    st_c3 = ExitStack()
    NPH = max(1, NP // 2)

    st_c2 = ExitStack()
    accw = [sb(f"accw{i}", (128, T), F32, st_c2) for i in range(2)]
    rows2P = sb("rows2P", (1, 512), F32, st_c2)
    rowb2 = sb("rowb2", (1, T), BF16, st_c2)
    sq2 = [sb(f"sq2_{i}", (128, T), BF16, st_c2) for i in range(2)]
    Wst2 = [sb(f"ws2_{i}", (128, 2048), BF16, st_c2) for i in range(2)]

    def wload2(par, dram, r0, c0, c1, key):
        for k in range(8):
            ld(Wst2[par][:, k * (c1 - c0):(k + 1) * (c1 - c0)],
               dram[r0 + k * 128:r0 + (k + 1) * 128, c0:c1], cast=True,
               sem=f"qW{par}", writes=(key,), silent=(k > 0))
        kb.finish_queue(f"qW{par}", [key])

    Y16o = Q16
    wload2(0, w_d["Wo"], 0, 0, 256, "ws2_0")
    for qr in range(4):
        par = qr % 2
        if qr + 1 < 4:
            wload2(1 - par, w_d["Wo"], 0, (qr + 1) * 256, (qr + 2) * 256,
                   f"ws2_{1 - par}")
        for dj in range(2):
            dout = qr * 2 + dj
            apar = dout % 2
            ld(accw[apar][:, :], acc_d[dout * 128:(dout + 1) * 128, :],
               sem=f"qA{apar}", reads=(f"accp{dout}",), writes=(f"accw{apar}",))
        for dj in range(2):
            dout = qr * 2 + dj
            pbase = (dout % 2) * 4
            apar = dout % 2
            for k in range(8):
                for pp in range(NP):
                    kb.op(TE, lambda e, par=par, k=k, dj=dj, pp=pp, pbase=pbase:
                          e.matmul(pb[pbase + pp][:, 0:512],
                                   Wst2[par][:, k * 256 + dj * 128:
                                            k * 256 + dj * 128 + 128],
                                   Y16p[k][:, pp * 512:(pp + 1) * 512],
                                   start=(k == 0), stop=(k == 7)),
                          reads=(f"ws2_{par}", f"y16p_{k}"),
                          writes=(f"pb{pbase + pp}",), inc=(k == 7))
            for pp in range(NP):
                kb.op(V, lambda e, dout=dout, pp=pp, pbase=pbase:
                      e.tensor_tensor(Y16o[dout][:, pp * 512:(pp + 1) * 512],
                                      pb[pbase + pp][:, 0:512],
                                      XN16[dout][:, pp * 512:(pp + 1) * 512],
                                      ALU.add),
                      reads=(f"pb{pbase + pp}", f"xn16_{dout}"),
                      writes=(f"q16_{dout}",))
                kb.op(V, lambda e, apar=apar, pp=pp, pbase=pbase:
                      e.tensor_tensor(accw[apar][:, pp * 512:(pp + 1) * 512],
                                      accw[apar][:, pp * 512:(pp + 1) * 512],
                                      pb[pbase + pp][:, 0:512], ALU.add),
                      reads=(f"pb{pbase + pp}", f"accw{apar}"),
                      writes=(f"accw{apar}",))
            kb.op(G, lambda e, apar=apar, dout=dout:
                  e.tensor_tensor(accw[apar][:, :], accw[apar][:, :],
                                  XN16[dout][:, :], ALU.add),
                  reads=(f"xn16_{dout}", f"accw{apar}"), writes=(f"accw{apar}",))
            ld(acc_d[dout * 128:(dout + 1) * 128, :], accw[apar][:, :],
               sem=f"qA{apar}", reads=(f"accw{apar}",), writes=(f"accp{dout}",))
    for c in range(8):
        kb.op(V, lambda e, c=c: e.tensor_tensor(sq2[c % 2][:, :],
                                                Y16o[c][:, :],
                                                Y16o[c][:, :], ALU.mult),
              reads=(f"q16_{c}",), writes=(f"sq2_{c % 2}",))
        for pp in range(NP):
            bk = 4 + pp
            kb.op(TE, lambda e, pp=pp, bk=bk, c=c:
                  e.matmul(pb[bk][0:1, 0:512], onesb[:, 0:1],
                           sq2[c % 2][:, pp * 512:(pp + 1) * 512],
                           start=(c == 0), stop=(c == 7)),
                  reads=(f"sq2_{c % 2}", "onesb"), writes=(f"pb{bk}",),
                  inc=True)
    for pp in range(NP):
        kb.op(S, lambda e, pp=pp:
              e.activation(rows2P[0:1, :], pb[4 + pp][0:1, 0:512], AF.Ln,
                           bias=epsc[0:1, 1:2], scale=1.0 / D),
              reads=(f"pb{4 + pp}", "epsc"), writes=("rows2P",))
        kb.op(S, lambda e, pp=pp:
              e.activation(rowb2[0:1, pp * 512:(pp + 1) * 512],
                           rows2P[0:1, :], AF.Exp, scale=-0.5),
              reads=("rows2P",), writes=("rowb2",))
    YN16 = K16
    for c in range(8):
        for pp in range(NP):
            bk = pp % 2
            kb.op(TE, lambda e, c=c, pp=pp, bk=bk:
                  e.matmul(pb[bk][:, 0:512], ln2b[:, c * 128:(c + 1) * 128],
                           rowb2[0:1, pp * 512:(pp + 1) * 512],
                           start=True, stop=True),
                  reads=("ln2b", "rowb2"), writes=(f"pb{bk}",))
            kb.op(V, lambda e, c=c, pp=pp, bk=bk:
                  e.tensor_tensor(YN16[c][:, pp * 512:(pp + 1) * 512],
                                  Y16o[c][:, pp * 512:(pp + 1) * 512],
                                  pb[bk][:, 0:512], ALU.mult),
                  reads=(f"q16_{c}", f"pb{bk}"), writes=(f"k16_{c}",))
    st_c2.close()
    st_c_bc.close()
    kb.fence()

    H16 = [sb(f"h16_{i}", (128, T // 2), BF16, st_c3) for i in range(24)] + \
          [XN16[i] for i in range(8)]
    W2R = sb("w2r", (128, 32 * D), BF16, st_c3)
    Wst1 = [sb(f"ws1_{i}", (128, 2048), BF16, st_c3) for i in range(2)]
    accw2 = [sb(f"acw2_{i}", (128, T // 2), F32, st_c3) for i in range(2)]

    def w1load(par, e16):
        for k in range(8):
            ld(Wst1[par][:, k * 256:(k + 1) * 256],
               w_d["W1"][k * 128:(k + 1) * 128, e16 * 256:(e16 + 1) * 256],
               cast=True, sem=f"qW{par}", writes=(f"ws1_{par}",),
               silent=(k > 0))
        kb.finish_queue(f"qW{par}", [f"ws1_{par}"])

    def w2rload(h0, h1):
        for hk in range(h0, h1):
            ld(W2R[:, hk * D:(hk + 1) * D],
               w_d["W2"][hk * 128:(hk + 1) * 128, :],
               cast=True, sem="qF", silent=(hk % 4 != 0))
            if hk % 4 == 3 or hk == h1 - 1:
                kb.finish_queue("qF", [f"w2r_{k}" for k in
                                       range(hk - hk % 4, hk + 1)])
    w1load(0, 0)
    w2rload(0, 4)
    w1load(1, 1)

    def w1_emit(tp2, e8):
        t0c = tp2 * (T // 2)
        par = e8 % 2
        if e8 + 1 < 16:
            w1load(1 - par, e8 + 1)
        for hl in range(2):
            hk = e8 * 2 + hl
            pbase = (hk % 2) * 2
            for k in range(8):
                for pc in range(NPH):
                    kb.op(TE, lambda e, par=par, k=k, hl=hl, pc=pc,
                          pbase=pbase, t0c=t0c:
                          e.matmul(pb[pbase + pc][:, 0:512],
                                   Wst1[par][:, k * 256 + hl * 128:
                                            k * 256 + hl * 128 + 128],
                                   YN16[k][:, t0c + pc * 512:
                                           t0c + (pc + 1) * 512],
                                   start=(k == 0), stop=(k == 7)),
                          reads=(f"ws1_{par}", f"k16_{k}"),
                          writes=(f"pb{pbase + pc}",), inc=(k == 7))
            for pc in range(NPH):
                kb.op(S, lambda e, hk=hk, pc=pc, pbase=pbase:
                      e.activation(H16[hk][:, pc * 512:(pc + 1) * 512],
                                   pb[pbase + pc][:, 0:512], AF.Relu),
                      reads=(f"pb{pbase + pc}",), writes=(f"h16_{hk}",))

    def w2_emit(dp, hk):
        for j, dout in enumerate((2 * dp, 2 * dp + 1)):
            for pc in range(NPH):
                bk = 4 + 2 * j + pc if NPH == 2 else 4 + j
                kb.op(TE, lambda e, dout=dout, pc=pc, bk=bk, hk=hk:
                      e.matmul(pb[bk][:, 0:512],
                               W2R[:, hk * D + dout * 128:
                                    hk * D + (dout + 1) * 128],
                               H16[hk][:, pc * 512:(pc + 1) * 512],
                               start=(hk == 0), stop=(hk == 31)),
                      reads=(f"w2r_{hk}", f"h16_{hk}"),
                      writes=(f"pb{bk}",),
                      inc=(j == 1 and pc == NPH - 1))

    def w2_evac_pre(tp2, dp):
        t0c = tp2 * (T // 2)
        for j, dout in enumerate((2 * dp, 2 * dp + 1)):
            ld(accw2[j][:, :], acc_d[dout * 128:(dout + 1) * 128,
                                     t0c:t0c + T // 2],
               sem=f"qA{j}", reads=(f"accp{dout}",), writes=(f"acw2_{j}",))

    def w2_evac(tp2, dp):
        t0c = tp2 * (T // 2)
        for j, dout in enumerate((2 * dp, 2 * dp + 1)):
            for pc in range(NPH):
                bk = 4 + 2 * j + pc if NPH == 2 else 4 + j
                kb.op(V, lambda e, j=j, pc=pc, bk=bk:
                      e.tensor_tensor(accw2[j][:, pc * 512:(pc + 1) * 512],
                                      accw2[j][:, pc * 512:(pc + 1) * 512],
                                      pb[bk][:, 0:512], ALU.add),
                      reads=(f"pb{bk}", f"acw2_{j}"), writes=(f"acw2_{j}",))
            ld(acc_d[dout * 128:(dout + 1) * 128, t0c:t0c + T // 2],
               accw2[j][:, :], sem=f"qA{j}", reads=(f"acw2_{j}",),
               writes=(f"accp{dout}_{tp2}",))

    for tp2 in range(2):
        if tp2 == 1:
            w1load(0, 0)
        w2_evac_pre(tp2, 0)
        w2lag = 0
        for e8 in range(16):
            if tp2 == 0:
                lo, hi = 4 + 2 * e8, min(6 + 2 * e8, 32)
                if lo < 32:
                    w2rload(lo, hi)
            w1_emit(tp2, e8)
            while w2lag <= 2 * e8 - 2:
                w2_emit(0, w2lag)
                w2lag += 1
        while w2lag < 32:
            w2_emit(0, w2lag)
            w2lag += 1
        w2_evac(tp2, 0)
        for dp in range(1, 4):
            w2_evac_pre(tp2, dp)
            for hk in range(32):
                w2_emit(dp, hk)
            w2_evac(tp2, dp)
    st_c3.close()
    st_c_k.close()
    kb.fence()

    st_d = ExitStack()
    ND = 8
    xo = [sb(f"xo{i}", (128, D), F32, st_d) for i in range(ND)]
    ot = [sb(f"ot{i}", (128, D), F32, st_d) for i in range(ND)]
    asl = [sb(f"asl{i}", (128, D), F32, st_d) for i in range(ND)]
    for tt in range(NTT):
        p = tt % ND
        hv = 0 if tt * 128 < T // 2 else 1
        ld(xo[p][:, :], x_d[tt * 128:(tt + 1) * 128, :], sem=f"qX{p % 4}",
           writes=(f"xo{p}",))
        kb.finish_queue(f"qX{p % 4}", [f"xo{p}"])
        src = A(acc_d, tt * 128,
                [[T, 128], [128 * T, 8], [1, 128]])
        dst = A(asl[p], 0, [[D, 128], [128, 8], [1, 128]])
        kb.op("sync", lambda e, d=dst, s=src: e.dma_start(d, s),
              reads=tuple(f"accp{c}_{hv}" for c in range(8)),
              writes=(f"asl{p}",), dma_sem=f"qD{p % 4}")
        kb.finish_queue(f"qD{p % 4}", [f"asl{p}"])
        for c in range(8):
            bk = 6 + (c % 2)
            sl = ((c // 2) % 2) * 128
            kb.op(TE, lambda e, p=p, c=c, bk=bk, sl=sl:
                  e.transpose(pb[bk][:, sl:sl + 128],
                              asl[p][:, c * 128:(c + 1) * 128], iden_f[:, :]),
                  reads=(f"asl{p}", "iden_f"), writes=(f"pb{bk}",))
            kb.op(V, lambda e, p=p, c=c, bk=bk, sl=sl:
                  e.tensor_tensor(ot[p][:, c * 128:(c + 1) * 128],
                                  pb[bk][:, sl:sl + 128],
                                  xo[p][:, c * 128:(c + 1) * 128], ALU.add),
                  reads=(f"pb{bk}", f"xo{p}"), writes=(f"ot{p}",))
        kb.op("sync", lambda e, tt=tt, p=p:
              e.dma_start(out_d[tt * 128:(tt + 1) * 128, :], ot[p][:, :]),
              reads=(f"ot{p}",), writes=("outd",), dma_sem="qOut")
    st_d.close()
    kb.fence()
    kb.wait("sync", "qOut", kb.cnt["qOut"])

    kb.emit()
    kb.stack.close()
    return nc


def kernel(**inputs):
    x = np.ascontiguousarray(np.asarray(inputs["L0_attn_x"], dtype=np.float32))
    T = x.shape[1]
    base = {
        "ln1": inputs["L0_attn_ln1_w"], "Wq": inputs["L0_attn_Wq"],
        "Wk": inputs["L0_attn_Wk"], "Wv": inputs["L0_attn_Wv"],
        "relb": inputs["L0_attn_rel_bias"], "scale": inputs["L0_attn_scale"],
        "Wo": inputs["L0_attn_Wo"], "ln2": inputs["L0_attn_ln2_w"],
        "W1": inputs["L0_attn_W1"], "W2": inputs["L0_attn_W2"],
        "gw": inputs["L0_ffn_gate_w"], "gb": inputs["L0_ffn_gate_b"],
        "ew": inputs["L0_ffn_expert_w"], "eb": inputs["L0_ffn_expert_b"],
    }
    base = {k: np.ascontiguousarray(np.asarray(v, dtype=np.float32))
            for k, v in base.items()}
    base["E"] = _bucket_onehot(T)
    nc = build_nc(T=T)
    in_maps = [dict(base, x=np.ascontiguousarray(x[i]))
               for i in range(x.shape[0])]
    res = run_bass_kernel_spmd(nc, in_maps, core_ids=list(range(len(in_maps))))
    out = np.stack([res.results[i]["out"] for i in range(len(in_maps))], axis=0)
    return out.astype(np.float32)


# version 32
# speedup vs baseline: 1.0279x; 1.0029x over previous
import math
import numpy as np
from contextlib import ExitStack

import concourse.bass as bass
import concourse.mybir as mybir
import concourse.bass_utils as _bass_utils
from concourse.bass_utils import run_bass_kernel_spmd

if not getattr(_bass_utils, "_ldw_opt_patched", False):
    _orig_run_command = _bass_utils.run_command

    def _run_command_ldw(cmd, *a, **kw):
        if isinstance(cmd, (list, tuple)):
            cmd = ["--enable-ldw-opt=true" if c == "--enable-ldw-opt=false"
                   else c for c in cmd]
        return _orig_run_command(cmd, *a, **kw)

    _bass_utils.run_command = _run_command_ldw
    _bass_utils._ldw_opt_patched = True

F32 = mybir.dt.float32
BF16 = mybir.dt.bfloat16
AF = mybir.ActivationFunctionType
ALU = mybir.AluOpType
AX = mybir.AxisListType

D = 1024
H = 16
HID = 4096
NEXP = 10
NTRU = 8
LN_EPS = 1e-5
RMS_EPS = 1e-6
KQ_EPS = 1e-6
NEG = -1e30
BIG = 1e9
QK_BOUND = 1.05


def _bucket_onehot(S):
    n = S + 128
    E = np.zeros((32, n), dtype=np.float32)
    i = np.arange(128, n)
    d = (i - 127).astype(np.int64)
    safe = np.maximum(d, 1).astype(np.float32)
    t = np.log(safe / np.float32(16.0))
    t = t / np.float32(math.log(128.0 / 16.0))
    t = t * np.float32(16.0)
    large = np.minimum(16 + t.astype(np.int32), 31)
    bucket = np.where(d < 16, d, large).astype(np.int64)
    E[bucket, i] = 1.0
    return E


class KB:

    ENGINES = ("sync", "gpsimd", "tensor", "vector", "scalar")

    def __init__(self, nc):
        self.nc = nc
        self.plan = {e: [] for e in self.ENGINES}
        self.sems = {}
        self.cnt = {}
        self.waited = {e: {} for e in self.ENGINES}
        self.bufs = {}
        self.pending = {e: [] for e in self.ENGINES}
        self.eng_sem = {"gpsimd": "G", "tensor": "T", "vector": "V",
                        "scalar": "S", "sync": None}
        self.fence_waits = {}
        self.stack = ExitStack()

    def sem(self, name):
        if name not in self.sems:
            self.sems[name] = self.stack.enter_context(self.nc.semaphore(name))
            self.cnt[name] = 0
        return self.sems[name]

    def op(self, eng, fn, reads=(), writes=(), inc=True, dma_sem=None,
           extra_waits=()):
        waits = list(extra_waits)

        def need(w):
            if w is None:
                return
            if w[0] == "PEND":
                if w[1] != eng:
                    raise RuntimeError(f"cross-engine access to pending buf {w}")
                return
            waits.append(w)

        for b in reads:
            ent = self.bufs.get(b)
            if ent:
                need(ent.get("w"))
        for b in writes:
            ent = self.bufs.get(b)
            if ent:
                need(ent.get("w"))
                for r in ent.get("r", ()):
                    need(r)
                for peng in ent.get("pr", ()):
                    if peng != eng:
                        raise RuntimeError(
                            f"write to {b} with pending readers on {peng}")
        ww = {}
        for s, v in list(waits) + list(self.fence_waits.items()):
            ww[s] = max(ww.get(s, 0), v)
        w_em = []
        for s, v in ww.items():
            if self.waited[eng].get(s, -1) >= v:
                continue
            self.waited[eng][s] = v
            self.sem(s)
            w_em.append((s, v))

        if dma_sem is not None:
            self.sem(dma_sem)
            self.cnt[dma_sem] += 16
            eff = (dma_sem, self.cnt[dma_sem])
        elif inc and self.eng_sem[eng]:
            s = self.eng_sem[eng]
            self.sem(s)
            self.cnt[s] += 1
            eff = (s, self.cnt[s])
        else:
            eff = ("PEND", eng)

        if eff[0] != "PEND":
            for key, kind in self.pending[eng]:
                ent = self.bufs.setdefault(key, {"w": None, "r": [], "pr": set()})
                if kind == "w":
                    ent["w"] = eff
                    ent["r"] = []
                else:
                    ent["r"].append(eff)
                ent["pr"].discard(eng)
            self.pending[eng] = []

        for b in reads:
            ent = self.bufs.setdefault(b, {"w": None, "r": [], "pr": set()})
            if eff[0] == "PEND":
                self.pending[eng].append((b, "r"))
                ent["pr"].add(eng)
            else:
                ent["r"].append(eff)
        for b in writes:
            ent = self.bufs.setdefault(b, {"w": None, "r": [], "pr": set()})
            if eff[0] == "PEND":
                ent["w"] = ("PEND", eng)
                self.pending[eng].append((b, "w"))
                ent["pr"].add(eng)
            else:
                ent["w"] = eff
                ent["r"] = []
        self.plan[eng].append((w_em, fn, None if eff[0] == "PEND" else eff,
                               dma_sem is not None))

    def fence(self):
        for eng in self.ENGINES:
            if self.pending[eng]:
                raise RuntimeError(f"fence with pending ops on {eng}")
        self.fence_waits = dict(self.cnt)

    def finish_queue(self, sem, keys):
        for k in keys:
            ent = self.bufs.setdefault(k, {"w": None, "r": [], "pr": set()})
            ent["w"] = (sem, self.cnt[sem])
            ent["r"] = []

    def wait(self, eng, sem, val):
        self.sem(sem)
        self.plan[eng].append(([(sem, val)], None, None, False))

    def emit(self):
        kb = self
        with kb.nc.Block() as block:
            for eng in self.ENGINES:
                steps = self.plan[eng]
                if not steps:
                    continue

                def body(e, steps=steps):
                    for waits, fn, eff, is_dma in steps:
                        for s, v in waits:
                            e.wait_ge(kb.sems[s], v)
                        if fn is None:
                            continue
                        inst = fn(e)
                        if eff is not None:
                            inst.then_inc(kb.sems[eff[0]], 16 if is_dma else 1)
                getattr(block, eng)(body)


def build_nc(T=2048):
    nc = bass.Bass("TRN2", target_bir_lowering=False)
    NTT = T // 128
    NP = T // 512
    CP = T + 128

    x_d = nc.dram_tensor("x", [T, D], F32, kind="ExternalInput")
    w_d = {}
    for nm, shp in [("ln1", [D]), ("Wq", [D, D]), ("Wk", [D, D]), ("Wv", [D, D]),
                    ("relb", [32, H]), ("scale", [H]), ("Wo", [D, D]),
                    ("ln2", [D]), ("W1", [D, HID]), ("W2", [HID, D]),
                    ("gw", [NEXP, D]), ("gb", [NEXP]),
                    ("ew", [NTRU, D, D]), ("eb", [NTRU, D]),
                    ("E", [32, CP])]:
        w_d[nm] = nc.dram_tensor(nm, shp, F32, kind="ExternalInput")
    out_d = nc.dram_tensor("out", [T, D], F32, kind="ExternalOutput")
    scr_d = nc.dram_tensor("cscr", [H, CP], F32, kind="Internal")
    acc_d = nc.dram_tensor("accscr", [8 * 128, T], F32, kind="Internal")

    kb = KB(nc)
    A = bass.AP
    ES = kb.stack

    def sb(name, shape, dt, st=None):
        return (st or ES).enter_context(nc.sbuf_tensor(name, list(shape), dt))

    pb = [None] * 8
    for i in (0, 1, 2, 3):
        pb[i] = ES.enter_context(nc.psum_tensor(f"pb{i}", [128, 512], F32))
    st_ps_a = ExitStack()
    for i in (4, 5, 6, 7):
        pb[i] = st_ps_a.enter_context(nc.psum_tensor(f"pb{i}a", [128, 512], F32))

    iden_f = sb("iden_f", (128, 128), F32)
    Jmat = sb("Jmat", (128, 128), BF16)
    onesb = sb("onesb", (128, 2), BF16)
    onesf = sb("onesf", (1, 64), F32)
    onesbr = sb("onesbr", (1, 64), BF16)
    iota_f = sb("iota_f", (128, 128), F32)
    iota_p = sb("iota_p", (128, 1), F32)
    iota10 = sb("iota10", (128, 12), F32)
    iotaB = sb("iotaB", (128, 12), F32)
    epsc = sb("epsc", (128, 4), F32)
    selq = sb("selq", (128, 2), BF16)
    sel2T = sb("sel2T", (2, 128), BF16)
    relbS = sb("relbS", (32, H), F32)
    relbT = sb("relbT", (H, 32), F32)
    scale16 = sb("scale16", (H, 2), F32)
    scaleP = sb("scaleP", (2, 8), F32)
    gmaxc = sb("gmaxc", (H, 2), F32)
    gbc = sb("gbc", (NEXP, 1), F32)
    ln1b = sb("ln1b", (1, D), BF16)
    ln2b = sb("ln2b", (1, D), BF16)
    ebCol = sb("ebCol", (128, NTRU * 8), F32)
    g1 = sb("g1", (128, 12), F32)
    g2 = sb("g2", (128, 12), F32)
    g3 = sb("g3", (128, 12), F32)
    gs = sb("gs", (128, 16), F32)

    XN16 = [sb(f"xn16_{c}", (128, T), BF16) for c in range(8)]

    MISC = "qM"

    def ld(dst, src, cast=False, sem=MISC, reads=(), writes=(), silent=False,
           slow=False):
        eng = "gpsimd" if cast else "sync"
        ew = () if silent else (((sem, kb.cnt.get(sem, 0)),)
                                if kb.cnt.get(sem, 0) else ())
        kb.op(eng, lambda e, d=dst, s=src, sl=slow:
              e.dma_start(d, s, allow_slow_non_contiguous=sl),
              reads=() if silent else reads, writes=() if silent else writes,
              dma_sem=sem, extra_waits=ew)

    ld(relbS[:, :], w_d["relb"][:, :], writes=("relbS",))
    ld(relbT[:, :], A(w_d["relb"], 0, [[1, H], [H, 32]]), writes=("relbT",), slow=True)
    ld(scale16[:, 0:1], A(w_d["scale"], 0, [[1, H], [1, 1]]), writes=("scale16",))
    ld(scaleP[:, :], A(w_d["scale"], 0, [[1, 2], [2, 8]]), writes=("scaleP",), slow=True)
    ld(gbc[:, :], A(w_d["gb"], 0, [[1, NEXP], [1, 1]]), writes=("gbc",))
    ld(ln1b[:, :], A(w_d["ln1"], 0, [[D, 1], [1, D]]), cast=True, sem="qC",
       writes=("ln1b",))
    ld(ln2b[:, :], A(w_d["ln2"], 0, [[D, 1], [1, D]]), cast=True, sem="qC",
       writes=("ln2b",))
    for e_ in range(NTRU):
        ld(ebCol[:, e_ * 8:(e_ + 1) * 8], A(w_d["eb"], e_ * D, [[1, 128], [128, 8]]),
           writes=("ebCol",), slow=True)

    G, V, S, TE = "gpsimd", "vector", "scalar", "tensor"
    kb.op(G, lambda e: e.iota(iota_f[:, :], [[1, 128]], channel_multiplier=0,
                              allow_small_or_imprecise_dtypes=True),
          writes=("iota_f",))
    kb.op(G, lambda e: e.iota(iota_p[:, :], [[0, 1]], channel_multiplier=1,
                              allow_small_or_imprecise_dtypes=True),
          writes=("iota_p",))
    kb.op(G, lambda e: e.iota(iota10[:, 0:NEXP], [[1, NEXP]], channel_multiplier=0,
                              allow_small_or_imprecise_dtypes=True),
          writes=("iota10",))
    kb.op(V, lambda e: e.tensor_scalar(iotaB[:, 0:NEXP], iota10[:, 0:NEXP],
                                       1.0e4, None, ALU.add),
          reads=("iota10",), writes=("iotaB",))
    kb.op(G, lambda e: e.memset(onesb[:, :], 1.0), writes=("onesb",))
    kb.op(G, lambda e: e.memset(onesf[:, :], 1.0), writes=("onesf",))
    kb.op(G, lambda e: e.memset(onesbr[:, :], 1.0), writes=("onesbr",))
    kb.op(G, lambda e: e.memset(epsc[:, 0:1], LN_EPS), writes=("epsc",))
    kb.op(G, lambda e: e.memset(epsc[:, 1:2], RMS_EPS), writes=("epsc",))
    kb.op(G, lambda e: e.memset(epsc[:, 2:3], KQ_EPS), writes=("epsc",))
    kb.op(G, lambda e: e.memset(epsc[:, 3:4], 1e-38), writes=("epsc",))
    kb.op(V, lambda e: e.tensor_scalar(iden_f[:, :], iota_f[:, :], iota_p[:, :],
                                       None, ALU.is_equal),
          reads=("iota_f", "iota_p"), writes=("iden_f",))
    kb.op(V, lambda e: e.tensor_scalar(Jmat[:, :], iota_f[:, :], iota_p[:, :],
                                       127.0, ALU.add, ALU.is_equal),
          reads=("iota_f", "iota_p"), writes=("Jmat",))
    s2i = sb("s2i", (2, 128), F32)
    sel2F = sb("sel2F", (2, 128), F32)
    ip2 = sb("ip2", (2, 1), F32)
    kb.op(G, lambda e: e.iota(s2i[:, :], [[1, 2], [0, 64]], channel_multiplier=0,
                              allow_small_or_imprecise_dtypes=True),
          writes=("s2i",))
    kb.op(G, lambda e: e.iota(ip2[:, :], [[0, 1]], channel_multiplier=1,
                              allow_small_or_imprecise_dtypes=True),
          writes=("ip2",))
    kb.op(V, lambda e: e.tensor_scalar(sel2T[:, :], s2i[:, :], ip2[:, :], None,
                                       ALU.is_equal),
          reads=("s2i", "ip2"), writes=("sel2T",))
    kb.op(V, lambda e: e.tensor_scalar(sel2F[:, :], s2i[:, :], ip2[:, :], None,
                                       ALU.is_equal),
          reads=("s2i", "ip2"), writes=("sel2F",))
    kb.op(TE, lambda e: e.transpose(pb[0][:, 0:2], sel2F[:, :],
                                    iden_f[0:2, 0:2]),
          reads=("sel2F", "iden_f"), writes=("pb0",))
    kb.op(V, lambda e: e.tensor_copy(selq[:, :], pb[0][:, 0:2]),
          reads=("pb0",), writes=("selq",))

    st_cw = ExitStack()
    cwT = sb("cwT", (NTRU, T), BF16, st_cw)
    ohE = sb("ohE", (NTRU, NTRU * 128), BF16, st_cw)
    st_a = ExitStack()
    XNF = [sb(f"xnf{c}", (128, T), F32, st_a) for c in range(8)]
    st_a2 = ExitStack()
    EsS = sb("EsS", (32, CP), F32, st_a2)
    cpadS = sb("cpadS", (H, CP), F32, st_a2)
    xt = [sb(f"xt{i}", (128, D), F32, st_a2) for i in range(2)]
    xnt = [sb(f"xnt{i}", (128, D), F32, st_a2) for i in range(2)]
    sqt = [sb(f"sqt{i}", (128, D), F32, st_a2) for i in range(2)]
    st_s = sb("st_s", (128, 16), F32, st_a2)

    ld(EsS[:, :], w_d["E"][:, :], writes=("EsS",))
    kb.finish_queue(MISC, ["relbS", "relbT", "scale16", "scaleP", "gbc",
                           "ebCol", "EsS"])
    kb.finish_queue("qC", ["ln1b", "ln2b"])
    kb.op(S, lambda e: e.activation(scale16[:, 1:2], scale16[:, 0:1], AF.Abs),
          reads=("scale16",), writes=("scale16",))
    kb.op(V, lambda e: e.reduce_max(gmaxc[:, 0:1], relbT[:, :], axis=AX.X,
                                    apply_absolute_value=True),
          reads=("relbT",), writes=("gmaxc",))
    kb.op(V, lambda e: e.tensor_scalar(gmaxc[:, 0:1], gmaxc[:, 0:1], QK_BOUND,
                                       None, ALU.add),
          reads=("gmaxc",), writes=("gmaxc",))
    kb.op(V, lambda e: e.tensor_tensor(gmaxc[:, 1:2], gmaxc[:, 0:1],
                                       scale16[:, 1:2], ALU.mult),
          reads=("gmaxc", "scale16"), writes=("gmaxc",))
    ofs = 0
    while ofs < CP:
        w = min(512, CP - ofs)
        bk = (ofs // 512) % 2
        kb.op(TE, lambda e, o=ofs, w=w, bk=bk:
              e.matmul(pb[bk][0:H, 0:w], relbS[:, :], EsS[:, o:o + w],
                       start=True, stop=True),
              reads=("relbS", "EsS"), writes=(f"pb{bk}",))
        kb.op(V, lambda e, o=ofs, w=w, bk=bk:
              e.tensor_scalar(cpadS[:, o:o + w], pb[bk][0:H, 0:w],
                              scale16[:, 0:1], gmaxc[:, 1:2],
                              ALU.mult, ALU.subtract),
              reads=(f"pb{bk}", "scale16", "gmaxc"), writes=("cpadS",))
        ofs += w
    kb.op(V, lambda e: e.memset(cpadS[:, 0:128], NEG), writes=("cpadS",))
    kb.op(S, lambda e: e.activation(cpadS[:, :], cpadS[:, :], AF.Exp),
          reads=("cpadS",), writes=("cpadS",))
    ld(scr_d[:, :], cpadS[:, :], sem="qS", reads=("cpadS",), writes=("cscr",))

    for tt in range(NTT):
        p = tt % 2
        o = p * 8
        sk = f"st_s{p}"
        ld(xt[p][:, :], x_d[tt * 128:(tt + 1) * 128, :], sem=f"qX{p}",
           writes=(f"xt{p}",))
        kb.op(V, lambda e, p=p, o=o:
              e.reduce_sum(st_s[:, o:o + 1], xt[p][:, :], axis=AX.X),
              reads=(f"xt{p}",), writes=(sk,))
        kb.op(V, lambda e, p=p: e.tensor_tensor(sqt[p][:, :], xt[p][:, :],
                                                xt[p][:, :], ALU.mult),
              reads=(f"xt{p}",), writes=(f"sqt{p}",))
        kb.op(V, lambda e, p=p, o=o:
              e.reduce_sum(st_s[:, o + 1:o + 2], sqt[p][:, :], axis=AX.X),
              reads=(f"sqt{p}",), writes=(sk,))
        kb.op(V, lambda e, o=o:
              e.tensor_scalar(st_s[:, o + 2:o + 3], st_s[:, o:o + 1], 1.0 / D,
                              None, ALU.mult),
              reads=(sk,), writes=(sk,))
        kb.op(V, lambda e, o=o:
              e.tensor_tensor(st_s[:, o + 3:o + 4], st_s[:, o + 2:o + 3],
                              st_s[:, o + 2:o + 3], ALU.mult),
              reads=(sk,), writes=(sk,))
        kb.op(V, lambda e, o=o:
              e.tensor_scalar(st_s[:, o + 4:o + 5], st_s[:, o + 1:o + 2],
                              1.0 / D, st_s[:, o + 3:o + 4],
                              ALU.mult, ALU.subtract),
              reads=(sk,), writes=(sk,))
        kb.op(S, lambda e, o=o:
              e.activation(st_s[:, o + 5:o + 6], st_s[:, o + 4:o + 5], AF.Ln,
                           bias=epsc[:, 0:1]),
              reads=(sk, "epsc"), writes=(sk,))
        kb.op(S, lambda e, o=o:
              e.activation(st_s[:, o + 6:o + 7], st_s[:, o + 5:o + 6], AF.Exp,
                           scale=-0.5),
              reads=(sk,), writes=(sk,))
        kb.op(V, lambda e, p=p, o=o:
              e.tensor_scalar(xnt[p][:, :], xt[p][:, :],
                              st_s[:, o + 2:o + 3], st_s[:, o + 6:o + 7],
                              ALU.subtract, ALU.mult),
              reads=(f"xt{p}", sk), writes=(f"xnt{p}",))
        for c in range(8):
            bk = 6 + (c % 2)
            sl = ((c // 2) % 2) * 128
            kb.op(TE, lambda e, p=p, c=c, bk=bk, sl=sl:
                  e.transpose(pb[bk][:, sl:sl + 128],
                              xnt[p][:, c * 128:(c + 1) * 128], iden_f[:, :]),
                  reads=(f"xnt{p}", "iden_f"), writes=(f"pb{bk}",))
            kb.op(S, lambda e, tt=tt, c=c, bk=bk, sl=sl:
                  e.copy(XNF[c][:, tt * 128:(tt + 1) * 128],
                         pb[bk][:, sl:sl + 128]),
                  reads=(f"pb{bk}",), writes=(f"xnf{c}",))
    for c in range(8):
        kb.op(G, lambda e, c=c: e.tensor_copy(XN16[c][:, :], XNF[c][:, :]),
              reads=(f"xnf{c}",), writes=(f"xn16_{c}",))
    st_a2.close()
    kb.fence()

    st_b1 = ExitStack()
    gwS = sb("gwS", (NEXP, D), F32, st_b1)
    gwT = sb("gwT", (128, 80), F32, st_b1)
    lgS = sb("lgS", (NEXP, T), F32, st_b1)
    lgT = sb("lgT", (128, NTT * 16), F32, st_b1)
    cwTok = sb("cwTok", (128, NTT * 8), F32, st_b1)

    ld(gwS[:, :], w_d["gw"][:, :], writes=("gwS",))
    kb.finish_queue(MISC, ["gwS"])
    oh_i = sb("oh_i", (NTRU, NTRU * 128), F32, st_b1)
    ip8 = sb("ip8", (NTRU, 1), F32, st_b1)
    kb.op(G, lambda e: e.iota(oh_i[:, :], [[1, NTRU], [0, 128]],
                              channel_multiplier=0,
                              allow_small_or_imprecise_dtypes=True),
          writes=("oh_i",))
    kb.op(G, lambda e: e.iota(ip8[:, :], [[0, 1]], channel_multiplier=1,
                              allow_small_or_imprecise_dtypes=True),
          writes=("ip8",))
    kb.op(V, lambda e: e.tensor_scalar(ohE[:, :], oh_i[:, :], ip8[:, :], None,
                                       ALU.is_equal),
          reads=("oh_i", "ip8"), writes=("ohE",))
    for c in range(8):
        bk = c % 2
        kb.op(TE, lambda e, c=c, bk=bk:
              e.transpose(pb[bk][:, 0:NEXP], gwS[:, c * 128:(c + 1) * 128],
                          iden_f[0:NEXP, 0:NEXP]),
              reads=("gwS", "iden_f"), writes=(f"pb{bk}",))
        kb.op(V, lambda e, c=c, bk=bk:
              e.tensor_copy(gwT[:, c * 10:(c + 1) * 10], pb[bk][:, 0:NEXP]),
              reads=(f"pb{bk}",), writes=("gwT",))
    for pp in range(NP):
        for c in range(8):
            kb.op(TE, lambda e, pp=pp, c=c:
                  e.matmul(pb[4 + pp % 4][0:NEXP, 0:512],
                           gwT[:, c * 10:(c + 1) * 10],
                           XNF[c][:, pp * 512:(pp + 1) * 512],
                           start=(c == 0), stop=(c == 7)),
                  reads=("gwT", f"xnf{c}"), writes=(f"pb{4 + pp % 4}",),
                  inc=(c == 7))
        kb.op(V, lambda e, pp=pp:
              e.tensor_scalar(lgS[:, pp * 512:(pp + 1) * 512],
                              pb[4 + pp % 4][0:NEXP, 0:512], gbc[:, 0:1],
                              None, ALU.add),
              reads=(f"pb{4 + pp % 4}", "gbc"), writes=("lgS",))
    for tt in range(NTT):
        bk = tt % 2
        kb.op(TE, lambda e, tt=tt, bk=bk:
              e.transpose(pb[bk][:, 0:NEXP], lgS[:, tt * 128:(tt + 1) * 128],
                          iden_f[0:NEXP, 0:NEXP]),
              reads=("lgS", "iden_f"), writes=(f"pb{bk}",))
        kb.op(V, lambda e, tt=tt, bk=bk:
              e.tensor_copy(lgT[:, tt * 16:tt * 16 + NEXP], pb[bk][:, 0:NEXP]),
              reads=(f"pb{bk}",), writes=("lgT",))
    gcol = sb("gcol", (128, 16 * 8), F32, st_b1)
    gw1 = sb("gw1", (128, NTT * 16), F32, st_b1)
    gw2 = sb("gw2", (128, NTT * 16), F32, st_b1)
    gw3 = sb("gw3", (128, NTT * 16), F32, st_b1)

    def d3(h, width=16, n=10):
        return A(h, 0, [[NTT * width, 128], [width, NTT], [1, n]])

    def b3(slot):
        return A(gcol, slot * NTT, [[16 * 8, 128], [1, NTT], [0, 10]])

    def gcolf(slot):
        return gcol[:, slot * NTT:(slot + 1) * NTT]

    iota3 = A(iota10, 0, [[12, 128], [0, NTT], [1, 10]])
    iotaB3 = A(iotaB, 0, [[12, 128], [0, NTT], [1, 10]])

    def vg(fn, reads, writes):
        kb.op(V, fn, reads=reads, writes=writes)

    vg(lambda e: e.tensor_reduce(gcolf(0), d3(lgT), AX.X, ALU.max),
       ("lgT",), ("gcol",))
    vg(lambda e: e.tensor_tensor(d3(gw1), d3(lgT), b3(0), ALU.subtract),
       ("lgT", "gcol"), ("gw1",))
    kb.op(S, lambda e: e.activation(d3(gw1), d3(gw1), AF.Exp),
          reads=("gw1",), writes=("gw1",))
    vg(lambda e: e.tensor_reduce(gcolf(1), d3(gw1), AX.X, ALU.add),
       ("gw1",), ("gcol",))
    vg(lambda e: e.reciprocal(gcolf(1), gcolf(1)), ("gcol",), ("gcol",))
    vg(lambda e: e.tensor_tensor(d3(gw1), d3(gw1), b3(1), ALU.mult),
       ("gw1", "gcol"), ("gw1",))
    vg(lambda e: e.tensor_reduce(gcolf(2), d3(gw1), AX.X, ALU.max),
       ("gw1",), ("gcol",))
    vg(lambda e: e.tensor_tensor(d3(gw2), d3(gw1), b3(2), ALU.is_equal),
       ("gw1", "gcol"), ("gw2",))
    vg(lambda e: e.scalar_tensor_tensor(d3(gw3), d3(gw2), -1.0e4, iotaB3,
                                        ALU.mult, ALU.add),
       ("gw2", "iotaB"), ("gw3",))
    vg(lambda e: e.tensor_reduce(gcolf(3), d3(gw3), AX.X, ALU.min),
       ("gw3",), ("gcol",))
    vg(lambda e: e.tensor_tensor(d3(gw2), iota3, b3(3), ALU.is_equal),
       ("iota10", "gcol"), ("gw2",))
    vg(lambda e: e.scalar_tensor_tensor(d3(gw3), d3(gw2), -BIG, d3(gw1),
                                        ALU.mult, ALU.add),
       ("gw2", "gw1"), ("gw3",))
    vg(lambda e: e.tensor_reduce(gcolf(4), d3(gw3), AX.X, ALU.max),
       ("gw3",), ("gcol",))
    vg(lambda e: e.tensor_tensor(d3(gw1), d3(gw3), b3(4), ALU.is_equal),
       ("gw3", "gcol"), ("gw1",))
    vg(lambda e: e.scalar_tensor_tensor(d3(gw3), d3(gw1), -1.0e4, iotaB3,
                                        ALU.mult, ALU.add),
       ("gw1", "iotaB"), ("gw3",))
    vg(lambda e: e.tensor_reduce(gcolf(5), d3(gw3), AX.X, ALU.min),
       ("gw3",), ("gcol",))
    vg(lambda e: e.tensor_tensor(d3(gw3), iota3, b3(5), ALU.is_equal),
       ("iota10", "gcol"), ("gw3",))
    vg(lambda e: e.tensor_scalar(gcolf(6), gcolf(3), 7.5, None, ALU.is_lt),
       ("gcol",), ("gcol",))
    vg(lambda e: e.tensor_tensor(gcolf(6), gcolf(6), gcolf(2), ALU.mult),
       ("gcol",), ("gcol",))
    vg(lambda e: e.tensor_scalar(gcolf(7), gcolf(5), 7.5, None, ALU.is_lt),
       ("gcol",), ("gcol",))
    vg(lambda e: e.tensor_tensor(gcolf(7), gcolf(7), gcolf(4), ALU.mult),
       ("gcol",), ("gcol",))
    vg(lambda e: e.tensor_tensor(gcolf(2), gcolf(6), gcolf(7), ALU.add),
       ("gcol",), ("gcol",))
    vg(lambda e: e.tensor_scalar(gcolf(3), gcolf(2), 0.0, None, ALU.is_le),
       ("gcol",), ("gcol",))
    vg(lambda e: e.tensor_tensor(gcolf(2), gcolf(2), gcolf(3), ALU.add),
       ("gcol",), ("gcol",))
    vg(lambda e: e.reciprocal(gcolf(2), gcolf(2)), ("gcol",), ("gcol",))
    vg(lambda e: e.tensor_tensor(gcolf(6), gcolf(6), gcolf(2), ALU.mult),
       ("gcol",), ("gcol",))
    vg(lambda e: e.tensor_tensor(gcolf(7), gcolf(7), gcolf(2), ALU.mult),
       ("gcol",), ("gcol",))
    vg(lambda e: e.tensor_tensor(d3(gw2, n=8), d3(gw2, n=8),
                                 A(gcol, 6 * NTT, [[16 * 8, 128], [1, NTT], [0, 8]]),
                                 ALU.mult),
       ("gw2", "gcol"), ("gw2",))
    vg(lambda e: e.tensor_tensor(d3(gw3, n=8), d3(gw3, n=8),
                                 A(gcol, 7 * NTT, [[16 * 8, 128], [1, NTT], [0, 8]]),
                                 ALU.mult),
       ("gw3", "gcol"), ("gw3",))
    vg(lambda e: e.tensor_tensor(A(cwTok, 0, [[NTT * 8, 128], [8, NTT], [1, 8]]),
                                 d3(gw2, n=8), d3(gw3, n=8), ALU.add),
       ("gw2", "gw3"), ("cwTok",))
    for tt in range(NTT):
        bk = tt % 2
        kb.op(TE, lambda e, tt=tt, bk=bk:
              e.transpose(pb[bk][0:8, 0:128], cwTok[:, tt * 8:(tt + 1) * 8],
                          iden_f[:, :]),
              reads=("cwTok", "iden_f"), writes=(f"pb{bk}",))
        kb.op(V, lambda e, tt=tt, bk=bk:
              e.tensor_copy(cwT[:, tt * 128:(tt + 1) * 128], pb[bk][0:8, 0:128]),
              reads=(f"pb{bk}",), writes=("cwT",))
    st_b1.close()
    st_a.close()
    kb.fence()

    st_b2 = ExitStack()
    ACC = [sb(f"acc{c}", (128, T), F32, st_b2) for c in range(8)]
    cwbA = [sb(f"cwb{e}", (128, T), BF16, st_b2) for e in range(NTRU)]
    mtmp = [sb(f"mtmp{i}", (128, T), F32, st_b2) for i in range(2)]
    We16 = [sb(f"we16_{i}", (128, 4096), BF16, st_b2) for i in range(2)]

    for e_ in range(NTRU):
        for pp in range(NP):
            bk = pp % 2
            kb.op(TE, lambda e, e_=e_, pp=pp, bk=bk:
                  e.matmul(pb[bk][:, 0:512], ohE[:, e_ * 128:(e_ + 1) * 128],
                           cwT[:, pp * 512:(pp + 1) * 512],
                           start=True, stop=True),
                  reads=("ohE", "cwT"), writes=(f"pb{bk}",))
            kb.op(V, lambda e, e_=e_, pp=pp, bk=bk:
                  e.tensor_copy(cwbA[e_][:, pp * 512:(pp + 1) * 512],
                                pb[bk][:, 0:512]),
                  reads=(f"pb{bk}",), writes=(f"cwb{e_}",))

    def ew_load(e_, hlf, par):
        for k in range(8):
            ld(We16[par][:, k * 512:(k + 1) * 512],
               w_d["ew"][e_, k * 128:(k + 1) * 128, hlf * 512:(hlf + 1) * 512],
               cast=True, sem=f"qW{par}", writes=(f"we16_{par}",),
               silent=(k > 0))
        kb.finish_queue(f"qW{par}", [f"we16_{par}"])

    ew_load(0, 0, 0)
    for e_ in range(NTRU):
        for hlf in range(2):
            par = hlf
            nxt_e, nxt_h = (e_, 1) if hlf == 0 else (e_ + 1, 0)
            if nxt_e < NTRU:
                ew_load(nxt_e, nxt_h, 1 - hlf)
            for dj in range(4):
                dout = hlf * 4 + dj
                col = dj * 128
                pbase = (dout % 2) * 4
                for k in range(8):
                    for pp in range(NP):
                        bk = pbase + pp
                        kb.op(TE, lambda e, par=par, k=k, col=col, pp=pp, bk=bk:
                              e.matmul(pb[bk][:, 0:512],
                                       We16[par][:, k * 512 + col:
                                                 k * 512 + col + 128],
                                       XN16[k][:, pp * 512:(pp + 1) * 512],
                                       start=(k == 0), stop=(k == 7)),
                              reads=(f"we16_{par}", f"xn16_{k}"),
                              writes=(f"pb{bk}",), inc=(k == 7))
                mp = dout % 2
                for pp in range(NP):
                    bk = pbase + pp
                    if e_ == 0:
                        kb.op(V, lambda e, e_=e_, dout=dout, pp=pp, bk=bk:
                              e.scalar_tensor_tensor(
                                  ACC[dout][:, pp * 512:(pp + 1) * 512],
                                  pb[bk][:, 0:512],
                                  ebCol[:, e_ * 8 + dout:e_ * 8 + dout + 1],
                                  cwbA[e_][:, pp * 512:(pp + 1) * 512],
                                  ALU.add, ALU.mult),
                              reads=(f"pb{bk}", "ebCol", f"cwb{e_}"),
                              writes=(f"acc{dout}",))
                    else:
                        kb.op(V, lambda e, e_=e_, dout=dout, pp=pp, bk=bk,
                              mp=mp:
                              e.scalar_tensor_tensor(
                                  mtmp[mp][:, pp * 512:(pp + 1) * 512],
                                  pb[bk][:, 0:512],
                                  ebCol[:, e_ * 8 + dout:e_ * 8 + dout + 1],
                                  cwbA[e_][:, pp * 512:(pp + 1) * 512],
                                  ALU.add, ALU.mult),
                              reads=(f"pb{bk}", "ebCol", f"cwb{e_}"),
                              writes=(f"mtmp{mp}",))
                if e_ > 0:
                    kb.op(G, lambda e, dout=dout, mp=mp:
                          e.tensor_tensor(ACC[dout][:, :], ACC[dout][:, :],
                                          mtmp[mp][:, :], ALU.add),
                          reads=(f"acc{dout}", f"mtmp{mp}"),
                          writes=(f"acc{dout}",))
    for c in range(8):
        ld(acc_d[c * 128:(c + 1) * 128, :], ACC[c][:, :], sem="qA0",
           reads=(f"acc{c}",), writes=(f"accp{c}",))
    st_b2.close()
    st_cw.close()
    kb.fence()

    st_c_k = ExitStack()
    K16 = [sb(f"k16_{c}", (128, T), BF16, st_c_k) for c in range(8)]
    st_c_bc = ExitStack()
    Y16p = [sb(f"y16p_{c}", (128, T), BF16, st_c_bc) for c in range(8)]
    Q16 = [sb(f"q16_{c}", (128, T), BF16, st_c_bc) for c in range(8)]
    st_c_vt = ExitStack()
    VtL = [sb(f"vtl{t}", (128, H * 65), BF16, st_c_vt) for t in range(NTT)]
    st_c0 = ExitStack()
    sqP = [sb(f"sq{i}", (128, T), BF16, st_c0) for i in range(2)]
    Vf32 = sb("vf32", (128, T), F32, st_c0)
    rowscrP = sb("rowscrP", (2, 512), F32, st_c0)
    rowb16 = [sb(f"rowb16_{i}", (2, T), BF16, st_c0) for i in range(2)]
    Wst = [sb(f"wst{i}", (128, 1024), BF16, st_c0) for i in range(2)]

    XN2 = Y16p

    def wload(par, dram, r0, c0, c1, key, sem=None):
        for k in range(8):
            ld(Wst[par][:, k * (c1 - c0):(k + 1) * (c1 - c0)],
               dram[r0 + k * 128:r0 + (k + 1) * 128, c0:c1], cast=True,
               sem=sem or f"qW{par}", writes=(key,), silent=(k > 0))
        kb.finish_queue(sem or f"qW{par}", [key])

    wload(0, w_d["Wq"], 0, 0, 128, "wst0")
    wload(1, w_d["Wq"], 0, 128, 256, "wst1")

    for c in range(8):
        kb.op(V, lambda e, c=c: e.tensor_tensor(sqP[c % 2][:, :], XN16[c][:, :],
                                                XN16[c][:, :], ALU.mult),
              reads=(f"xn16_{c}",), writes=(f"sq{c % 2}",))
        for pp in range(NP):
            bk = 4 + pp
            kb.op(TE, lambda e, c=c, pp=pp, bk=bk:
                  e.matmul(pb[bk][0:1, 0:512], onesb[:, 0:1],
                           sqP[c % 2][:, pp * 512:(pp + 1) * 512],
                           start=(c == 0), stop=(c == 7)),
                  reads=(f"sq{c % 2}", "onesb"), writes=(f"pb{bk}",), inc=True)
    for pp in range(NP):
        kb.op(S, lambda e, pp=pp:
              e.activation(rowscrP[0:1, :], pb[4 + pp][0:1, 0:512], AF.Ln,
                           bias=epsc[0:1, 1:2], scale=1.0 / D),
              reads=(f"pb{4 + pp}", "epsc"), writes=("rowscrP",))
        kb.op(S, lambda e, pp=pp:
              e.activation(rowb16[0][0:1, pp * 512:(pp + 1) * 512],
                           rowscrP[0:1, :], AF.Exp, scale=-0.5),
              reads=("rowscrP",), writes=("rowb16_0",))
    for c in range(8):
        for pp in range(NP):
            bk = 4 + pp % 2
            kb.op(TE, lambda e, c=c, pp=pp, bk=bk:
                  e.matmul(pb[bk][:, 0:512], ln1b[:, c * 128:(c + 1) * 128],
                           rowb16[0][0:1, pp * 512:(pp + 1) * 512],
                           start=True, stop=True),
                  reads=("ln1b", "rowb16_0"), writes=(f"pb{bk}",))
            kb.op(V, lambda e, c=c, pp=pp, bk=bk:
                  e.tensor_tensor(XN2[c][:, pp * 512:(pp + 1) * 512],
                                  XN16[c][:, pp * 512:(pp + 1) * 512],
                                  pb[bk][:, 0:512], ALU.mult),
                  reads=(f"xn16_{c}", f"pb{bk}"), writes=(f"y16p_{c}",))
        pbase0 = 0 if NP > 2 else 0
        for pp in range(NP):
            kb.op(TE, lambda e, c=c, pp=pp:
                  e.matmul(pb[pp][:, 0:512], Wst[0][:, c * 128:(c + 1) * 128],
                           XN2[c][:, pp * 512:(pp + 1) * 512],
                           start=(c == 0), stop=(c == 7)),
                  reads=("wst0", f"y16p_{c}"), writes=(f"pb{pp}",),
                  inc=(c == 7))
    for pp in range(NP):
        kb.op(V, lambda e, pp=pp:
              e.tensor_copy(Q16[0][:, pp * 512:(pp + 1) * 512],
                            pb[pp][:, 0:512]),
              reads=(f"pb{pp}",), writes=("q16_0",))

    def proj_unit(wname, dst, dkey, dout, nxt):
        def go():
            par = dout % 2
            if nxt is not None:
                wload(1 - par, w_d[nxt[0]], 0, nxt[1] * 128,
                      (nxt[1] + 1) * 128, f"wst{1 - par}")
            pbase = (dout % 2) * 2 if NP <= 2 else 0
            for k in range(8):
                for pp in range(NP):
                    bk = pbase + pp if NP <= 2 else pp
                    kb.op(TE, lambda e, par=par, k=k, pp=pp, bk=bk:
                          e.matmul(pb[bk][:, 0:512],
                                   Wst[par][:, k * 128:(k + 1) * 128],
                                   XN2[k][:, pp * 512:(pp + 1) * 512],
                                   start=(k == 0), stop=(k == 7)),
                          reads=(f"wst{par}", f"y16p_{k}"),
                          writes=(f"pb{bk}",), inc=(k == 7))
            for pp in range(NP):
                bk = (pbase + pp) if NP <= 2 else pp
                if dst is None:
                    kb.op(V, lambda e, pp=pp, bk=bk:
                          e.tensor_copy(Vf32[:, pp * 512:(pp + 1) * 512],
                                        pb[bk][:, 0:512]),
                          reads=(f"pb{bk}",), writes=("vf32",))
                else:
                    kb.op(V, lambda e, dout=dout, pp=pp, bk=bk:
                          e.tensor_copy(dst[dout][:, pp * 512:(pp + 1) * 512],
                                        pb[bk][:, 0:512]),
                          reads=(f"pb{bk}",), writes=(dkey.format(dout),))
        return go

    def qn_sq(XB, keyfmt, c):
        pr = c % 2
        kb.op(V, lambda e, c=c, pr=pr:
              e.tensor_tensor(sqP[pr][:, :], XB[c][:, :], XB[c][:, :],
                              ALU.mult),
              reads=(keyfmt.format(c),), writes=(f"sq{pr}",))

    def qn_stats(XB, keyfmt, c, fold_scale, bank0):
        pr = c % 2
        for pp in range(NP):
            bk = bank0 + pp % 2
            kb.op(TE, lambda e, pr=pr, pp=pp, bk=bk:
                  e.matmul(pb[bk][0:2, 0:512], selq[:, :],
                           sqP[pr][:, pp * 512:(pp + 1) * 512],
                           start=True, stop=True),
                  reads=(f"sq{pr}", "selq"), writes=(f"pb{bk}",))
            kb.op(S, lambda e, bk=bk:
                  e.activation(rowscrP[0:2, :], pb[bk][0:2, 0:512], AF.Ln,
                               bias=epsc[0:2, 2:3]),
                  reads=(f"pb{bk}", "epsc"), writes=("rowscrP",))
            kb.op(S, lambda e, pr=pr, pp=pp:
                  e.activation(rowb16[pr][0:2, pp * 512:(pp + 1) * 512],
                               rowscrP[0:2, :], AF.Exp, scale=-0.5),
                  reads=("rowscrP",), writes=(f"rowb16_{pr}",))
            if fold_scale:
                kb.op(V, lambda e, c=c, pr=pr, pp=pp:
                      e.tensor_scalar(rowb16[pr][0:2, pp * 512:(pp + 1) * 512],
                                      rowb16[pr][0:2, pp * 512:(pp + 1) * 512],
                                      scaleP[:, c:c + 1], None, ALU.mult),
                      reads=(f"rowb16_{pr}", "scaleP"),
                      writes=(f"rowb16_{pr}",))

    def qn_apply_piece(XB, keyfmt, c, pp, bk):
        pr = c % 2
        kb.op(TE, lambda e, pr=pr, pp=pp, bk=bk:
              e.matmul(pb[bk][:, 0:512], sel2T[:, :],
                       rowb16[pr][0:2, pp * 512:(pp + 1) * 512],
                       start=True, stop=True),
              reads=(f"rowb16_{pr}", "sel2T"), writes=(f"pb{bk}",))
        kb.op(V, lambda e, c=c, pp=pp, bk=bk:
              e.tensor_tensor(XB[c][:, pp * 512:(pp + 1) * 512],
                              XB[c][:, pp * 512:(pp + 1) * 512],
                              pb[bk][:, 0:512], ALU.mult),
              reads=(keyfmt.format(c), f"pb{bk}"), writes=(keyfmt.format(c),))

    for d in range(1, 8):
        nxt = ("Wq", d + 1) if d < 7 else ("Wk", 0)
        proj_unit("Wq", Q16, "q16_{}", d, nxt)()
    for d in range(8):
        nxt = ("Wk", d + 1) if d < 7 else ("Wv", 0)
        qn_sq(Q16, "q16_{}", d)
        proj_unit("Wk", K16, "k16_{}", d, nxt)()
        qn_stats(Q16, "q16_{}", d, True, 4)
        if d > 0:
            for pp in range(NP):
                qn_apply_piece(Q16, "q16_{}", d - 1, pp, 6 + pp % 2)
    for pp in range(NP):
        qn_apply_piece(Q16, "q16_{}", 7, pp, 6 + pp % 2)

    def v_transpose(c, tt):
        reg = (tt % 4) * 128
        kb.op(TE, lambda e, tt=tt, reg=reg:
              e.transpose(pb[4][:, reg:reg + 128],
                          Vf32[:, tt * 128:(tt + 1) * 128], iden_f[:, :]),
              reads=("vf32", "iden_f"), writes=("pb4",))
        for hf, eng in ((0, V), (1, V)):
            hh = 2 * c + hf
            kb.op(eng, lambda e, tt=tt, reg=reg, hf=hf, hh=hh:
                  e.tensor_copy(VtL[tt][:, hh * 65:hh * 65 + 64],
                                pb[4][:, reg + hf * 64:reg + hf * 64 + 64]),
                  reads=("pb4",), writes=(f"vtl{tt}",))

    for d in range(8):
        par = d % 2
        qn_sq(K16, "k16_{}", d)
        if d < 7:
            wload(1 - par, w_d["Wv"], 0, (d + 1) * 128, (d + 2) * 128,
                  f"wst{1 - par}")
        pbase = (d % 2) * 2 if NP <= 2 else 0
        for k in range(8):
            for pp in range(NP):
                bk = pbase + pp if NP <= 2 else pp
                kb.op(TE, lambda e, par=par, k=k, pp=pp, bk=bk:
                      e.matmul(pb[bk][:, 0:512],
                               Wst[par][:, k * 128:(k + 1) * 128],
                               XN2[k][:, pp * 512:(pp + 1) * 512],
                               start=(k == 0), stop=(k == 7)),
                      reads=(f"wst{par}", f"y16p_{k}"),
                      writes=(f"pb{bk}",), inc=(k == 7))
            if d > 0 and 2 * k < NTT:
                v_transpose(d - 1, 2 * k)
                v_transpose(d - 1, 2 * k + 1)
            if d > 0 and k % 2 == 1 and (k - 1) // 2 < NP:
                qn_apply_piece(K16, "k16_{}", d - 1, (k - 1) // 2, 7)
        for pp in range(NP):
            bk = pbase + pp if NP <= 2 else pp
            kb.op(V, lambda e, pp=pp, bk=bk:
                  e.tensor_copy(Vf32[:, pp * 512:(pp + 1) * 512],
                                pb[bk][:, 0:512]),
                  reads=(f"pb{bk}",), writes=("vf32",))
        qn_stats(K16, "k16_{}", d, False, 5)
    for tt in range(NTT):
        v_transpose(7, tt)
    for pp in range(NP):
        qn_apply_piece(K16, "k16_{}", 7, pp, 7)
    for tt in range(NTT):
        kb.op(V, lambda e, tt=tt:
              e.memset(A(VtL[tt], 64, [[H * 65, 128], [65, H]]), 1.0),
              writes=(f"vtl{tt}",))
    st_c0.close()
    kb.fence()

    st_ps_a.close()
    st_ps_c1 = ExitStack()
    s16 = [st_ps_c1.enter_context(nc.psum_tensor(f"s16_{i}", [128, 512], F32))
           for i in range(3)]
    pbc = [st_ps_c1.enter_context(nc.psum_tensor("pbc0", [128, 512], F32))]
    st_c1 = ExitStack()
    Tst = [sb(f"tst{i}", (128, T), BF16, st_c1) for i in range(2)]
    Pt = [sb(f"pt{i}", (128, T), BF16, st_c1) for i in range(4)]
    denr = sb("denr", (1, 512), F32, st_c1)
    denb = sb("denb", (1, 1024), BF16, st_c1)
    BC16 = sb("bc16", (64, 1024), BF16, st_c1)
    O16t = sb("o16t", (64, T), BF16, st_c1)

    ld(Tst[0][:, :], A(scr_d, 0, [[1, 128], [1, T]]), cast=True, sem="qT0",
       reads=("cscr",), writes=("tst0",))

    NB = min(4, NP)
    vsum = sb("vsum", (1, H * 65), F32, st_c1)
    vsumT = sb("vsumT", (65, H), F32, st_c1)
    for g0, g1 in ((0, 512), (512, 1024), (1024, H * 65)):
        for tt in range(NTT):
            kb.op(TE, lambda e, g0=g0, g1=g1, tt=tt:
                  e.matmul(pbc[0][0:1, 0:g1 - g0], onesb[:, 0:1],
                           VtL[tt][:, g0:g1], start=(tt == 0),
                           stop=(tt == NTT - 1)),
                  reads=(f"vtl{tt}", "onesb"), writes=("pbc0",),
                  inc=(tt == NTT - 1))
        kb.op(V, lambda e, g0=g0, g1=g1:
              e.tensor_copy(vsum[:, g0:g1], pbc[0][0:1, 0:g1 - g0]),
              reads=("pbc0",), writes=("vsum",))
    for h in range(H):
        kb.op(TE, lambda e, h=h:
              e.transpose(pbc[0][0:65, h:h + 1], vsum[0:1, h * 65:(h + 1) * 65],
                          iden_f[0:1, 0:1]),
              reads=("vsum", "iden_f"), writes=("pbc0",), inc=(h == H - 1))
    kb.op(V, lambda e: e.tensor_copy(vsumT[:, :], pbc[0][0:65, 0:H]),
          reads=("pbc0",), writes=("vsumT",))

    def mk_post_bank(h, b):
        c, hf = h // 2, h % 2

        def post():
            kb.op(TE, lambda e, b=b:
                  e.matmul(pbc[0][0:64, 0:512], onesbr[:, :],
                           denb[:, (b % 2) * 512:(b % 2) * 512 + 512],
                           start=True, stop=True),
                  reads=("denb", "onesbr"), writes=("pbc0",))
            kb.op(V, lambda e, b=b:
                  e.tensor_copy(BC16[:, (b % 2) * 512:(b % 2) * 512 + 512],
                                pbc[0][0:64, 0:512]),
                  reads=("pbc0",), writes=("bc16",))
            dst = (lambda sl: Y16p[c][0:64, sl]) if hf == 0 else                   (lambda sl: O16t[:, sl])
            dkey = f"y16p_{c}" if hf == 0 else "o16t"
            kb.op(V, lambda e, b=b, dst=dst:
                  e.tensor_tensor(dst(slice(b * 512, (b + 1) * 512)),
                                  pb[b][0:64, 0:512],
                                  BC16[:, (b % 2) * 512:(b % 2) * 512 + 512],
                                  ALU.mult),
                  reads=(f"pb{b}", "bc16"), writes=(dkey,))
            if b == 0:
                kb.op(V, lambda e, h=h, dst=dst:
                      e.tensor_scalar(dst(slice(0, 1)), vsumT[0:64, h:h + 1],
                                      1.0 / T, None, ALU.mult),
                      reads=("vsumT",), writes=(dkey,))
            if b == NB - 1 and hf == 1:
                kb.op(G, lambda e, c=c:
                      e.dma_start(Y16p[c][64:128, :], O16t[:, :]),
                      reads=("o16t",), writes=(f"y16p_{c}",), dma_sem="qO")
        return post

    fifo = []
    postq = []
    cct = [0]

    def pop_one():
        h, b, q, w, j0, kt, pslot, ptk, lb = fifo.pop(0)
        if kt == 0:
            keep = []
            for ent in postq:
                if ent[2] == b:
                    ent[0]()
                else:
                    keep.append(ent)
            postq[:] = keep
        kb.op(TE, lambda e, h=h, b=b, q=q, w=w, j0=j0, kt=kt, pslot=pslot,
              lb=lb:
              e.matmul(pb[b][0:65, q % 512:q % 512 + w],
                       VtL[kt][:, h * 65:(h + 1) * 65],
                       Pt[pslot][:, j0:j0 + w],
                       start=(kt == 0), stop=lb),
              reads=(f"vtl{kt}", ptk), writes=(f"pb{b}",), inc=lb)
        if lb:
            kb.op(S, lambda e, b=b:
                  e.activation(denr[:, :], pb[b][64:65, 0:512], AF.Ln,
                               bias=epsc[0:1, 3:4]),
                  reads=(f"pb{b}", "epsc"), writes=("denr",))
            kb.op(S, lambda e, b=b:
                  e.activation(denb[:, (b % 2) * 512:(b % 2) * 512 + 512],
                               denr[:, :], AF.Exp, scale=-1.0),
                  reads=("denr",), writes=("denb",))
            postq.append((mk_post_bank(h, b), cct[0], b))

    def tflip(par, pp):
        kb.op(TE, lambda e, par=par, pp=pp:
              e.matmul(pbc[0][:, 0:512], Jmat[:, :],
                       Tst[par][:, pp * 512:(pp + 1) * 512],
                       start=True, stop=True),
              reads=("Jmat", f"tst{par}"), writes=("pbc0",))
        kb.op(V, lambda e, par=par, pp=pp:
              e.tensor_copy(Tst[par][:, pp * 512:(pp + 1) * 512],
                            pbc[0][:, 0:512]),
              reads=("pbc0",), writes=(f"tst{par}",))

    for pp in range(NP):
        tflip(0, pp)
    for h in range(H):
        par = h % 2
        c, hf = h // 2, h % 2
        po = 64 * hf
        if h + 1 < H:
            ld(Tst[1 - par][:, :], A(scr_d, (h + 1) * CP, [[1, 128], [1, T]]),
               cast=True, sem=f"qT{1 - par}", reads=("cscr",),
               writes=(f"tst{1 - par}",))
        for kt in range(NTT):
            if NTT // 2 <= kt < NTT // 2 + NP and h + 1 < H:
                tflip(1 - par, kt - NTT // 2)
            ktg = h * NTT + kt
            while fifo and fifo[0][5] + (fifo[0][0] - h) * NTT <= kt - 3:
                pop_one()
            q0 = kt * 128
            cs = q0
            ci = 0
            while cs < T:
                while postq and postq[0][1] <= cct[0] - 4:
                    postq.pop(0)[0]()
                ce = min((cs // 512 + 1) * 512, T)
                w2 = ce - cs
                sbk = cct[0] % 3
                pslot = ktg % 4
                ptk = f"pt{pslot}_c{ci}"
                jc = cs - q0
                kb.op(TE, lambda e, sbk=sbk, c=c, po=po, kt=kt, q=cs, w=w2:
                      e.matmul(s16[sbk][:, 0:w],
                               K16[c][po:po + 64, kt * 128:(kt + 1) * 128],
                               Q16[c][po:po + 64, q:q + w],
                               start=True, stop=True),
                      reads=(f"k16_{c}", f"q16_{c}"),
                      writes=(f"s16_{sbk}",), inc=True)
                kb.op(S, lambda e, sbk=sbk, pslot=pslot, jc=jc, w2=w2:
                      e.activation(Pt[pslot][:, jc:jc + w2], s16[sbk][:, 0:w2],
                                   AF.Exp),
                      reads=(f"s16_{sbk}",), writes=(ptk,))
                kb.op(V, lambda e, par=par, pslot=pslot, jc=jc, w2=w2:
                      e.tensor_tensor(Pt[pslot][:, jc:jc + w2],
                                      Pt[pslot][:, jc:jc + w2],
                                      Tst[par][:, jc:jc + w2], ALU.mult),
                      reads=(ptk, f"tst{par}"), writes=(ptk,))
                bb = cs // 512
                lb = (kt == min(4 * bb + 3, NTT - 1))
                fifo.append((h, bb, cs, w2, jc, kt, pslot, ptk, lb))
                cct[0] += 1
                while len(fifo) > 48:
                    pop_one()
                cs = ce
                ci += 1
    while fifo:
        pop_one()
    while postq:
        postq.pop(0)[0]()
    st_c1.close()
    st_c_vt.close()
    st_ps_c1.close()
    for i in (4, 5, 6, 7):
        pb[i] = ES.enter_context(nc.psum_tensor(f"pb{i}b", [128, 512], F32))
    kb.fence()

    st_c3 = ExitStack()
    NPH = max(1, NP // 2)

    st_c2 = ExitStack()
    accw = [sb(f"accw{i}", (128, T), F32, st_c2) for i in range(2)]
    rows2P = sb("rows2P", (1, 512), F32, st_c2)
    rowb2 = sb("rowb2", (1, T), BF16, st_c2)
    sq2 = [sb(f"sq2_{i}", (128, T), BF16, st_c2) for i in range(2)]
    Wst2 = [sb(f"ws2_{i}", (128, 2048), BF16, st_c2) for i in range(2)]

    def wload2(par, dram, r0, c0, c1, key):
        for k in range(8):
            ld(Wst2[par][:, k * (c1 - c0):(k + 1) * (c1 - c0)],
               dram[r0 + k * 128:r0 + (k + 1) * 128, c0:c1], cast=True,
               sem=f"qW{par}", writes=(key,), silent=(k > 0))
        kb.finish_queue(f"qW{par}", [key])

    Y16o = Q16
    wload2(0, w_d["Wo"], 0, 0, 256, "ws2_0")
    for qr in range(4):
        par = qr % 2
        if qr + 1 < 4:
            wload2(1 - par, w_d["Wo"], 0, (qr + 1) * 256, (qr + 2) * 256,
                   f"ws2_{1 - par}")
        for dj in range(2):
            dout = qr * 2 + dj
            apar = dout % 2
            ld(accw[apar][:, :], acc_d[dout * 128:(dout + 1) * 128, :],
               sem=f"qA{apar}", reads=(f"accp{dout}",), writes=(f"accw{apar}",))
        for dj in range(2):
            dout = qr * 2 + dj
            pbase = (dout % 2) * 4
            apar = dout % 2
            for k in range(8):
                for pp in range(NP):
                    kb.op(TE, lambda e, par=par, k=k, dj=dj, pp=pp, pbase=pbase:
                          e.matmul(pb[pbase + pp][:, 0:512],
                                   Wst2[par][:, k * 256 + dj * 128:
                                            k * 256 + dj * 128 + 128],
                                   Y16p[k][:, pp * 512:(pp + 1) * 512],
                                   start=(k == 0), stop=(k == 7)),
                          reads=(f"ws2_{par}", f"y16p_{k}"),
                          writes=(f"pb{pbase + pp}",), inc=(k == 7))
            for pp in range(NP):
                kb.op(V, lambda e, dout=dout, pp=pp, pbase=pbase:
                      e.tensor_tensor(Y16o[dout][:, pp * 512:(pp + 1) * 512],
                                      pb[pbase + pp][:, 0:512],
                                      XN16[dout][:, pp * 512:(pp + 1) * 512],
                                      ALU.add),
                      reads=(f"pb{pbase + pp}", f"xn16_{dout}"),
                      writes=(f"q16_{dout}",))
                kb.op(V, lambda e, apar=apar, pp=pp, pbase=pbase:
                      e.tensor_tensor(accw[apar][:, pp * 512:(pp + 1) * 512],
                                      accw[apar][:, pp * 512:(pp + 1) * 512],
                                      pb[pbase + pp][:, 0:512], ALU.add),
                      reads=(f"pb{pbase + pp}", f"accw{apar}"),
                      writes=(f"accw{apar}",))
            kb.op(G, lambda e, apar=apar, dout=dout:
                  e.tensor_tensor(accw[apar][:, :], accw[apar][:, :],
                                  XN16[dout][:, :], ALU.add),
                  reads=(f"xn16_{dout}", f"accw{apar}"), writes=(f"accw{apar}",))
            ld(acc_d[dout * 128:(dout + 1) * 128, :], accw[apar][:, :],
               sem=f"qA{apar}", reads=(f"accw{apar}",), writes=(f"accp{dout}",))
    for c in range(8):
        kb.op(V, lambda e, c=c: e.tensor_tensor(sq2[c % 2][:, :],
                                                Y16o[c][:, :],
                                                Y16o[c][:, :], ALU.mult),
              reads=(f"q16_{c}",), writes=(f"sq2_{c % 2}",))
        for pp in range(NP):
            bk = 4 + pp
            kb.op(TE, lambda e, pp=pp, bk=bk, c=c:
                  e.matmul(pb[bk][0:1, 0:512], onesb[:, 0:1],
                           sq2[c % 2][:, pp * 512:(pp + 1) * 512],
                           start=(c == 0), stop=(c == 7)),
                  reads=(f"sq2_{c % 2}", "onesb"), writes=(f"pb{bk}",),
                  inc=True)
    for pp in range(NP):
        kb.op(S, lambda e, pp=pp:
              e.activation(rows2P[0:1, :], pb[4 + pp][0:1, 0:512], AF.Ln,
                           bias=epsc[0:1, 1:2], scale=1.0 / D),
              reads=(f"pb{4 + pp}", "epsc"), writes=("rows2P",))
        kb.op(S, lambda e, pp=pp:
              e.activation(rowb2[0:1, pp * 512:(pp + 1) * 512],
                           rows2P[0:1, :], AF.Exp, scale=-0.5),
              reads=("rows2P",), writes=("rowb2",))
    YN16 = K16
    for c in range(8):
        for pp in range(NP):
            bk = pp % 2
            kb.op(TE, lambda e, c=c, pp=pp, bk=bk:
                  e.matmul(pb[bk][:, 0:512], ln2b[:, c * 128:(c + 1) * 128],
                           rowb2[0:1, pp * 512:(pp + 1) * 512],
                           start=True, stop=True),
                  reads=("ln2b", "rowb2"), writes=(f"pb{bk}",))
            kb.op(V, lambda e, c=c, pp=pp, bk=bk:
                  e.tensor_tensor(YN16[c][:, pp * 512:(pp + 1) * 512],
                                  Y16o[c][:, pp * 512:(pp + 1) * 512],
                                  pb[bk][:, 0:512], ALU.mult),
                  reads=(f"q16_{c}", f"pb{bk}"), writes=(f"k16_{c}",))
    st_c2.close()
    st_c_bc.close()
    kb.fence()

    H16 = [sb(f"h16_{i}", (128, T // 2), BF16, st_c3) for i in range(24)] + \
          [XN16[i] for i in range(8)]
    W2R = sb("w2r", (128, 32 * D), BF16, st_c3)
    Wst1 = [sb(f"ws1_{i}", (128, 2048), BF16, st_c3) for i in range(2)]
    accw2 = [sb(f"acw2_{i}", (128, T // 2), F32, st_c3) for i in range(2)]

    def w1load(par, e16):
        for k in range(8):
            ld(Wst1[par][:, k * 256:(k + 1) * 256],
               w_d["W1"][k * 128:(k + 1) * 128, e16 * 256:(e16 + 1) * 256],
               cast=True, sem=f"qW{par}", writes=(f"ws1_{par}",),
               silent=(k > 0))
        kb.finish_queue(f"qW{par}", [f"ws1_{par}"])

    def w2rload(h0, h1):
        for hk in range(h0, h1):
            ld(W2R[:, hk * D:(hk + 1) * D],
               w_d["W2"][hk * 128:(hk + 1) * 128, :],
               cast=True, sem="qF", silent=(hk % 4 != 0))
            if hk % 4 == 3 or hk == h1 - 1:
                kb.finish_queue("qF", [f"w2r_{k}" for k in
                                       range(hk - hk % 4, hk + 1)])
    w1load(0, 0)
    w2rload(0, 4)
    w1load(1, 1)

    def w1_emit(tp2, e8):
        t0c = tp2 * (T // 2)
        par = e8 % 2
        if e8 + 1 < 16:
            w1load(1 - par, e8 + 1)
        for hl in range(2):
            hk = e8 * 2 + hl
            pbase = (hk % 2) * 2
            for k in range(8):
                for pc in range(NPH):
                    kb.op(TE, lambda e, par=par, k=k, hl=hl, pc=pc,
                          pbase=pbase, t0c=t0c:
                          e.matmul(pb[pbase + pc][:, 0:512],
                                   Wst1[par][:, k * 256 + hl * 128:
                                            k * 256 + hl * 128 + 128],
                                   YN16[k][:, t0c + pc * 512:
                                           t0c + (pc + 1) * 512],
                                   start=(k == 0), stop=(k == 7)),
                          reads=(f"ws1_{par}", f"k16_{k}"),
                          writes=(f"pb{pbase + pc}",), inc=(k == 7))
            for pc in range(NPH):
                kb.op(S, lambda e, hk=hk, pc=pc, pbase=pbase:
                      e.activation(H16[hk][:, pc * 512:(pc + 1) * 512],
                                   pb[pbase + pc][:, 0:512], AF.Relu),
                      reads=(f"pb{pbase + pc}",), writes=(f"h16_{hk}",))

    def w2_emit(dp, hk):
        for j, dout in enumerate((2 * dp, 2 * dp + 1)):
            for pc in range(NPH):
                bk = 4 + 2 * j + pc if NPH == 2 else 4 + j
                kb.op(TE, lambda e, dout=dout, pc=pc, bk=bk, hk=hk:
                      e.matmul(pb[bk][:, 0:512],
                               W2R[:, hk * D + dout * 128:
                                    hk * D + (dout + 1) * 128],
                               H16[hk][:, pc * 512:(pc + 1) * 512],
                               start=(hk == 0), stop=(hk == 31)),
                      reads=(f"w2r_{hk}", f"h16_{hk}"),
                      writes=(f"pb{bk}",),
                      inc=(j == 1 and pc == NPH - 1))

    def w2_evac_pre(tp2, dp):
        t0c = tp2 * (T // 2)
        for j, dout in enumerate((2 * dp, 2 * dp + 1)):
            ld(accw2[j][:, :], acc_d[dout * 128:(dout + 1) * 128,
                                     t0c:t0c + T // 2],
               sem=f"qA{j}", reads=(f"accp{dout}",), writes=(f"acw2_{j}",))

    def w2_evac(tp2, dp):
        t0c = tp2 * (T // 2)
        for j, dout in enumerate((2 * dp, 2 * dp + 1)):
            for pc in range(NPH):
                bk = 4 + 2 * j + pc if NPH == 2 else 4 + j
                kb.op(V, lambda e, j=j, pc=pc, bk=bk:
                      e.tensor_tensor(accw2[j][:, pc * 512:(pc + 1) * 512],
                                      accw2[j][:, pc * 512:(pc + 1) * 512],
                                      pb[bk][:, 0:512], ALU.add),
                      reads=(f"pb{bk}", f"acw2_{j}"), writes=(f"acw2_{j}",))
            ld(acc_d[dout * 128:(dout + 1) * 128, t0c:t0c + T // 2],
               accw2[j][:, :], sem=f"qA{j}", reads=(f"acw2_{j}",),
               writes=(f"accp{dout}_{tp2}",))

    for tp2 in range(2):
        if tp2 == 1:
            w1load(0, 0)
        w2_evac_pre(tp2, 0)
        w2lag = 0
        for e8 in range(16):
            if tp2 == 0:
                lo, hi = 4 + 2 * e8, min(6 + 2 * e8, 32)
                if lo < 32:
                    w2rload(lo, hi)
            w1_emit(tp2, e8)
            while w2lag <= 2 * e8 - 2:
                w2_emit(0, w2lag)
                w2lag += 1
        while w2lag < 32:
            w2_emit(0, w2lag)
            w2lag += 1
        w2_evac(tp2, 0)
        for dp in range(1, 4):
            w2_evac_pre(tp2, dp)
            for hk in range(32):
                w2_emit(dp, hk)
            w2_evac(tp2, dp)
    st_c3.close()
    st_c_k.close()
    kb.fence()

    st_d = ExitStack()
    ND = 8
    xo = [sb(f"xo{i}", (128, D), F32, st_d) for i in range(ND)]
    ot = [sb(f"ot{i}", (128, D), F32, st_d) for i in range(ND)]
    asl = [sb(f"asl{i}", (128, D), F32, st_d) for i in range(ND)]
    for tt in range(NTT):
        p = tt % ND
        hv = 0 if tt * 128 < T // 2 else 1
        ld(xo[p][:, :], x_d[tt * 128:(tt + 1) * 128, :], sem=f"qX{p % 4}",
           writes=(f"xo{p}",))
        kb.finish_queue(f"qX{p % 4}", [f"xo{p}"])
        src = A(acc_d, tt * 128,
                [[T, 128], [128 * T, 8], [1, 128]])
        dst = A(asl[p], 0, [[D, 128], [128, 8], [1, 128]])
        kb.op("sync", lambda e, d=dst, s=src: e.dma_start(d, s),
              reads=tuple(f"accp{c}_{hv}" for c in range(8)),
              writes=(f"asl{p}",), dma_sem=f"qD{p % 4}")
        kb.finish_queue(f"qD{p % 4}", [f"asl{p}"])
        for c in range(8):
            bk = 6 + (c % 2)
            sl = ((c // 2) % 2) * 128
            kb.op(TE, lambda e, p=p, c=c, bk=bk, sl=sl:
                  e.transpose(pb[bk][:, sl:sl + 128],
                              asl[p][:, c * 128:(c + 1) * 128], iden_f[:, :]),
                  reads=(f"asl{p}", "iden_f"), writes=(f"pb{bk}",))
            kb.op(V, lambda e, p=p, c=c, bk=bk, sl=sl:
                  e.tensor_tensor(ot[p][:, c * 128:(c + 1) * 128],
                                  pb[bk][:, sl:sl + 128],
                                  xo[p][:, c * 128:(c + 1) * 128], ALU.add),
                  reads=(f"pb{bk}", f"xo{p}"), writes=(f"ot{p}",))
        kb.op("sync", lambda e, tt=tt, p=p:
              e.dma_start(out_d[tt * 128:(tt + 1) * 128, :], ot[p][:, :]),
              reads=(f"ot{p}",), writes=("outd",), dma_sem="qOut")
    st_d.close()
    kb.fence()
    kb.wait("sync", "qOut", kb.cnt["qOut"])

    kb.emit()
    kb.stack.close()
    return nc


def kernel(**inputs):
    x = np.ascontiguousarray(np.asarray(inputs["L0_attn_x"], dtype=np.float32))
    T = x.shape[1]
    base = {
        "ln1": inputs["L0_attn_ln1_w"], "Wq": inputs["L0_attn_Wq"],
        "Wk": inputs["L0_attn_Wk"], "Wv": inputs["L0_attn_Wv"],
        "relb": inputs["L0_attn_rel_bias"], "scale": inputs["L0_attn_scale"],
        "Wo": inputs["L0_attn_Wo"], "ln2": inputs["L0_attn_ln2_w"],
        "W1": inputs["L0_attn_W1"], "W2": inputs["L0_attn_W2"],
        "gw": inputs["L0_ffn_gate_w"], "gb": inputs["L0_ffn_gate_b"],
        "ew": inputs["L0_ffn_expert_w"], "eb": inputs["L0_ffn_expert_b"],
    }
    base = {k: np.ascontiguousarray(np.asarray(v, dtype=np.float32))
            for k, v in base.items()}
    base["E"] = _bucket_onehot(T)
    nc = build_nc(T=T)
    in_maps = [dict(base, x=np.ascontiguousarray(x[i]))
               for i in range(x.shape[0])]
    res = run_bass_kernel_spmd(nc, in_maps, core_ids=list(range(len(in_maps))))
    out = np.stack([res.results[i]["out"] for i in range(len(in_maps))], axis=0)
    return out.astype(np.float32)
